# version 05
# speedup vs baseline: 1.1221x; 1.0641x over previous
import numpy as np
import ml_dtypes
from contextlib import ExitStack
import concourse.bass as bass
import concourse.mybir as mybir
from concourse.bass_utils import run_bass_kernel_spmd

F32 = mybir.dt.float32
BF16 = mybir.dt.bfloat16
AF = mybir.ActivationFunctionType
ALU = mybir.AluOpType
AX = mybir.AxisListType

T = 2048
NC = 8
D = 1024
EPS_LN = 1e-5

EN = ("t", "v", "s", "g", "y")


class Plan:

    def __init__(self, nc, ctx):
        self.nc = nc
        self.ctx = ctx
        self.q = {e: [] for e in EN}
        self.semh = {}
        self.semn = {}
        self._psum_stack = []
        self.remap = None

    def begin_block(self):
        self.remap = {}

    GLOBAL_TAGS = frozenset(("wstage", "lg5", "const", "rload", "rstore",
                             "rdmp"))

    def key(self, name):
        if self.remap is None:
            return name
        if len(name) == 3 and name[0] == "P" and name[1:].isdigit():
            return name
        if name in self.GLOBAL_TAGS or name.startswith("cc"):
            return name
        return self.remap.setdefault(name, "P%02d" % len(self.remap))

    def sem(self, name):
        name = self.key(name)
        if name not in self.semh:
            self.semh[name] = self.ctx.enter_context(self.nc.semaphore(name))
            self.semn[name] = 0
        return self.semh[name]

    def planned(self, name):
        self.sem(name)
        return self.semn[self.key(name)]

    def plan(self, e):
        def deco(fn):
            self.q[e].append(fn)
        return deco

    def wait(self, e, name, val=None):
        h = self.sem(name)
        v = self.semn[self.key(name)] if val is None else val
        self.q[e].append(lambda eng, h=h, v=v: eng.wait_ge(h, v))

    def wait_if(self, e, name):
        k = self.key(name)
        if k in self.semn and self.semn[k] > 0:
            self.wait(e, name)

    def op(self, e, semname, k, fn, mark=None):
        h = self.sem(semname)
        kk = self.key(semname)
        self.semn[kk] += k
        v = self.semn[kk]

        def _thunk(eng, h=h, k=k, fn=fn):
            fn(eng).then_inc(h, k)
        _thunk.mark = mark
        self.q[e].append(_thunk)
        return v

    def run(self, e, fn):
        self.q[e].append(fn)

    def wait_all(self, e):
        for name in list(self.semn):
            self.wait(e, name)


def emit_block_sections(nc, P):
    with nc.Block() as block:
        @block.tensor
        def _(eng):
            for f in P.q["t"]:
                f(eng)

        @block.vector
        def _(eng):
            for f in P.q["v"]:
                f(eng)

        @block.scalar
        def _(eng):
            for f in P.q["s"]:
                f(eng)

        @block.gpsimd
        def _(eng):
            for f in P.q["g"]:
                f(eng)

        @block.sync
        def _(eng):
            for f in P.q["y"]:
                f(eng)


class Env:
    pass


def fm(arena, c, sl=None, width=T):
    if sl is None:
        return arena[:, c * width:(c + 1) * width]
    a, b = sl
    return arena[:, c * width + a: c * width + b]



def stage_weight(P, env, name, shape):
    nc = P.nc
    src = nc.dram_tensor(name, list(shape), F32, kind="ExternalInput").ap()
    dst = nc.dram_tensor(name + "_bf", list(shape), BF16, kind="Internal").ap()
    n = int(np.prod(shape))
    rows_per = max(1, min(shape[0], (4 << 20) // (int(shape[1]) * 4)))
    r = 0
    while r < shape[0]:
        rr = min(rows_per, shape[0] - r)
        P.op("g", "wstage", 16,
             lambda g, src=src, dst=dst, r=r, rr=rr: g.dma_start(
                 out=dst[r:r + rr, :], in_=src[r:r + rr, :]),
             mark="stage")
        r += rr
    return dst


def load_vec_input(P, env, name, shape, dtype=F32):
    nc = P.nc
    src = nc.dram_tensor(name, list(shape), dtype, kind="ExternalInput").ap()
    t = env.ctx.enter_context(nc.sbuf_tensor("sb_" + name, list(shape), dtype))
    ap = t[:, :] if len(shape) == 2 else t[:]
    P.op("y", "vload", 16, lambda y, ap=ap, src=src: y.dma_start(out=ap, in_=src))
    return ap



def emit_ln_span(P, env, src_chunks, out_chunks, add_into, tag, width=512,
                 src_bf=False, sqbuf=None, span_hook=None, span_idx=0):
    pb = env.pb
    sqb = env.sq if sqbuf is None else sqbuf
    if span_hook is not None:
        P.wait("v", "l3z_mm")
        P.wait("s", "l3z_mm")
    for c in range(NC):
        P.op("s", tag + "_sq", 1, lambda s, c=c: s.activation(
            sqb[:, c * width:(c + 1) * width], src_chunks[c], AF.Square))
    P.wait("t", tag + "_sq")
    bm, bs = pb[0], pb[1]
    for c in range(NC):
        last = c == NC - 1

        def mmm(t, c=c, last=last):
            return t.matmul(bm[0:1, 0:width],
                            env.ones_col_bf if src_bf else env.ones_col_f32,
                            src_chunks[c], start=(c == 0), stop=last)
        if last:
            P.op("t", tag + "_mm", 1, mmm)
        else:
            P.run("t", lambda t, f=mmm: f(t))
    for c in range(NC):
        last = c == NC - 1

        def mms(t, c=c, last=last):
            return t.matmul(bs[0:1, 0:width], env.ones_col_bf,
                            sqb[:, c * width:(c + 1) * width],
                            start=(c == 0), stop=last)
        if last:
            P.op("t", tag + "_mm", 1, mms)
        else:
            P.run("t", lambda t, f=mms: f(t))
    P.wait("v", tag + "_mm")
    m = env.stat_m[0:1, 0:width]
    t1 = env.stat_t[0:1, 0:width]
    tmp = env.stat_b[0:1, 0:width]
    P.op("v", tag + "_sv", 1, lambda v: v.tensor_scalar_mul(m, bm[0:1, 0:width], 1.0 / D))
    P.run("v", lambda v: v.tensor_scalar_mul(t1, bs[0:1, 0:width], 1.0 / D))
    P.wait("v", tag + "_sv")
    P.op("v", tag + "_sv", 1, lambda v: v.tensor_mul(tmp, m, m))
    P.wait("v", tag + "_sv")
    P.op("v", tag + "_st", 1, lambda v: v.tensor_sub(t1, t1, tmp))
    P.wait("s", tag + "_st")
    P.op("s", tag + "_sl", 1, lambda s: s.activation(t1, t1, AF.Ln,
                                                     bias=env.eps_col[0:1, 0:1]))
    P.wait("s", tag + "_sl")
    P.op("s", tag + "_r", 1, lambda s: s.activation(
        env.stat_r[0:1, 0:width], t1, AF.Exp, scale=-0.5))
    P.wait("v", tag + "_r")
    P.op("v", tag + "_b", 1, lambda v: v.scalar_tensor_tensor(
        env.stat_b[0:1, 0:width], m, -1.0, env.stat_r[0:1, 0:width],
        ALU.mult, ALU.mult))
    P.wait("t", tag + "_b")
    ba, bb = pb[2], pb[3]

    def bc(t):
        t.matmul(ba[:, 0:width], env.ones_row_f32, env.stat_r[0:1, 0:width],
                 start=True, stop=True)
        return t.matmul(bb[:, 0:width], env.ones_row_f32, env.stat_b[0:1, 0:width],
                        start=True, stop=True)
    P.op("t", tag + "_bc", 1, bc)
    P.wait("v", tag + "_bc")
    for c in range(NC):
        P.op("v", tag + "_n", 1, lambda v, c=c: v.tensor_mul(
            env.tmpf[:, 0:width], src_chunks[c], ba[:, 0:width]))
        P.wait("v", tag + "_n")
        if out_chunks is not None:
            P.op("v", tag + "_n", 1, lambda v, c=c: v.tensor_add(
                out_chunks[c], env.tmpf[:, 0:width], bb[:, 0:width]))
        if add_into is not None:
            P.op("v", tag + "_n", 1, lambda v, c=c: v.tensor_add(
                env.tmpf[:, 0:width], env.tmpf[:, 0:width], bb[:, 0:width]))
            P.wait("v", tag + "_n")
            P.op("v", tag + "_n", 1, lambda v, c=c: v.tensor_add(
                add_into[c], add_into[c], env.tmpf[:, 0:width]))
        P.wait("v", tag + "_n")
    if span_hook is not None:
        span_hook(P, env, span_idx, width)
    P.wait("t", tag + "_n")
    P.wait("s", tag + "_n")


def emit_ln(P, env, src_chunks, out_chunks, add_into=None, tag="ln", span_hook=None):
    for qi, q0 in enumerate(range(0, T, 512)):
        emit_ln_span(
            P, env,
            [sc[:, q0:q0 + 512] for sc in src_chunks],
            None if out_chunks is None else [oc[:, q0:q0 + 512] for oc in out_chunks],
            None if add_into is None else [ac[:, q0:q0 + 512] for ac in add_into],
            tag="ln", span_hook=span_hook, span_idx=qi)


def emit_ln_slice(P, env, src_chunks, add_into, tag, width=512, src_bf=False,
                  sqbuf=None):
    emit_ln_span(P, env, src_chunks, None, add_into, "ln", width, src_bf, sqbuf)



def emit_proj(P, env, wt_bf, I, O, xin, evac, tag, ts_range, nbk=8, banks=None,
              wslot_cols=1024, extra_acc=None):
    nI = I // 128
    nO = O // 128
    banks = banks if banks is not None else list(range(nbk))
    nslot = 2 if wslot_cols > 1024 else 4
    ldtag = [tag + "_ld%d" % i for i in range(nslot)]
    ictag = tag + "_ic"
    gtag = tag + "_g"
    evtag = tag + "_ev"

    for (t0, t1) in ts_range:
        width = t1 - t0
        for ocb0 in range(0, nO, nbk):
            ocs = list(range(ocb0, min(nO, ocb0 + nbk)))
            nb = len(ocs)
            for fam in ("pj", "pjtm"):
                P.wait_if("t", fam + "_ev")
            for tgx in ("l2A", "l2B"):
                for sfx in ("_se", "_oe", "_ae", "_t1c", "_kc", "_oc"):
                    P.wait_if("t", tgx + sfx)
            for sfx in ("_sc", "_ptc", "_oe"):
                P.wait_if("t", "at" + sfx)
            base_ic = P.planned(ictag)
            base_ld = [P.planned(t_) for t_ in ldtag]
            base_g = P.planned(gtag)
            for fam in ("pj", "pjtm"):
                P.wait_if("y", fam + "_g")
            for ic in range(nI):
                if ic >= nslot:
                    P.wait("y", ictag, base_ic + ic - nslot + 1)
                P.op("y", ldtag[ic % nslot], 16, lambda y, ic=ic, ocs=ocs, nb=nb: y.dma_start(
                    out=env.wsb[:, (ic % nslot) * wslot_cols:
                                (ic % nslot) * wslot_cols + nb * 128],
                    in_=wt_bf[ic * 128:(ic + 1) * 128,
                              ocs[0] * 128:(ocs[0] + nb) * 128]))
            for ic in range(nI):
                P.wait("t", ldtag[ic % nslot],
                       base_ld[ic % nslot] + 16 * (ic // nslot + 1))
                stop = ic == nI - 1 and extra_acc is None
                for j, oc in enumerate(ocs):
                    wt_ap = env.wsb[:, (ic % nslot) * wslot_cols + j * 128:
                                    (ic % nslot) * wslot_cols + (j + 1) * 128]
                    bank = env.pb[banks[j]]

                    def mm(t, wt_ap=wt_ap, bank=bank, ic=ic, t0=t0, t1=t1,
                           width=width, stop=stop):
                        return t.matmul(bank[:, 0:width], wt_ap, xin(ic, t0, t1),
                                        start=(ic == 0), stop=stop)
                    if stop and ic == nI - 1:
                        P.op("t", gtag, 1, mm)
                    elif j == nb - 1 and ic < nI - 1:
                        P.op("t", ictag, 1, mm)
                    else:
                        P.run("t", lambda t, f=mm: f(t))
            if extra_acc is not None:
                for j, oc in enumerate(ocs):
                    fnb = extra_acc(oc, t0, t1, env.pb[banks[j]])
                    P.op("t", gtag, 1, fnb)
            for j, oc in enumerate(ocs):
                evac(P, oc, t0, t1, env.pb[banks[j]][:, 0:width], gtag, base_g + j + 1)



def emit_l2_ffn(P, env, W):
    nc = P.nc
    HC = 32

    w1T = W["w1T_bf"]; w2T = W["w2T_bf"]; h1T = W["h1T_bf"]; h2T = W["h2T_bf"]; w3T = W["w3T_bf"]
    s1 = W["s1"]; s2 = W["s2"]; sh1 = W["sh1"]; sh2 = W["sh2"]; s3 = W["s3"]

    def xin_norm(ic, t0, t1):
        return fm(env.xnorm, ic)[:, t0:t1]

    def mk_evac(dst_arena, func, scales):
        def evac(P, oc, tt0, tt1, ps, gtag, gval):
            P.wait("s", gtag, gval)
            P.op("s", "pj_ev", 1, lambda s, oc=oc, ps=ps: s.activation(
                dst_arena[:, oc * 512:(oc + 1) * 512], ps, func,
                scale=scales[:, oc:oc + 1]))
        return evac

    for (t0, t1) in [(a, a + 512) for a in range(0, T, 512)]:
        emit_proj(P, env, w1T, D, 4096, xin_norm, mk_evac(env.ar1, AF.Silu, s1),
                  "pj", ts_range=[(t0, t1)])
        emit_proj(P, env, w2T, D, 4096, xin_norm, mk_evac(env.ar2, AF.Identity, s2),
                  "pj", ts_range=[(t0, t1)])
        P.wait("v", "pj_ev")
        for hc in range(HC):
            P.op("v", "lg1", 1, lambda v, hc=hc: v.tensor_mul(
                env.ar1[:, hc * 512:(hc + 1) * 512],
                env.ar1[:, hc * 512:(hc + 1) * 512],
                env.ar2[:, hc * 512:(hc + 1) * 512]))
        P.wait("t", "lg1")

        def xin_h(ic, tt0, tt1):
            return env.ar1[:, ic * 512:(ic + 1) * 512]

        def xin_g1(ic, tt0, tt1):
            return env.ar2[:, ic * 512:(ic + 1) * 512]

        emit_proj(P, env, h1T, 4096, 4096, xin_h, mk_evac(env.ar2, AF.Gelu, sh1),
                  "pj", ts_range=[(t0, t1)])
        emit_proj(P, env, h2T, 4096, 4096, xin_g1, mk_evac(env.ar1, AF.Gelu, sh2),
                  "pj", ts_range=[(t0, t1)])
        emit_proj(P, env, w3T, 4096, D, xin_h, mk_evac(env.bo, AF.Identity, s3),
                  "pj", ts_range=[(t0, t1)])

        P.wait("s", "pj_ev")
        P.wait("v", "pj_ev")
        src = [env.bo[:, c * 512:(c + 1) * 512] for c in range(NC)]
        add = [fm(env.xres, c)[:, t0:t1] for c in range(NC)]
        emit_ln_slice(P, env, src, add, tag=f"l2fpl{t0}")



def emit_proj_tm(P, env, wt_bf, O, xin_chunks, bias_row_bf, evac, tag, extra_acc=None, ts_sub=None):
    nI = D // 128
    noh = O // 512
    ldtag = [tag + "_ld0", tag + "_ld1"]
    ictag, gtag, evtag = tag + "_ic", tag + "_g", tag + "_ev"
    ntt = T // 128
    tt_lo, tt_hi = (0, ntt) if ts_sub is None else (ts_sub[0] // 128, ts_sub[1] // 128)
    for ttg0 in range(tt_lo, tt_hi, 4):
        tts = list(range(ttg0, min(ntt, ttg0 + 4)))
        for fam in ("pj", "pjtm"):
            P.wait_if("t", fam + "_ev")
            P.wait_if("y", fam + "_g")
        for tgx in ("l2A", "l2B"):
            for sfx in ("_se", "_oe", "_ae", "_t1c", "_kc", "_oc"):
                P.wait_if("t", tgx + sfx)
        for sfx in ("_sc", "_ptc", "_oe"):
            P.wait_if("t", "at" + sfx)
        base_ld = [P.planned(ldtag[0]), P.planned(ldtag[1])]
        base_ic = P.planned(ictag)
        base_g = P.planned(gtag)
        for ic in range(nI):
            if ic >= 2:
                P.wait("y", ictag, base_ic + ic - 1)
            P.op("y", ldtag[ic % 2], 16, lambda y, ic=ic: y.dma_start(
                out=env.wsb[:, (ic % 2) * 1024:(ic % 2) * 1024 + O],
                in_=wt_bf[ic * 128:(ic + 1) * 128, :]))
        plain = bias_row_bf is None and extra_acc is None
        for ic in range(nI):
            P.wait("t", ldtag[ic % 2], base_ld[ic % 2] + 16 * (ic // 2 + 1))
            for q, tt in enumerate(tts):
                for oh in range(noh):
                    bank = env.pb[q * noh + oh]
                    last_pos = q == len(tts) - 1 and oh == noh - 1
                    stop = plain and ic == nI - 1

                    def mm(t, ic=ic, tt=tt, oh=oh, bank=bank, stop=stop):
                        xi = (xin_chunks(ic, tt) if callable(xin_chunks)
                              else xin_chunks[ic][:, tt * 128:(tt + 1) * 128])
                        return t.matmul(
                            bank[:, 0:512],
                            xi,
                            env.wsb[:, (ic % 2) * 1024 + oh * 512:(ic % 2) * 1024 + (oh + 1) * 512],
                            start=(ic == 0), stop=stop)
                    if stop:
                        P.op("t", gtag, 1, mm)
                    elif last_pos and ic < nI - 1:
                        P.op("t", ictag, 1, mm)
                    else:
                        P.run("t", lambda t, f=mm: f(t))
        if not plain:
            for q, tt in enumerate(tts):
                for oh in range(noh):
                    bank = env.pb[q * noh + oh]

                    def mmb(t, tt=tt, oh=oh, bank=bank, q=q):
                        if extra_acc is not None:
                            return extra_acc(q, tt, oh, noh)(t)
                        return t.matmul(bank[:, 0:512], env.ones_row_bf,
                                        bias_row_bf[0:1, oh * 512:(oh + 1) * 512],
                                        start=False, stop=True)
                    P.op("t", gtag, 1, mmb)
        for q, tt in enumerate(tts):
            for oh in range(noh):
                evac(P, tt, oh, env.pb[q * noh + oh][:, 0:512], gtag,
                     base_g + q * noh + oh + 1)



def emit_attention(P, env, kv_out, out_sink, tag, post_hook=None):
    def k_slice(r, lc):
        return kv_out[r * 2048:r * 2048 + D, lc:lc + 512]

    def v_tile(kt):
        r, tl = kt // 16, kt % 16
        row = r * 2048 + 1024 + tl * 64
        return kv_out[row:row + 64, :].rearrange("a (b c) -> (a b) c", c=D)
    """Softmax attention. Q^T bf16 in env.ar1 (feature-major, scale & bias
    folded). k_cc: DRAM bf16 [2*D, T] (rank-major K^T halves, mask folded).
    v_cc: DRAM bf16 [2*T, D] token-major V. out_sink(dc) -> bf16 [128, T] AP.
    AR2 = P^T sheet per q-chunk; env.f1 = fp32 score sheet per q-tile.
    PSUM: banks 4,5 scores; 6,7 transposes; 0..7 PV groups.
    """
    NKC, NKT = 8, 32
    sc, smm = tag + "_sc", tag + "_smm"
    for fam in ("pj", "pjtm"):
        P.wait_if("t", fam + "_ev")
    kslot = [env.kbuf, env.wsb]
    for qc in range(4):
        for qt4 in range(4):
            qt = qc * 4 + qt4
            base_kld = [P.planned(tag + "_kld0"), P.planned(tag + "_kld1")]
            base_sc = P.planned(sc)
            base_smm = P.planned(smm)
            for kc in range(NKC):
                r, lc = kc // 4, (kc % 4) * 512
                if base_smm + kc - 1 >= 1:
                    P.wait("y", smm, base_smm + kc - 1)
                P.op("y", tag + "_kld%d" % (kc % 2), 16,
                     lambda y, r=r, lc=lc, kc=kc: y.dma_start(
                         out=kslot[kc % 2][:, 0:4096]
                         .rearrange("p (ic t) -> p ic t", ic=8),
                         in_=k_slice(r, lc).rearrange("(ic p) t -> p ic t", p=128)))
            P.wait("s", tag + "_pt")
            for kc in range(NKC):
                bank = env.pb[4 + kc % 2]
                P.wait("t", tag + "_kld%d" % (kc % 2),
                       base_kld[kc % 2] + 16 * (kc // 2 + 1))
                if base_sc + kc - 1 >= 1:
                    P.wait("t", sc, base_sc + kc - 1)
                for ic in range(NC):
                    last = ic == NC - 1

                    def mm(t, ic=ic, bank=bank, qt=qt, last=last, kc=kc):
                        return t.matmul(
                            bank[:, 0:512],
                            env.ar1[:, ic * T + qt * 128:ic * T + (qt + 1) * 128],
                            kslot[kc % 2][:, ic * 512:(ic + 1) * 512],
                            start=(ic == 0), stop=last)
                    if last:
                        P.op("t", smm, 1, mm)
                    else:
                        P.run("t", lambda t, f=mm: f(t))
                P.wait("s", smm)
                P.op("s", sc, 1, lambda s, kc=kc, bank=bank: s.activation(
                    env.f1[:, kc * 512:(kc + 1) * 512], bank[:, 0:512], AF.Identity))
            P.wait("v", sc)
            P.op("v", tag + "_vm", 1, lambda v: v.reduce_max(
                env.mcol[:, 0:8],
                env.f1[:, 0:4096].rearrange("p (a b) -> p a b", a=8), axis=AX.X))
            P.wait("v", tag + "_vm")
            P.op("v", tag + "_vm", 1, lambda v: v.reduce_max(
                env.mcol[:, 8:9], env.mcol[:, 0:8], axis=AX.X))
            P.wait("v", tag + "_vm")
            P.op("v", tag + "_nm", 1, lambda v: v.tensor_scalar_mul(
                env.mcol[:, 9:10], env.mcol[:, 8:9], -1.0))
            P.wait("s", tag + "_nm")
            P.op("s", tag + "_ex", 1, lambda s: s.activation(
                env.f1[:, 0:4096], env.f1[:, 0:4096], AF.Exp,
                bias=env.mcol[:, 9:10]))
            P.wait("v", tag + "_ex")
            P.op("v", tag + "_vc", 1, lambda v: v.reduce_sum(
                env.lcol[:, 2:10],
                env.f1[:, 0:4096].rearrange("p (a b) -> p a b", a=8), axis=AX.X))
            P.wait("v", tag + "_vc")
            P.op("v", tag + "_vc", 1, lambda v: v.reduce_sum(
                env.lcol[:, 0:1], env.lcol[:, 2:10], axis=AX.X))
            P.wait("v", tag + "_vc")
            P.op("v", tag + "_vc", 1, lambda v: v.reciprocal(
                env.lcol[:, 1:2], env.lcol[:, 0:1]))
            P.wait("v", tag + "_vc")
            P.op("v", tag + "_ps", 1, lambda v: v.tensor_scalar_mul(
                env.f1[:, 0:4096], env.f1[:, 0:4096], env.lcol[:, 1:2]))
            if getattr(env, "dbg_hook", None):
                env.dbg_hook(P, qt)
            P.wait("t", tag + "_ps")
            base_ptc = P.planned(tag + "_ptc")
            for kt in range(NKT):
                if base_ptc + kt - 1 >= 1:
                    P.wait("t", tag + "_ptc", base_ptc + kt - 1)
                bank = env.pb[6 + kt % 2]

                def tr(t, kt=kt, bank=bank):
                    return t.transpose(bank[:, 0:128],
                                       env.f1[:, kt * 128:(kt + 1) * 128],
                                       env.ident_f32)
                P.op("t", tag + "_pt", 1, tr)
                P.wait("v", tag + "_pt")
                P.op("v", tag + "_ptc", 1, lambda v, kt=kt, qt4=qt4, bank=bank: v.tensor_copy(
                    env.ar2[:, kt * 512 + qt4 * 128:kt * 512 + (qt4 + 1) * 128],
                    bank[:, 0:128]))
        P.wait("t", tag + "_ptc")
        P.wait("y", smm)
        base_vld = [P.planned(tag + "_vld0"), P.planned(tag + "_vld1")]
        base_pv = P.planned(tag + "_pv")
        for kt in range(NKT):
            if kt >= 2:
                P.wait("y", tag + "_pv", base_pv + kt - 1)
            P.op("y", tag + "_vld%d" % (kt % 2), 16, lambda y, kt=kt: y.dma_start(
                out=env.vbuf[:, (kt % 2) * 1024:(kt % 2 + 1) * 1024],
                in_=v_tile(kt)))
        for kt in range(NKT):
            P.wait("t", tag + "_vld%d" % (kt % 2),
                   base_vld[kt % 2] + 16 * (kt // 2 + 1))
            stop = kt == NKT - 1
            for dc in range(NC):
                bank = env.pb[dc]

                def mm(t, kt=kt, dc=dc, bank=bank, stop=stop):
                    return t.matmul(
                        bank[:, 0:512],
                        env.vbuf[:, (kt % 2) * 1024 + dc * 128:(kt % 2) * 1024 + (dc + 1) * 128],
                        env.ar2[:, kt * 512:(kt + 1) * 512],
                        start=(kt == 0), stop=stop)
                if stop:
                    P.op("t", tag + "_og", 1, mm)
                elif dc == NC - 1:
                    P.op("t", tag + "_pv", 1, mm)
                else:
                    P.run("t", lambda t, f=mm: f(t))
        base_og = P.planned(tag + "_og") - NC
        for dc in range(NC):
            P.wait("v", tag + "_og", base_og + dc + 1)
            P.op("v", tag + "_oe", 1, lambda v, dc=dc, qc=qc: v.tensor_copy(
                out_sink(dc, qc), env.pb[dc][:, 0:512]))
        P.wait("t", tag + "_oe")
        P.wait("s", tag + "_oe")
        if post_hook is not None:
            post_hook(qc)



def build_env(P, ctx, attn=False):
    nc = P.nc
    env = Env()
    env.ctx = ctx
    sb = lambda n, sh, dt: env.ctx.enter_context(nc.sbuf_tensor(n, sh, dt))[:, :]
    env.xres = sb("xres", [128, NC * T], F32)
    env.xnorm = sb("xnorm", [128, NC * T], BF16)
    env.ar1 = sb("ar1", [128, 16384], BF16)
    env.ar2 = sb("ar2", [128, 16384], BF16)
    env.sq = env.ar1
    env.bo = sb("bo", [128, NC * 512], F32)
    env.wsb = sb("wsb", [128, 4096], BF16)
    env.tmpf = sb("tmpf", [128, 512], F32)
    if attn:
        env.kbuf = sb("kbuf", [128, 4096], BF16)
        env.vbuf = env.wsb
    env.f1 = env.bo
    env.mcol = sb("mcol", [128, 16], F32)
    env.lcol = sb("lcol", [128, 16], F32)
    env.stat_m = sb("stat_m", [1, 512], F32)
    env.stat_t = sb("stat_t", [1, 512], F32)
    env.stat_r = sb("stat_r", [1, 512], F32)
    env.stat_b = sb("stat_b", [1, 512], F32)
    env.ones_col_f32 = sb("ones_col_f32", [128, 1], F32)
    env.ones_row_bf = sb("ones_row_bf", [1, 128], BF16)
    env.ones_col_bf = sb("ones_col_bf", [128, 1], BF16)
    env.ones_row_f32 = sb("ones_row_f32", [1, 128], F32)
    env.eps_col = sb("eps_col", [128, 1], F32)
    env.pb = [env.ctx.enter_context(nc.psum_tensor(f"pb{i}", [128, 512], F32))[:, :]
              for i in range(8)]
    P.run("v", lambda v: v.memset(env.ones_col_f32, 1.0))
    P.run("v", lambda v: v.memset(env.eps_col, EPS_LN))
    P.run("v", lambda v: v.memset(env.ones_col_bf, 1.0))
    P.run("v", lambda v: v.memset(env.ones_row_bf, 1.0))
    P.op("v", "const", 1, lambda v: v.memset(env.ones_row_f32, 1.0))
    env.ident_f32 = sb("g_idf_sb", [128, 128], F32)
    env.ident_bf = sb("g_idb_sb", [128, 128], BF16)
    idf = nc.dram_tensor("g_identf32", [128, 128], F32, kind="ExternalInput").ap()
    idb = nc.dram_tensor("g_identbf", [128, 128], BF16, kind="ExternalInput").ap()
    P.op("y", "vload", 16, lambda y: y.dma_start(out=env.ident_f32, in_=idf))
    P.op("y", "vload", 16, lambda y: y.dma_start(out=env.ident_bf, in_=idb))
    return env


def emit_load_resid(P, env, rin):
    for c in range(NC):
        P.op("y", "rload", 16, lambda y, c=c: y.dma_start(
            out=fm(env.xres, c), in_=rin[c * 128:(c + 1) * 128, :]))
    P.wait("v", "rload")
    P.wait("s", "rload")
    P.wait("t", "rload")


def emit_store_resid(P, env, rout):
    P.wait_all("y")
    for c in range(NC):
        P.op("y", "rstore", 16, lambda y, c=c: y.dma_start(
            out=rout[c * 128:(c + 1) * 128, :], in_=fm(env.xres, c)))
    P.wait("y", "rstore")


def emit_l3_attn(P, env, W):
    nc = P.nc
    ts_all = [(a, a + 512) for a in range(0, T, 512)]

    P.wait("v", "vload")
    P.op("v", "lg1", 1, lambda v: v.tensor_scalar_mul(W["bq"], W["bq"], 1.0 / 32.0))
    P.op("v", "lg1", 1, lambda v: v.engine_nop())

    def xin_norm(ic, t0, t1):
        return fm(env.xnorm, ic)[:, t0:t1]

    def evac_q(P, oc, t0, t1, ps, gtag, gval):
        P.wait("s", gtag, gval)
        P.wait("s", "lg1")
        P.op("s", "pj_ev", 1, lambda s, oc=oc, t0=t0, t1=t1, ps=ps: s.activation(
            env.ar1[:, oc * T + t0:oc * T + t1], ps, AF.Identity,
            scale=1.0 / 32.0, bias=W["bq"][:, oc:oc + 1]))
    emit_proj(P, env, W["wqT_bf"], D, D, xin_norm, evac_q, "pj", ts_range=ts_all)

    def evac_k(P, oc, t0, t1, ps, gtag, gval):
        P.wait("s", gtag, gval)
        P.op("s", "pj_ev", 1, lambda s, oc=oc, t0=t0, t1=t1, ps=ps: s.activation(
            env.ar2[:, oc * T + t0:oc * T + t1], ps, AF.Identity,
            bias=W["bk"][:, oc:oc + 1]))
    emit_proj(P, env, W["wkT_bf"], D, D, xin_norm, evac_k, "pj", ts_range=ts_all)

    P.wait("t", "pj_ev")
    b_gml = P.planned("l3a_gml")
    b_msk = P.planned("l3a_msk")
    b_mskv = P.planned("l3a_mskv")
    for qc in range(4):
        bank = env.pb[4 + qc % 2]
        if qc > 0:
            P.wait("y", "l3a_msk", b_msk + qc)
        P.op("y", "l3a_gml", 16, lambda y, qc=qc: y.dma_start(
            out=env.gm_sb, in_=W["gmask"][0:1, qc * 512:(qc + 1) * 512]))
        if qc >= 2:
            P.wait("t", "l3a_mskv", b_mskv + (qc - 1) * NC)
        P.wait("t", "l3a_gml", b_gml + 16 * (qc + 1))
        P.op("t", "l3a_msk", 1, lambda t, qc=qc, bank=bank: t.matmul(
            bank[:, 0:512], env.ones_row_f32,
            env.gm_sb, start=True, stop=True))
        P.wait("v", "l3a_msk", b_msk + qc + 1)
        for c in range(NC):
            P.op("v", "l3a_mskv", 1, lambda v, c=c, qc=qc, bank=bank: v.tensor_mul(
                env.ar2[:, c * T + qc * 512:c * T + (qc + 1) * 512],
                env.ar2[:, c * T + qc * 512:c * T + (qc + 1) * 512],
                bank[:, 0:512]))
    P.wait("y", "l3a_mskv")
    for c in range(NC):
        P.op("y", "lg4", 16, lambda y, c=c: y.dma_start(
            out=env.l3k_in[c * 128:(c + 1) * 128, :], in_=fm(env.ar2, c)))

    def evac_v(P, tt, oh, ps, gtag, gval):
        P.wait("v", gtag, gval)
        P.wait("v", "lg4")
        P.op("v", "pj_ev", 1, lambda v, tt=tt, oh=oh, ps=ps: v.tensor_copy(
            env.ar2[:, tt * 1024 + oh * 512:tt * 1024 + (oh + 1) * 512], ps))
    P.wait("t", "l3a_mskv")
    P.wait("t", "vload")
    emit_proj_tm(P, env, W["wvT_bf"], D,
                 [fm(env.xnorm, ic) for ic in range(NC)],
                 env.bvT_bf[0:1, :], evac_v, "pj")
    P.wait("y", "pj_ev")
    for c in range(4):
        P.op("y", "lg4", 16, lambda y, c=c: y.dma_start(
            out=env.l3v_in[c * 512:(c + 1) * 512, :].rearrange("(tt p) o -> p tt o", p=128),
            in_=env.ar2[:, c * 4096:(c + 1) * 4096].rearrange("p (tt o) -> p tt o", tt=4)))

    P.wait("g", "lg4")
    P.wait("g", "lg4")
    P.op("g", "ccl3", 1, lambda g: g.collective_compute(
        "AllGather", ALU.bypass, replica_groups=[[0, 1], [2, 3], [4, 5], [6, 7]],
        ins=[env.l3kv_in], outs=[env.l3kv_out]))
    P.wait("y", "ccl3")
    P.wait("t", "ccl3")

    P.wait("t", "pj_ev")
    emit_attention(P, env, env.l3kv_out,
                   lambda dc, qc: fm(env.xnorm, dc)[:, qc * 512:(qc + 1) * 512], "at")

    def xin_o(ic, t0, t1):
        return fm(env.xnorm, ic)[:, t0:t1]
    for (t0, t1) in ts_all:
        def evac_o(P, oc, tt0, tt1, ps, gtag, gval):
            P.wait("s", gtag, gval)
            P.op("s", "pj_ev", 1, lambda s, oc=oc, ps=ps: s.activation(
                env.bo[:, oc * 512:(oc + 1) * 512], ps, AF.Identity,
                bias=W["bo"][:, oc:oc + 1]))
        emit_proj(P, env, W["woT_bf"], D, D, xin_o, evac_o, "pj",
                  ts_range=[(t0, t1)])
        P.wait("v", "pj_ev")
        src_b = [env.bo[:, c * 512:(c + 1) * 512] for c in range(NC)]
        add = [fm(env.xres, c)[:, t0:t1] for c in range(NC)]
        emit_ln_slice(P, env, src_b, add, tag=f"l3pl{t0}")


def setup_l3a(P, env, pfx=""):
    nc = P.nc
    env.bvT_bf = env.ctx.enter_context(nc.sbuf_tensor("bvT_bf", [1, D], BF16))[:, :]
    env.l3kv_in = nc.dram_tensor(pfx + "l3kv_in", [T, T], BF16, kind="Internal").ap()
    env.l3kv_out = nc.dram_tensor(pfx + "l3kv_out", [2 * T, T], BF16, kind="Internal").ap()
    env.l3k_in = env.l3kv_in[0:D, :]
    env.l3v_in = env.l3kv_in[D:T, :].rearrange("a (b c) -> (a b) c", c=D)
    W = {}
    W["wqT_bf"] = stage_weight(P, env, pfx + "wqT", (D, D))
    W["wkT_bf"] = stage_weight(P, env, pfx + "wkT", (D, D))
    W["wvT_bf"] = stage_weight(P, env, pfx + "wvT", (D, D))
    W["woT_bf"] = stage_weight(P, env, pfx + "woT", (D, D))
    W["bq"] = load_vec_input(P, env, pfx + "bq", [128, 8])
    W["bk"] = load_vec_input(P, env, pfx + "bk", [128, 8])
    W["bo"] = load_vec_input(P, env, pfx + "bo_c", [128, 8])
    bvT_dram = nc.dram_tensor(pfx + "bvT", [1, D], F32, kind="ExternalInput").ap()
    P.op("g", "wstage", 16, lambda g: g.dma_start(out=env.bvT_bf, in_=bvT_dram))
    W["gmask"] = nc.dram_tensor(pfx + "gmask", [1, T], F32, kind="ExternalInput").ap()
    env.gm_sb = env.ctx.enter_context(nc.sbuf_tensor("gm_sb", [1, 512], F32))[:, :]
    W["ident"] = env.ident_f32
    P.wait("t", "wstage")
    P.wait("y", "wstage")
    P.wait("t", "vload")
    P.wait("s", "vload")
    P.wait("v", "vload")
    P.wait("v", "const")
    P.wait("s", "const")
    P.wait("t", "const")
    emit_ln(P, env, [fm(env.xres, c) for c in range(NC)],
            [fm(env.xnorm, c) for c in range(NC)], tag="preln")
    emit_l3_attn(P, env, W)


def build_l3a_program():
    nc = bass.Bass("TRN2", target_bir_lowering=False)
    with ExitStack() as ctx:
        P = Plan(nc, ctx)
        rin = nc.dram_tensor("rin", [D, T], F32, kind="ExternalInput").ap()
        rout = nc.dram_tensor("rout", [D, T], F32, kind="ExternalOutput").ap()
        env = build_env(P, ctx, attn=True)
        emit_load_resid(P, env, rin)
        setup_l3a(P, env)
        emit_store_resid(P, env, rout)
        emit_block_sections(nc, P)
    return nc


def host_l3a_inputs(inp, core):
    b, h = core // 2, core % 2
    m = {}
    m["wqT"] = np.ascontiguousarray(inp["L3_attn_Wq"].T)
    m["wkT"] = np.ascontiguousarray(inp["L3_attn_Wk"].T)
    m["wvT"] = np.ascontiguousarray(inp["L3_attn_Wv"].T)
    m["woT"] = np.ascontiguousarray(inp["L3_attn_Wo"].T)
    m["bq"] = np.ascontiguousarray(inp["L3_attn_bq"].reshape(8, 128).T)
    m["bk"] = np.ascontiguousarray(inp["L3_attn_bk"].reshape(8, 128).T)
    m["bo_c"] = np.ascontiguousarray(inp["L3_attn_bo"].reshape(8, 128).T)
    m["bvT"] = np.ascontiguousarray(inp["L3_attn_bv"].reshape(1, D))
    pos = np.arange(4096, dtype=np.float64)
    gm = np.exp(-0.5 * ((pos - 2048.0) / 1024.0) ** 2).astype(np.float32)
    m["gmask"] = np.ascontiguousarray(gm[h * T:(h + 1) * T].reshape(1, T))
    m["ident_f32"] = np.eye(128, dtype=np.float32)
    return m


def run_l3a(x_resid, inp):
    nc = build_l3a_program()
    in_maps = []
    for c in range(8):
        b, h = c // 2, c % 2
        m = host_l3a_inputs(inp, c)
        add_global_inputs(m)
        m["rin"] = np.ascontiguousarray(x_resid[b, h * T:(h + 1) * T, :].T)
        in_maps.append(m)
    res = run_bass_kernel_spmd(nc, in_maps, core_ids=list(range(8)))
    out = np.empty((4, 4096, D), np.float32)
    for c in range(8):
        b, h = c // 2, c % 2
        out[b, h * T:(h + 1) * T, :] = res.results[c]["rout"].T
    return out


def emit_l1_attn(P, env, W):
    nc = P.nc
    m8 = W["m8"]
    ap_ = W["apar"]
    P.wait("s", "vload")
    P.op("s", "l1sp", 1, lambda s: s.activation(m8, ap_, AF.Abs))
    P.wait("s", "l1sp")
    P.op("s", "l1sp", 1, lambda s: s.activation(
        env.tmpf[:, 0:8], m8, AF.Exp, scale=-1.0))
    P.wait("s", "l1sp")
    P.op("s", "l1sp", 1, lambda s: s.activation(
        env.tmpf[:, 8:16], env.tmpf[:, 0:8], AF.Ln,
        bias=env.ones_col_f32[:, 0:1]))
    P.wait("v", "l1sp")
    P.op("v", "l1sp", 1, lambda v: v.tensor_scalar_max(
        env.tmpf[:, 16:24], ap_, 0.0))
    P.wait("v", "l1sp")
    P.op("v", "l1sp", 1, lambda v: v.tensor_add(
        m8, env.tmpf[:, 8:16], env.tmpf[:, 16:24]))
    P.wait("v", "l1sp")
    P.op("v", "l1sp", 1, lambda v: v.tensor_scalar_mul(m8, m8, -8.0))

    P.wait("t", "l1wl")
    for (w_off, bias, dst) in ((0, W["bin"], env.ar1), (1024, W["ba"], env.ar2)):
        for (t0, t1) in [(a, a + 512) for a in range(0, T, 512)]:
            base_g = P.planned("l1g")
            P.wait("t", "lg2")
            for h in range(NC):
                bank = env.pb[h]
                P.op("t", "l1g", 1, lambda t, h=h, t0=t0, t1=t1, bank=bank,
                     w_off=w_off: t.matmul(
                         bank[:, 0:512],
                         env.wsb[:, w_off + h * 128:w_off + (h + 1) * 128],
                         fm(env.xnorm, h)[:, t0:t1], start=True, stop=True))
            for h in range(NC):
                bank = env.pb[h]
                P.wait("s", "l1g", base_g + h + 1)
                P.op("s", "lg2", 1, lambda s, h=h, t0=t0, bank=bank, bias=bias,
                     dst=dst: s.activation(
                         dst[:, h * T + t0:h * T + t0 + 512], bank[:, 0:512],
                         AF.Sigmoid, bias=bias[:, h:h + 1]))

    A = env.f2[:, 0:1024]
    B = env.f2[:, 1024:2048]
    Y = env.f1[:, 0:2048]
    PR = env.f1[:, 2048:4096]
    flag = W["flag"]
    finv = W["finv"]
    P.wait("v", "lg2")
    P.wait("s", "lg2")
    for c in range(NC):
        for half in range(2):
            o = half * 1024
            ga = env.ar2[:, c * T + o:c * T + o + 1024]
            gx = env.ar1[:, c * T + o:c * T + o + 1024]
            xn = fm(env.xnorm, c)[:, o:o + 1024]
            v_ = "l1v"
            P.wait("v", "lg1")
            P.op("v", v_, 1, lambda v, ga=ga, c=c: v.tensor_scalar(
                A, ga, m8[:, c:c + 1], None, ALU.mult))
            P.wait("s", v_)
            P.op("s", v_, 1, lambda s: s.activation(A, A, AF.Exp))
            P.wait("v", v_)
            P.op("v", v_, 1, lambda v: v.tensor_mul(B, A, A))
            P.wait("v", v_)
            P.op("v", v_, 1, lambda v: v.tensor_scalar(
                B, B, -1.0, 1.0, ALU.mult, ALU.add))
            P.wait("s", v_)
            P.op("s", v_, 1, lambda s: s.activation(B, B, AF.Sqrt))
            P.wait("v", v_)
            for qq in (0, 512):
                P.op("v", v_, 1, lambda v, xn=xn, gx=gx, qq=qq: v.tensor_mul(
                    env.tmpf[:, 0:512], xn[:, qq:qq + 512], gx[:, qq:qq + 512]))
                P.wait("v", v_)
                P.op("v", v_, 1, lambda v, qq=qq: v.tensor_mul(
                    B[:, qq:qq + 512], env.tmpf[:, 0:512], B[:, qq:qq + 512]))
                P.wait("v", v_)
            if half == 0:
                P.op("v", v_, 1, lambda v: v.tensor_mul(
                    A[:, 0:1], A[:, 0:1], flag))
                P.op("v", v_, 1, lambda v, xn=xn, gx=gx: v.tensor_mul(
                    env.tmpf[:, 0:1], xn[:, 0:1], gx[:, 0:1]))
                P.wait("v", v_)
                P.op("v", v_, 1, lambda v: v.tensor_mul(
                    B[:, 0:1], B[:, 0:1], flag))
                P.op("v", v_, 1, lambda v: v.tensor_mul(
                    env.tmpf[:, 0:1], env.tmpf[:, 0:1], finv))
                P.wait("v", v_)
                P.op("v", v_, 1, lambda v: v.tensor_add(
                    B[:, 0:1], B[:, 0:1], env.tmpf[:, 0:1]))
                P.wait("v", v_)
            if half == 0:
                P.op("v", "lg1", 1, lambda v, o=o: v.tensor_tensor_scan(
                    Y[:, o:o + 1024], A, B, 0.0, ALU.mult, ALU.add))
                P.op("v", "lg1", 1, lambda v, o=o: v.tensor_tensor_scan(
                    PR[:, o:o + 1024], A, A, 1.0, ALU.mult, ALU.bypass))
                P.wait("v", "lg1")
            else:
                P.op("v", "lg1", 1, lambda v, o=o: v.tensor_tensor_scan(
                    Y[:, o:o + 1024], A, B, Y[:, 1023:1024], ALU.mult, ALU.add))
                P.op("v", "lg1", 1, lambda v, o=o: v.tensor_tensor_scan(
                    PR[:, o:o + 1024], A, A, PR[:, 1023:1024], ALU.mult, ALU.bypass))
                P.wait("v", "lg1")
        P.op("v", "lg1", 1, lambda v, c=c: v.tensor_copy(
            env.ar1[:, c * T:(c + 1) * T], Y))
        P.op("v", "lg1", 1, lambda v, c=c: v.tensor_copy(
            env.ar2[:, c * T:(c + 1) * T], PR))
        P.op("v", "lg1", 1, lambda v, c=c: v.tensor_copy(
            W["cout"][:, c:c + 1], Y[:, 2047:2048]))
        P.wait("v", "lg1")

    P.wait("y", "lg1")
    P.op("y", "lg4", 16, lambda y: y.dma_start(out=env.l1c_in, in_=W["cout"]))
    P.wait("g", "lg4")
    P.op("g", "ccl1", 1, lambda g: g.collective_compute(
        "AllGather", ALU.bypass, replica_groups=[[0, 1], [2, 3], [4, 5], [6, 7]],
        ins=[env.l1c_in], outs=[env.l1c_out]))
    P.wait("y", "ccl1")
    P.op("y", "lg4", 16, lambda y: y.dma_start(
        out=W["cin"], in_=env.l1c_out[0:128, :]))
    P.wait("v", "lg4")

    for c in (() if getattr(env, "l1a_no_fix", False) else range(NC)):
        P.op("v", "lg1", 1, lambda v, c=c: v.tensor_mul(
            env.tmpf[:, 0:1], W["cin"][:, c:c + 1], flag))
        P.wait("v", "lg1")
        P.op("v", "lg1", 1, lambda v, c=c: v.tensor_scalar(
            Y, env.ar2[:, c * T:(c + 1) * T], env.tmpf[:, 0:1], None, ALU.mult))
        P.wait("v", "lg1")
        P.op("v", "lg1", 1, lambda v, c=c: v.tensor_add(
            env.ar1[:, c * T:(c + 1) * T], env.ar1[:, c * T:(c + 1) * T],
            env.f1[:, 0:2048]))
        P.wait("v", "lg1")

    P.wait("s", "lg1")
    P.wait("t", "lg1")
    for (t0, t1) in (() if getattr(env, "l1a_no_postln", False) else
                     [(a, a + 512) for a in range(0, T, 512)]):
        src_b = [env.ar1[:, c * T + t0:c * T + t1] for c in range(NC)]
        add = [fm(env.xres, c)[:, t0:t1] for c in range(NC)]
        emit_ln_slice(P, env, src_b, add, tag="ln", src_bf=True,
                      sqbuf=env.ar2[:, 0:4096])


def setup_l1a(P, env, pfx=""):
    nc = P.nc
    env.f2 = env.kbuf.bitcast(F32)
    env.l1c_in = nc.dram_tensor(pfx + "l1c_in", [128, 8], F32, kind="Internal").ap()
    env.l1c_out = nc.dram_tensor(pfx + "l1c_out", [256, 8], F32, kind="Internal").ap()
    W = {}
    winT = nc.dram_tensor(pfx + "winT", [D, 128], F32, kind="ExternalInput").ap()
    waT = nc.dram_tensor(pfx + "waT", [D, 128], F32, kind="ExternalInput").ap()
    winT_bf = stage_weight_from(P, env, winT, pfx + "winT_bf", (D, 128))
    waT_bf = stage_weight_from(P, env, waT, pfx + "waT_bf", (D, 128))
    P.wait("y", "wstage")
    P.op("y", "lg4", 16, lambda y: y.dma_start(
        out=env.wsb[:, 0:1024].rearrange("p (h j) -> p h j", h=8),
        in_=winT_bf.rearrange("(h p) j -> p h j", p=128)))
    P.op("y", "lg4", 16, lambda y: y.dma_start(
        out=env.wsb[:, 1024:2048].rearrange("p (h j) -> p h j", h=8),
        in_=waT_bf.rearrange("(h p) j -> p h j", p=128)))
    W["bin"] = load_vec_input(P, env, pfx + "bin", [128, 8])
    W["ba"] = load_vec_input(P, env, pfx + "ba", [128, 8])
    W["apar"] = load_vec_input(P, env, pfx + "apar", [128, 8])
    W["flag"] = load_vec_input(P, env, pfx + "flag", [128, 1])
    W["finv"] = load_vec_input(P, env, pfx + "finv", [128, 1])
    W["m8"] = env.ctx.enter_context(nc.sbuf_tensor("m8", [128, 8], F32))[:, :]
    W["cout"] = env.ctx.enter_context(nc.sbuf_tensor("cout", [128, 8], F32))[:, :]
    W["cin"] = env.ctx.enter_context(nc.sbuf_tensor("cin", [128, 8], F32))[:, :]
    for e in ("t", "y"):
        P.wait(e, "wstage")
    for e in ("t", "s", "v"):
        P.wait(e, "vload")
    for e in ("v", "s", "t"):
        P.wait(e, "const")
    emit_ln(P, env, [fm(env.xres, c) for c in range(NC)],
            [fm(env.xnorm, c) for c in range(NC)], tag="ln")
    emit_l1_attn(P, env, W)


def build_l1a_program():
    nc = bass.Bass("TRN2", target_bir_lowering=False)
    with ExitStack() as ctx:
        P = Plan(nc, ctx)
        rin = nc.dram_tensor("rin", [D, T], F32, kind="ExternalInput").ap()
        rout = nc.dram_tensor("rout", [D, T], F32, kind="ExternalOutput").ap()
        env = build_env(P, ctx, attn=True)
        emit_load_resid(P, env, rin)
        setup_l1a(P, env)
        emit_store_resid(P, env, rout)
        emit_block_sections(nc, P)
    return nc


def stage_weight_from(P, env, src_ap, name, shape):
    nc = P.nc
    dst = nc.dram_tensor(name, list(shape), BF16, kind="Internal").ap()
    P.op("g", "wstage", 16, lambda g: g.dma_start(out=dst, in_=src_ap),
         mark="stage")
    return dst


def host_l1a_inputs(inp, core):
    b, h = core // 2, core % 2
    m = {}
    m["winT"] = np.ascontiguousarray(inp["L1_attn_w_in"].reshape(1024, 128))
    m["waT"] = np.ascontiguousarray(inp["L1_attn_w_a"].reshape(1024, 128))
    m["bin"] = np.ascontiguousarray(inp["L1_attn_b_in"].reshape(8, 128).T)
    m["ba"] = np.ascontiguousarray(inp["L1_attn_b_a"].reshape(8, 128).T)
    m["apar"] = np.ascontiguousarray(inp["L1_attn_a_param"].reshape(8, 128).T)
    m["flag"] = np.full((128, 1), float(h), np.float32)
    m["finv"] = np.full((128, 1), 1.0 - float(h), np.float32)
    return m


def run_l1a(x_resid, inp):
    nc = build_l1a_program()
    in_maps = []
    for c in range(8):
        b, h = c // 2, c % 2
        m = host_l1a_inputs(inp, c)
        add_global_inputs(m)
        m["rin"] = np.ascontiguousarray(x_resid[b, h * T:(h + 1) * T, :].T)
        in_maps.append(m)
    res = run_bass_kernel_spmd(nc, in_maps, core_ids=list(range(8)))
    out = np.empty((4, 4096, D), np.float32)
    for c in range(8):
        b, h = c // 2, c % 2
        out[b, h * T:(h + 1) * T, :] = res.results[c]["rout"].T
    return out


def l3f_router_hook(W):
    def hook(P, env, si, width):
        q0 = si * 512
        pz, pr_, pb_ = env.pb[4], env.pb[5], env.pb[6]
        P.wait("t", "l3z_e")
        for ic in range(NC):
            def mm(t, ic=ic, q0=q0):
                return t.matmul(pz[0:64, 0:512], W["rwf"][:, ic * 64:(ic + 1) * 64],
                                fm(env.xres, ic)[:, q0:q0 + 512],
                                start=(ic == 0), stop=(ic == NC - 1))
            P.run("t", lambda t, f=mm: f(t))
        def bc(t):
            t.matmul(pr_[0:64, 0:512], env.ones_row_f32[0:1, 0:64],
                     env.stat_r[0:1, 0:512], start=True, stop=True)
            return t.matmul(pb_[0:64, 0:512], env.ones_row_f32[0:1, 0:64],
                            env.stat_b[0:1, 0:512], start=True, stop=True)
        P.op("t", "l3z_mm", 1, bc)
        P.wait("v", "l3z_mm")
        zf = env.f1[0:64, q0:q0 + 512]
        P.op("v", "l3z_e", 1, lambda v: v.tensor_copy(zf, pz[0:64, 0:512]))
        P.wait("v", "l3z_e")
        P.op("v", "l3z_e", 1, lambda v: v.tensor_mul(zf, zf, pr_[0:64, 0:512]))
        P.wait("v", "l3z_e")
        P.op("v", "l3z_e", 1, lambda v: v.scalar_tensor_tensor(
            zf, pb_[0:64, 0:512], W["scol"], zf, ALU.mult, ALU.add))
        P.wait("v", "l3z_e")
        P.op("v", "l3z_e", 1, lambda v: v.tensor_scalar_add(zf, zf, W["rbcol"]))
    return hook


def emit_l3_ffn(P, env, W):
    nc = P.nc
    NTT = 16
    zt, m8t, mt = W["zt"], W["m8t"], W["mt"]
    cw_sb = W["cw_sb"]

    for tt in range(NTT):
        bank = env.pb[tt % 2]
        if tt >= 2:
            P.wait("t", "l3f_zc", P.planned("l3f_zc") - 1)
        P.op("t", "lg3", 1, lambda t, tt=tt, bank=bank: t.transpose(
            bank[:, 0:64], env.f1[0:64, tt * 128:(tt + 1) * 128],
            env.ident_f32[0:64, 0:64]))
        P.wait("v", "lg3")
        P.op("v", "l3f_zc", 1, lambda v, bank=bank: v.tensor_copy(zt, bank[:, 0:64]))
        P.wait("v", "l3f_zc")
        P.op("v", "lg1", 1, lambda v: v.max(out=m8t[:, 0:8], in_=zt))
        P.wait("v", "lg1")
        P.op("v", "lg1", 1, lambda v: v.tensor_scalar_mul(
            m8t[:, 8:9], m8t[:, 0:1], -1.0))
        P.wait("v", "lg1")
        P.op("v", "lg1", 1, lambda v: v.tensor_scalar(
            mt, zt, m8t[:, 7:8], None, ALU.is_ge))
        P.wait("s", "lg1")
        P.op("s", "lg2", 1, lambda s: s.activation(zt, zt, AF.Exp,
                                                      bias=m8t[:, 8:9]))
        P.wait("v", "lg2")
        P.op("v", "lg1", 1, lambda v: v.tensor_mul(zt, zt, mt))
        P.wait("v", "lg1")
        P.op("v", "lg1", 1, lambda v: v.reduce_sum(m8t[:, 9:10], zt, axis=AX.X))
        P.wait("v", "lg1")
        P.op("v", "lg1", 1, lambda v: v.reciprocal(m8t[:, 10:11], m8t[:, 9:10]))
        P.wait("v", "lg1")
        P.op("v", "lg1", 1, lambda v: v.tensor_scalar_mul(zt, zt, m8t[:, 10:11]))
        P.wait("t", "lg1")
        bank2 = env.pb[2 + tt % 2]
        P.op("t", "lg3", 1, lambda t, bank2=bank2: t.transpose(
            bank2[0:64, 0:128], zt, env.ident_f32))
        P.wait("v", "lg3")
        P.op("v", "l3f_zc", 1, lambda v, tt=tt, bank2=bank2: v.tensor_copy(
            cw_sb[:, tt * 128:(tt + 1) * 128], bank2[0:64, 0:128]))
    P.wait("g", "l3f_zc")
    P.op("g", "lg5", 16, lambda g: g.dma_start(out=env.l3f_cw, in_=cw_sb))

    hsh = env.f1.bitcast(BF16)
    w1bf, w2bf, sw1bf, sw2bf = W["w1_bf"], W["w2_bf"], W["sw1_bf"], W["sw2_bf"]
    NV = 80

    def h_slice(ve, hc):
        if ve < 32:
            return env.ar1[:, ve * 512 + hc * 256: ve * 512 + (hc + 1) * 256]
        if ve < 64:
            return env.ar2[:, (ve - 32) * 512 + hc * 256:(ve - 32) * 512 + (hc + 1) * 256]
        return hsh[:, (ve - 64) * 512 + hc * 256:(ve - 64) * 512 + (hc + 1) * 256]

    def w1_src(ve):
        if ve < 64:
            return w1bf[ve * 1024:(ve + 1) * 1024, :].rearrange(
                "(ic p) o -> p ic o", p=128)
        s, k = (ve - 64) // 8, (ve - 64) % 8
        return sw1bf[s * 1024:(s + 1) * 1024, k * 256:(k + 1) * 256].rearrange(
            "(ic p) o -> p ic o", p=128)

    def w2_src(ve):
        if ve < 64:
            return w2bf[ve * 256:(ve + 1) * 256, :].rearrange(
                "(hc p) o -> p hc o", p=128)
        s, k = (ve - 64) // 8, (ve - 64) % 8
        return sw2bf[s * 2048 + k * 256:s * 2048 + (k + 1) * 256, :].rearrange(
            "(hc p) o -> p hc o", p=128)

    for t0 in range(0, T, 256):
        P.wait("t", "lg2")
        base_wl = P.planned("l3f_wl")
        base_cw = P.planned("l3f_cwl")
        base_hg = P.planned("l3f_hg")
        base_he = P.planned("l3f_he")
        base_cb = P.planned("l3f_cb")
        base_hee = P.planned("l3f_hee")
        for ve in range(NV):
            if ve >= 2:
                P.wait("y", "l3f_hg", base_hg + 2 * ve - 2)
            P.op("y", "l3f_wl", 16, lambda y, ve=ve: y.dma_start(
                out=env.wsb[:, (ve % 2) * 2048:(ve % 2) * 2048 + 2048]
                    .rearrange("p (ic o) -> p ic o", o=256),
                in_=w1_src(ve)))
            if ve < 64 and ve % 2 == 0:
                pr = ve // 2
                if pr >= 1:
                    P.wait("g", "l3f_cb", base_cb + 2 * pr)
                P.op("g", "l3f_cwl", 16, lambda g_, pr=pr, t0=t0: g_.dma_start(
                    out=W["cwrow"][0:1, :].rearrange("p (a b) -> p a b", a=2),
                    in_=env.l3f_cw[2 * pr:2 * pr + 2, t0:t0 + 256][None, :, :]))
        for ve in range(NV):
            P.wait("t", "l3f_wl", base_wl + 16 * (ve + 1))
            routed = ve < 64
            if routed:
                cwb = env.pb[4 + ve % 2]
                P.wait("t", "l3f_cwl", base_cw + 16 * (ve // 2 + 1))
                if ve >= 2:
                    P.wait("t", "l3f_hee", base_hee + ve - 1)
                P.op("t", "l3f_cb", 1, lambda t, ve=ve, cwb=cwb: t.matmul(
                    cwb[:, 0:256], env.ones_row_bf,
                    W["cwrow"][0:1, (ve % 2) * 256:(ve % 2) * 256 + 256],
                    start=True, stop=True))
            for hc in range(2):
                bank = env.pb[(ve % 2) * 2 + hc]
                if ve >= 2:
                    P.wait("t", "l3f_he", base_he + 2 * ve + hc - 3)
                for ic in range(NC):
                    def mm(t, ve=ve, hc=hc, ic=ic, bank=bank, t0=t0):
                        return t.matmul(
                            bank[:, 0:256],
                            env.wsb[:, (ve % 2) * 2048 + ic * 256 + hc * 128:
                                    (ve % 2) * 2048 + ic * 256 + (hc + 1) * 128],
                            fm(env.xnorm, ic)[:, t0:t0 + 256],
                            start=(ic == 0), stop=(ic == NC - 1))
                    if ic == NC - 1:
                        P.op("t", "l3f_hg", 1, mm)
                    else:
                        P.run("t", lambda t, f=mm: f(t))
                P.wait("s", "l3f_hg")
                P.op("s", "l3f_he", 1, lambda s, ve=ve, hc=hc, bank=bank: s.activation(
                    h_slice(ve, hc), bank[:, 0:256], AF.Silu,
                    bias=W["b1cols"][:, 2 * ve + hc:2 * ve + hc + 1]))
            P.wait("v", "l3f_he")
            if routed:
                def tt2(v, ve=ve, cwb=cwb):
                    v.tensor_mul(h_slice(ve, 0), h_slice(ve, 0), cwb[:, 0:256])
                    return v.tensor_mul(h_slice(ve, 1), h_slice(ve, 1), cwb[:, 0:256])
                P.op("v", "l3f_hee", 1, tt2)
            else:
                P.op("v", "l3f_hee", 1, lambda v, ve=ve: v.tensor_copy(
                    env.tmpf[:, 0:1], h_slice(ve, 0)[:, 0:1]))
        P.wait("t", "l3f_hee")
        P.wait("y", "l3f_hg")
        base_w2 = P.planned("l3f_w2l")
        base_pv = P.planned("l3f_pv")
        base_og = P.planned("l3f_og")
        for ve in range(NV):
            if ve >= 2:
                P.wait("y", "l3f_pv", base_pv + ve - 1)
            P.op("y", "l3f_w2l", 16, lambda y, ve=ve: y.dma_start(
                out=env.wsb[:, (ve % 2) * 2048:(ve % 2) * 2048 + 2048]
                    .rearrange("p (hc o) -> p hc o", o=1024),
                in_=w2_src(ve)))
        for ve in range(NV):
            P.wait("t", "l3f_w2l", base_w2 + 16 * (ve + 1))
            for hc in range(2):
                for oc in range(NC):
                    last = hc == 1 and oc == NC - 1

                    def mm(t, ve=ve, hc=hc, oc=oc, t0=t0):
                        return t.matmul(
                            env.pb[oc][:, 0:256],
                            env.wsb[:, (ve % 2) * 2048 + hc * 1024 + oc * 128:
                                    (ve % 2) * 2048 + hc * 1024 + (oc + 1) * 128],
                            h_slice(ve, hc), start=(ve == 0 and hc == 0),
                            stop=False)
                    if last:
                        P.op("t", "l3f_pv", 1, mm)
                    else:
                        P.run("t", lambda t, f=mm: f(t))
        P.wait("y", "l3f_pv", base_pv + 80)
        P.op("y", "lg4", 16, lambda y: y.dma_start(
            out=env.wsb[0:64, 0:1024], in_=W["b2_bf"]))
        P.op("y", "lg4", 16, lambda y: y.dma_start(
            out=env.wsb[0:2, 1024:2048], in_=W["sb2_bf"]))
        P.wait("v", "l3f_pv", base_pv + 80)
        P.op("v", "lg1", 1, lambda v: v.memset(env.wsb[0:2, 2048:2304], 1.0))
        P.wait("t", "lg4")
        P.wait("t", "lg1")
        for oc in range(NC):
            P.run("t", lambda t, oc=oc, t0=t0: t.matmul(
                env.pb[oc][:, 0:256], env.wsb[0:64, oc * 128:(oc + 1) * 128],
                cw_sb[:, t0:t0 + 256], start=False, stop=False))
            P.op("t", "l3f_og", 1, lambda t, oc=oc: t.matmul(
                env.pb[oc][:, 0:256], env.wsb[0:2, 1024 + oc * 128:1024 + (oc + 1) * 128],
                env.wsb[0:2, 2048:2304], start=False, stop=True))
        for oc in range(NC):
            P.wait("s", "l3f_og", base_og + oc + 1)
            P.op("s", "lg2", 1, lambda s, oc=oc: s.activation(
                env.bo2[:, oc * 256:(oc + 1) * 256], env.pb[oc][:, 0:256],
                AF.Identity))
        P.wait("v", "lg2")
        P.wait("s", "lg2")
        src_b = [env.bo2[:, c * 256:(c + 1) * 256] for c in range(NC)]
        add = [fm(env.xres, c)[:, t0:t0 + 256] for c in range(NC)]
        emit_ln_span(P, env, src_b, None, add, "ln", width=256)


def setup_l3f(P, env, pfx=""):
    nc = P.nc
    env.bo2 = env.f1[:, 0:2048]
    env.l3f_cw = nc.dram_tensor(pfx + "l3f_cw", [64, T], F32, kind="Internal").ap()
    W = {}
    W["w1_bf"] = stage_weight(P, env, pfx + "w1r", (65536, 256))
    W["w2_bf"] = stage_weight(P, env, pfx + "w2r", (16384, 1024))
    W["sw1_bf"] = stage_weight(P, env, pfx + "sw1r", (2048, 2048))
    W["sw2_bf"] = stage_weight(P, env, pfx + "sw2r", (4096, 1024))
    W["rwf"] = load_vec_input(P, env, pfx + "rwf", [128, 512])
    W["rbcol"] = load_vec_input(P, env, pfx + "rbcol", [64, 1])
    W["scol"] = env.ctx.enter_context(nc.sbuf_tensor("scol", [64, 1], F32))[:, :]
    W["b1cols"] = load_vec_input(P, env, pfx + "b1cols", [128, 160])
    W["b2_bf"] = stage_weight(P, env, pfx + "b2d", (64, 1024))
    W["sb2_bf"] = stage_weight(P, env, pfx + "sb2d", (2, 1024))
    W["ident"] = env.ident_f32
    sbt = lambda n, sh, dt=F32: env.ctx.enter_context(nc.sbuf_tensor(n, sh, dt))[:, :]
    if hasattr(env, "kbuf"):
        W["zt"] = env.kbuf.bitcast(F32)[:, 0:64]
        W["m8t"] = env.kbuf.bitcast(F32)[:, 64:80]
        W["mt"] = env.kbuf[:, 160:224]
        W["cw_sb"] = env.kbuf[0:64, 1024:3072]
    else:
        W["zt"] = sbt("zt", [128, 64])
        W["m8t"] = sbt("m8t", [128, 16])
        W["mt"] = sbt("mt", [128, 64], BF16)
        W["cw_sb"] = sbt("cw_sbt", [64, T], BF16)
    W["cwrow"] = env.tmpf[0:1, 256:512].bitcast(BF16)
    for e in ("t", "y", "g"):
        P.wait(e, "wstage")
    for e in ("t", "s", "v"):
        P.wait(e, "vload")
    for e in ("v", "s", "t"):
        P.wait(e, "const")
    P.wait("t", "vload")
    P.wait("v", "const")
    def smm(t):
        for ic in range(NC):
            t.matmul(env.pb[7][0:1, 0:64], env.ones_col_f32,
                     W["rwf"][:, ic * 64:(ic + 1) * 64],
                     start=(ic == 0), stop=(ic == NC - 1))
        return t.transpose(env.pb[6][0:64, 0:1], env.pb[7][0:1, 0:64],
                           env.ident_f32) if False else None
    P.op("t", "l3f_sc", 1, lambda t: [smm(t), t.nop()][1]) if False else None
    for ic in range(NC):
        def mm(t, ic=ic):
            return t.matmul(env.pb[7][0:1, 0:64], env.ones_col_f32,
                            W["rwf"][:, ic * 64:(ic + 1) * 64],
                            start=(ic == 0), stop=(ic == NC - 1))
        if ic == NC - 1:
            P.op("t", "l3f_sc", 1, mm)
        else:
            P.run("t", lambda t, f=mm: f(t))
    P.wait("v", "l3f_sc")
    P.op("v", "l3f_sc", 1, lambda v: v.tensor_copy(
        env.tmpf[0:1, 0:64], env.pb[7][0:1, 0:64]))
    P.wait("t", "l3f_sc")
    P.op("t", "l3f_sc", 1, lambda t: t.transpose(
        env.pb[6][0:64, 0:1], env.tmpf[0:1, 0:64], env.ident_f32[0:1, 0:1]))
    P.wait("v", "l3f_sc")
    P.op("v", "l3z_e", 1, lambda v: v.tensor_copy(W["scol"], env.pb[6][0:64, 0:1]))
    emit_ln(P, env, [fm(env.xres, c) for c in range(NC)],
            [fm(env.xnorm, c) for c in range(NC)], tag="ln",
            span_hook=l3f_router_hook(W))
    emit_l3_ffn(P, env, W)


def build_l3f_program():
    nc = bass.Bass("TRN2", target_bir_lowering=False)
    with ExitStack() as ctx:
        P = Plan(nc, ctx)
        rin = nc.dram_tensor("rin", [D, T], F32, kind="ExternalInput").ap()
        rout = nc.dram_tensor("rout", [D, T], F32, kind="ExternalOutput").ap()
        env = build_env(P, ctx, attn=False)
        emit_load_resid(P, env, rin)
        setup_l3f(P, env)
        emit_store_resid(P, env, rout)
        emit_block_sections(nc, P)
    return nc


def host_l3f_inputs(inp, core):
    m = {}
    m["w1r"] = np.ascontiguousarray(inp["L3_ffn_w1"].reshape(65536, 256))
    m["w2r"] = np.ascontiguousarray(inp["L3_ffn_w2"].reshape(16384, 1024))
    m["sw1r"] = np.ascontiguousarray(inp["L3_ffn_sw1"].reshape(2048, 2048))
    m["sw2r"] = np.ascontiguousarray(inp["L3_ffn_sw2"].reshape(4096, 1024))
    m["rwf"] = np.ascontiguousarray(
        inp["L3_ffn_router_w"].reshape(8, 128, 64).transpose(1, 0, 2).reshape(128, 512))
    m["rbcol"] = np.ascontiguousarray(inp["L3_ffn_router_b"].reshape(64, 1))
    b1 = inp["L3_ffn_b1"]
    sb1 = inp["L3_ffn_sb1"]
    cols = np.zeros((128, 160), np.float32)
    for e in range(64):
        cols[:, 2 * e:2 * e + 2] = b1[e].reshape(2, 128).T
    for s in range(2):
        for k in range(8):
            ve = 64 + 8 * s + k
            cols[:, 2 * ve:2 * ve + 2] = sb1[s, k * 256:(k + 1) * 256].reshape(2, 128).T
    m["b1cols"] = cols
    m["b2d"] = np.ascontiguousarray(inp["L3_ffn_b2"])
    m["sb2d"] = np.ascontiguousarray(inp["L3_ffn_sb2"])
    m["ident_f32"] = np.eye(128, dtype=np.float32)
    return m


def run_l3f(x_resid, inp):
    nc = build_l3f_program()
    in_maps = []
    for c in range(8):
        b, h = c // 2, c % 2
        m = host_l3f_inputs(inp, c)
        add_global_inputs(m)
        m["rin"] = np.ascontiguousarray(x_resid[b, h * T:(h + 1) * T, :].T)
        in_maps.append(m)
    res = run_bass_kernel_spmd(nc, in_maps, core_ids=list(range(8)))
    out = np.empty((4, 4096, D), np.float32)
    for c in range(8):
        b, h = c // 2, c % 2
        out[b, h * T:(h + 1) * T, :] = res.results[c]["rout"].T
    return out


def emit_l1_ffn(P, env, W):
    nc = P.nc
    NTT = 16
    f1 = env.f1

    ztm = W["ztm"]
    cols = W["wkcols"]
    P.wait_if("t", "l3z_e")
    for tr in range(3):
        for tt in range(NTT):
            bank = env.pb[4 + tt % 2]
            if P.planned("l1f_wk") >= 2:
                P.wait("t", "l1f_wk", P.planned("l1f_wk") - 1)
            P.op("t", "lg3", 1, lambda t, tr=tr, tt=tt, bank=bank: t.transpose(
                bank[:, 0:15], f1[32 * tr:32 * tr + 15, tt * 128:(tt + 1) * 128],
                env.ident_f32[32 * tr:32 * tr + 15, 32 * tr:32 * tr + 15]))
            P.wait("v", "lg3")
            P.op("v", "l1f_wk", 1, lambda v, bank=bank: v.tensor_copy(
                ztm[:, 0:15], bank[:, 0:15]))
            wk = "l1f_wk"

            def vop(fn):
                P.wait("v", wk)
                P.op("v", wk, 1, fn)
            s_ = lambda i: ztm[:, i:i + 1]
            c_ = lambda i: cols[:, i:i + 1]
            vop(lambda v: v.tensor_scalar(c_(0), s_(0), 0.0, None, ALU.is_ge))
            vop(lambda v: v.tensor_sub(c_(4), s_(2), s_(1)))
            vop(lambda v: v.scalar_tensor_tensor(c_(4), c_(4), 1.0, c_(0),
                                                 ALU.mult, ALU.mult))
            vop(lambda v: v.tensor_add(c_(4), c_(4), s_(1)))
            vop(lambda v: v.tensor_scalar(c_(1), c_(4), 0.0, None, ALU.is_ge))
            vop(lambda v: v.tensor_sub(c_(4), s_(4), s_(3)))
            vop(lambda v: v.tensor_mul(c_(4), c_(4), c_(1)))
            vop(lambda v: v.tensor_add(c_(4), c_(4), s_(3)))
            vop(lambda v: v.tensor_sub(c_(5), s_(6), s_(5)))
            vop(lambda v: v.tensor_mul(c_(5), c_(5), c_(1)))
            vop(lambda v: v.tensor_add(c_(5), c_(5), s_(5)))
            vop(lambda v: v.tensor_sub(c_(5), c_(5), c_(4)))
            vop(lambda v: v.tensor_mul(c_(5), c_(5), c_(0)))
            vop(lambda v: v.tensor_add(c_(4), c_(4), c_(5)))
            vop(lambda v: v.tensor_scalar(c_(2), c_(4), 0.0, None, ALU.is_ge))
            for i in range(4):
                vop(lambda v, i=i: v.tensor_sub(c_(4 + i if False else 4), s_(8 + 2 * i), s_(7 + 2 * i)))
                vop(lambda v, i=i: v.tensor_mul(c_(4), c_(4), c_(2)))
                vop(lambda v, i=i: v.tensor_add(ztm[:, i:i + 1] if False else c_(4), c_(4), s_(7 + 2 * i)))
                vop(lambda v, i=i: v.tensor_copy(cols[:, 4 + 0:5] if False else ztm[:, i:i + 1], c_(4)))
            for j in range(2):
                vop(lambda v, j=j: v.tensor_sub(c_(4), ztm[:, 2 * j + 1:2 * j + 2],
                                                ztm[:, 2 * j:2 * j + 1]))
                vop(lambda v, j=j: v.tensor_mul(c_(4), c_(4), c_(1)))
                vop(lambda v, j=j: v.tensor_add(c_(4), c_(4), ztm[:, 2 * j:2 * j + 1]))
                vop(lambda v, j=j: v.tensor_copy(c_(6 + j), c_(4)))
            vop(lambda v: v.tensor_sub(c_(4), c_(7), c_(6)))
            vop(lambda v: v.tensor_mul(c_(4), c_(4), c_(0)))
            vop(lambda v: v.tensor_add(c_(4), c_(4), c_(6)))
            vop(lambda v: v.tensor_scalar(c_(3), c_(4), 0.0, None, ALU.is_ge))
            vop(lambda v: v.tensor_scalar_mul(c_(4), c_(0), 8.0))
            vop(lambda v: v.scalar_tensor_tensor(c_(5), c_(1), 4.0, c_(4),
                                                 ALU.mult, ALU.add))
            vop(lambda v: v.scalar_tensor_tensor(c_(4), c_(2), 2.0, c_(5),
                                                 ALU.mult, ALU.add))
            vop(lambda v: v.tensor_add(c_(4), c_(4), c_(3)))
            vop(lambda v: v.tensor_scalar(ztm[:, 0:16], W["iota16"], c_(4), None,
                                          ALU.is_equal))
            P.wait("t", "l1f_wk")
            bank2 = env.pb[6 + tt % 2]
            P.op("t", "lg3", 1, lambda t, bank2=bank2: t.transpose(
                bank2[0:16, 0:128], ztm[:, 0:16], env.ident_f32))
            P.wait("v", "lg3")
            f1b = env.f1.bitcast(BF16)
            P.op("v", "l1f_wk", 1, lambda v, tr=tr, tt=tt, bank2=bank2, f1b=f1b: v.tensor_copy(
                f1b[32 * tr:32 * tr + 16, 4096 + tt * 128:4096 + (tt + 1) * 128],
                bank2[0:16, 0:128]))

    P.wait("y", "l1f_wk")
    for tr in range(3):
        P.op("y", "lg4", 16, lambda y, tr=tr: y.dma_start(
            out=env.l1f_mrows[16 * tr:16 * tr + 16, :],
            in_=env.f1.bitcast(BF16)[32 * tr:32 * tr + 16, 4096:6144]))

    def xin_norm(ic, t0, t1):
        return fm(env.xnorm, ic)[:, t0:t1]

    def xin_h(ic, t0, t1):
        return env.ar2[:, ic * T + t0:ic * T + t1]

    for tri, tree in enumerate(("v", "k", "q")):
        tr = {"q": 0, "k": 1, "v": 2}[tree]
        w1bf = W[f"w1{tree}_bf"]
        w2bf = W[f"w2{tree}_bf"]
        b1c = W["b1cols"]

        def evac_h(P, oc, t0, t1, ps, gtag, gval, tr=tr, b1c=b1c):
            P.wait("s", gtag, gval)
            P.op("s", "pj_ev", 1, lambda s, oc=oc, t0=t0, t1=t1, ps=ps: s.activation(
                env.ar2[:, oc * T + t0:oc * T + t1], ps, AF.Relu,
                bias=b1c[:, 8 * tr + oc:8 * tr + oc + 1]))
        emit_proj(P, env, w1bf, D, D, xin_norm, evac_h, "pj",
                  ts_range=[(a, a + 512) for a in range(0, T, 512)])
        P.wait("t", "pj_ev")
        P.wait("g", "lg4")
        rowb = env.tmpf[0:1, :].bitcast(BF16)
        for t0 in range(0, T, 512):
            for lp in range(8):
                bank = env.pb[4 + lp % 2]
                P.wait("g", "lg3")
                P.op("g", "lg5", 16, lambda g, tr=tr, lp=lp, t0=t0: g.dma_start(
                    out=rowb.rearrange("p (a b) -> p a b", a=2),
                    in_=env.l1f_mrows[16 * tr + 2 * lp:16 * tr + 2 * lp + 2,
                                      t0:t0 + 512][None, :, :]))
                if P.planned("l1f_mm") >= 2:
                    P.wait("t", "l1f_mm", P.planned("l1f_mm") - 1)
                P.wait("t", "lg5")

                def bc(t, bank=bank):
                    t.matmul(bank[0:64, 0:512], env.ones_row_bf[0:1, 0:64],
                             rowb[0:1, 0:512], start=True, stop=True)
                    return t.matmul(bank[64:128, 0:512], env.ones_row_bf[0:1, 0:64],
                                    rowb[0:1, 512:1024], start=True, stop=True)
                P.op("t", "lg3", 1, bc)
                P.wait("v", "lg3")
                P.op("v", "l1f_mm", 1, lambda v, lp=lp, t0=t0, bank=bank: v.tensor_mul(
                    env.ar2[:, lp * T + t0:lp * T + t0 + 512],
                    env.ar2[:, lp * T + t0:lp * T + t0 + 512], bank[:, 0:512]))
        P.wait("t", "l1f_mm")
        if tree == "q":
            def evac_o(P, oc, t0, t1, ps, gtag, gval):
                P.wait("s", gtag, gval)
                P.op("s", "pj_ev", 1, lambda s, oc=oc, t0=t0, t1=t1, ps=ps: s.activation(
                    env.ar1[:, oc * T + t0:oc * T + t1], ps, AF.Identity,
                    scale=1.0 / 32.0))

            def extra_q(oc, t0, t1, bank, tr=tr):
                def f(t, oc=oc, t0=t0, t1=t1, bank=bank):
                    return t.matmul(
                        bank[:, 0:t1 - t0],
                        W["b2all"][32 * tr:32 * tr + 16, oc * 128:(oc + 1) * 128],
                        env.f1.bitcast(BF16)[32 * tr:32 * tr + 16,
                                             4096 + t0:4096 + t1],
                        start=False, stop=True)
                return f
            emit_proj(P, env, w2bf, D, D, xin_h, evac_o, "pj",
                      ts_range=[(a, a + 512) for a in range(0, T, 512)],
                      extra_acc=extra_q)
        elif tree == "k":
            def evac_k(P, oc, t0, t1, ps, gtag, gval):
                P.wait("v", gtag, gval)
                P.wait("v", "lg4")
                P.wait_if("v", "lg4")
                P.op("v", "pj_ev", 1, lambda v, oc=oc, ps=ps: v.tensor_copy(
                    env.kbuf[:, oc * 512:(oc + 1) * 512], ps))
                P.wait("y", "pj_ev")
                P.op("y", "lg4", 16, lambda y, oc=oc, t0=t0: y.dma_start(
                    out=env.l1kv_in[oc * 128:(oc + 1) * 128, t0:t0 + 512],
                    in_=env.kbuf[:, oc * 512:(oc + 1) * 512]))

            def extra_k(oc, t0, t1, bank, tr=tr):
                def f(t, oc=oc, t0=t0, t1=t1, bank=bank):
                    return t.matmul(
                        bank[:, 0:t1 - t0],
                        W["b2all"][32 * tr:32 * tr + 16, oc * 128:(oc + 1) * 128],
                        env.f1.bitcast(BF16)[32 * tr:32 * tr + 16,
                                             4096 + t0:4096 + t1],
                        start=False, stop=True)
                return f
            emit_proj(P, env, w2bf, D, D, xin_h, evac_k, "pj",
                      ts_range=[(a, a + 512) for a in range(0, T, 512)],
                      extra_acc=extra_k)
        else:
            def evac_v(P, tt, oh, ps, gtag, gval):
                P.wait("v", gtag, gval)
                P.wait("v", "lg4")
                P.wait_if("v", "lg4")
                P.op("v", "pjtm_ev", 1, lambda v, tt=tt, oh=oh, ps=ps: v.tensor_copy(
                    env.kbuf[:, (tt % 4) * 1024 + oh * 512:(tt % 4) * 1024 + (oh + 1) * 512],
                    ps))
                if tt % 4 == 3 and oh == 1:
                    P.wait("y", "pjtm_ev")
                    ttg = tt // 4
                    P.op("y", "lg4", 16, lambda y, ttg=ttg: y.dma_start(
                        out=env.l1v_in[ttg * 512:(ttg + 1) * 512, :].rearrange(
                            "(tt p) o -> p tt o", p=128),
                        in_=env.kbuf[:, 0:4096].rearrange("p (tt o) -> p tt o", tt=4)))

            def extra_v(q, tt, oh, noh, tr=tr):
                def f(t, q=q, tt=tt, oh=oh, noh=noh):
                    return t.matmul(
                        env.pb[q * noh + oh][:, 0:512],
                        env.f1.bitcast(BF16)[32 * tr:32 * tr + 16,
                                             4096 + tt * 128:4096 + (tt + 1) * 128],
                        W["b2all"][32 * tr:32 * tr + 16, oh * 512:(oh + 1) * 512],
                        start=False, stop=True)
                return f
            emit_proj_tm(P, env, w2bf, D, xin_h_chunks(env), None, evac_v, "pjtm",
                         extra_acc=extra_v)

    P.wait("g", "lg4")
    P.wait("g", "lg4")
    P.op("g", "ccl1f", 1, lambda g: g.collective_compute(
        "AllGather", ALU.bypass, replica_groups=[[0, 1], [2, 3], [4, 5], [6, 7]],
        ins=[env.l1kv_in], outs=[env.l1kv_out]))
    P.wait("y", "ccl1f")
    P.wait("t", "ccl1f")

    def post(qc):
        t0 = qc * 512
        src_b = [env.f1[:, c * 512:(c + 1) * 512] for c in range(NC)]
        add = [fm(env.xres, c)[:, t0:t0 + 512] for c in range(NC)]
        emit_ln_span(P, env, src_b, None, add, "ln", width=512,
                     sqbuf=env.xnorm[:, 0:4096])
    emit_attention(P, env, env.l1kv_out,
                   lambda dc, qc: env.f1[:, dc * 512:(dc + 1) * 512],
                   "at", post_hook=post)


def xin_h_chunks(env):
    return [env.ar2[:, ic * T:(ic + 1) * T] for ic in range(NC)]


def l1f_z_hook(W):
    def hook(P, env, si, width):
        q0 = si * 512
        pz, pr_, pb_ = env.pb[4], env.pb[5], env.pb[6]
        P.wait("t", "l3z_e")
        def bc(t):
            t.matmul(pr_[0:96, 0:512], env.ones_row_f32[0:1, 0:96],
                     env.stat_r[0:1, 0:512], start=True, stop=True)
            return t.matmul(pb_[0:96, 0:512], env.ones_row_f32[0:1, 0:96],
                            env.stat_b[0:1, 0:512], start=True, stop=True)
        P.op("t", "l3z_mm", 1, bc)
        for tr in range(3):
            for ic in range(NC):
                def mm(t, tr=tr, ic=ic, q0=q0):
                    return t.matmul(
                        pz[32 * tr:32 * tr + 15, 0:512],
                        W["ndwf"][:, ic * 48 + 16 * tr:ic * 48 + 16 * tr + 15],
                        fm(env.xres, ic)[:, q0:q0 + 512],
                        start=(ic == 0), stop=(ic == NC - 1))
                if ic == NC - 1 and tr == 2:
                    P.op("t", "l3z_mm", 1, mm)
                else:
                    P.run("t", lambda t, f=mm: f(t))
        P.wait("v", "l3z_mm")
        for tr in range(3):
            zf = env.f1[32 * tr:32 * tr + 15, q0:q0 + 512]
            P.op("v", "l3z_e", 1, lambda v, tr=tr, zf=zf: v.tensor_copy(
                zf, pz[32 * tr:32 * tr + 15, 0:512]))
            P.wait("v", "l3z_e")
            P.op("v", "l3z_e", 1, lambda v, tr=tr, zf=zf: v.tensor_mul(
                zf, zf, pr_[32 * tr:32 * tr + 15, 0:512]))
            P.wait("v", "l3z_e")
            P.op("v", "l3z_e", 1, lambda v, tr=tr, zf=zf: v.scalar_tensor_tensor(
                zf, pb_[32 * tr:32 * tr + 15, 0:512],
                W["nscol"][32 * tr:32 * tr + 15, 0:1], zf, ALU.mult, ALU.add))
            P.wait("v", "l3z_e")
            P.op("v", "l3z_e", 1, lambda v, tr=tr, zf=zf: v.tensor_scalar_add(
                zf, zf, W["nbcol"][32 * tr:32 * tr + 15, 0:1]))
            P.wait("v", "l3z_e")
    return hook


def setup_l1f(P, env, pfx=""):
    nc = P.nc
    env.l1kv_in = nc.dram_tensor(pfx + "l1kv_in", [T, T], BF16, kind="Internal").ap()
    env.l1f_mrows = nc.dram_tensor(pfx + "l1f_mrows", [48, T], BF16, kind="Internal").ap()
    env.l1kv_out = nc.dram_tensor(pfx + "l1kv_out", [2 * T, T], BF16, kind="Internal").ap()
    env.l1v_in = env.l1kv_in[D:T, :].rearrange("a (b c) -> (a b) c", c=D)
    W = {}
    for tree in ("q", "k", "v"):
        W[f"w1{tree}_bf"] = stage_weight(P, env, pfx + f"w1{tree}", (D, D))
        W[f"w2{tree}_bf"] = stage_weight(P, env, pfx + f"w2{tree}", (D, D))
    W["ndwf"] = load_vec_input(P, env, pfx + "ndwf", [128, 384])
    W["nbcol"] = load_vec_input(P, env, pfx + "nbcol", [96, 1])
    W["nscol"] = env.ctx.enter_context(nc.sbuf_tensor("nscol", [96, 1], F32))[:, :]
    W["b2all"] = load_vec_input(P, env, pfx + "b2all", [96, 1024], BF16)
    W["b1cols"] = load_vec_input(P, env, pfx + "b1cols", [128, 24])
    W["iota16"] = load_vec_input(P, env, pfx + "iota16", [128, 16], BF16)
    W["ident"] = env.ident_f32
    W["ztm"] = env.mcol
    W["wkcols"] = env.lcol[:, 0:8]
    for e in ("t", "y"):
        P.wait(e, "wstage")
    for e in ("t", "s", "v"):
        P.wait(e, "vload")
    for e in ("v", "s", "t"):
        P.wait(e, "const")
    for tr in range(3):
        for ic in range(NC):
            def mm(t, tr=tr, ic=ic):
                return t.matmul(env.pb[7][0:1, 0:15], env.ones_col_f32,
                                W["ndwf"][:, ic * 48 + 16 * tr:ic * 48 + 16 * tr + 15],
                                start=(ic == 0), stop=(ic == NC - 1))
            if ic == NC - 1:
                P.op("t", "l1f_sc", 1, mm)
            else:
                P.run("t", lambda t, f=mm: f(t))
        P.wait("v", "l1f_sc")
        P.op("v", "l1f_sc", 1, lambda v, tr=tr: v.tensor_copy(
            env.tmpf[0:1, 0:15], env.pb[7][0:1, 0:15]))
        P.wait("t", "l1f_sc")
        P.op("t", "l1f_sc", 1, lambda t, tr=tr: t.transpose(
            env.pb[6][0:15, 0:1], env.tmpf[0:1, 0:15], env.ident_f32[0:1, 0:1]))
        P.wait("v", "l1f_sc")
        P.op("v", "l3z_e", 1, lambda v, tr=tr: v.tensor_copy(
            W["nscol"][32 * tr:32 * tr + 15, 0:1], env.pb[6][0:15, 0:1]))
    emit_ln(P, env, [fm(env.xres, c) for c in range(NC)],
            [fm(env.xnorm, c) for c in range(NC)], tag="ln",
            span_hook=l1f_z_hook(W))
    emit_l1_ffn(P, env, W)


def build_l1f_program():
    nc = bass.Bass("TRN2", target_bir_lowering=False)
    with ExitStack() as ctx:
        P = Plan(nc, ctx)
        rin = nc.dram_tensor("rin", [D, T], F32, kind="ExternalInput").ap()
        rout = nc.dram_tensor("rout", [D, T], F32, kind="ExternalOutput").ap()
        env = build_env(P, ctx, attn=True)
        emit_load_resid(P, env, rin)
        setup_l1f(P, env)
        emit_store_resid(P, env, rout)
        emit_block_sections(nc, P)
    return nc


def host_l1f_inputs(inp, core):
    m = {}
    for tree in ("q", "k", "v"):
        w1 = inp[f"L1_ffn_{tree}_w1"]
        w2 = inp[f"L1_ffn_{tree}_w2"]
        m[f"w1{tree}"] = np.ascontiguousarray(
            w1.transpose(1, 0, 2).reshape(1024, 1024))
        m[f"w2{tree}"] = np.ascontiguousarray(w2.reshape(1024, 1024))
    ndwf = np.zeros((128, 384), np.float32)
    nbcol = np.zeros((96, 1), np.float32)
    b2all = np.zeros((96, 1024), np.float32)
    b1cols = np.zeros((128, 24), np.float32)
    for tr, tree in enumerate(("q", "k", "v")):
        nw = inp[f"L1_ffn_{tree}_node_w"]
        nwp = nw.T.reshape(8, 128, 15).transpose(1, 0, 2)
        for ic in range(8):
            ndwf[:, ic * 48 + 16 * tr:ic * 48 + 16 * tr + 15] = nwp[:, ic, :]
        nbcol[32 * tr:32 * tr + 15, 0] = inp[f"L1_ffn_{tree}_node_b"]
        b2all[32 * tr:32 * tr + 16, :] = inp[f"L1_ffn_{tree}_b2"]
        b1cols[:, 8 * tr:8 * tr + 8] = inp[f"L1_ffn_{tree}_b1"].reshape(1024).reshape(8, 128).T
    m["ndwf"] = ndwf
    m["nbcol"] = nbcol
    import ml_dtypes as _md2
    m["b2all"] = b2all.astype(_md2.bfloat16)
    m["b1cols"] = b1cols
    import ml_dtypes as _md3
    m["iota16"] = np.tile(np.arange(16, dtype=np.float32), (128, 1)).astype(_md3.bfloat16)
    m["ident_f32"] = np.eye(128, dtype=np.float32)
    return m


def run_l1f(x_resid, inp):
    nc = build_l1f_program()
    in_maps = []
    for c in range(8):
        b, h = c // 2, c % 2
        m = host_l1f_inputs(inp, c)
        add_global_inputs(m)
        m["rin"] = np.ascontiguousarray(x_resid[b, h * T:(h + 1) * T, :].T)
        in_maps.append(m)
    res = run_bass_kernel_spmd(nc, in_maps, core_ids=list(range(8)))
    out = np.empty((4, 4096, D), np.float32)
    for c in range(8):
        b, h = c // 2, c % 2
        out[b, h * T:(h + 1) * T, :] = res.results[c]["rout"].T
    return out


def emit_l2a_qkg(P, env, W, t0, want_q):
    def xin_norm(ic, tt0, tt1):
        return fm(env.xnorm, ic)[:, tt0:tt1]

    for tgx in ("l2A", "l2B"):
        for sfx in ("_se", "_oe", "_ae", "_t1c", "_kc", "_oc", "_otr"):
            P.wait_if("t", tgx + sfx)

    if want_q:
        def evac_q(P, oc, tt0, tt1, ps, gtag, gval):
            P.wait("s", gtag, gval)
            P.op("s", "pj_ev", 1, lambda s, oc=oc, ps=ps: s.activation(
                env.ar1[:, oc * 512:(oc + 1) * 512], ps, AF.Identity, scale=0.125))
        emit_proj(P, env, W["wq_bf"], D, D, xin_norm, evac_q, "pj",
                  ts_range=[(t0, t0 + 512)])

    def evac_k(P, oc, tt0, tt1, ps, gtag, gval):
        P.wait("s", gtag, gval)
        P.op("s", "pj_ev", 1, lambda s, oc=oc, ps=ps: s.activation(
            env.ar1[:, 4096 + oc * 512:4096 + (oc + 1) * 512], ps, AF.Identity))
    emit_proj(P, env, W["wk_bf"], D, D, xin_norm, evac_k, "pj",
              ts_range=[(t0, t0 + 512)])

    P.wait("t", "pj_ev")
    P.wait("t", "l2g_e")
    for ic in range(NC):
        def mm(t, ic=ic, t0=t0):
            return t.matmul(env.pb[6][0:16, 0:512], W["wg1c"][:, ic * 16:(ic + 1) * 16],
                            fm(env.xnorm, ic)[:, t0:t0 + 512],
                            start=(ic == 0), stop=(ic == NC - 1))
        if ic == NC - 1:
            P.op("t", "lg3", 1, mm)
        else:
            P.run("t", lambda t, f=mm: f(t))
    P.wait("v", "lg3")
    P.op("v", "lg1", 1, lambda v: v.tensor_copy(W["t1g"], env.pb[6][0:16, 0:512]))
    P.wait("t", "lg1")
    base_ge = P.planned("l2g_e")
    P.wait("t", "l2A_se")
    P.wait("t", "l2B_se")
    for oc in range(NC):
        bank = env.pb[4 + oc % 2]
        if oc >= 2:
            P.wait("t", "l2g_e", base_ge + 3 * (oc - 1))
        P.op("t", "lg3", 1, lambda t, oc=oc, bank=bank: t.matmul(
            bank[:, 0:512], W["wg2c"][0:16, oc * 128:(oc + 1) * 128], W["t1g"],
            start=True, stop=True))
        P.wait("s", "lg3")
        zf = env.f1[:, oc * 512:(oc + 1) * 512]
        P.op("s", "lg2", 1, lambda s, oc=oc, bank=bank, zf=zf: s.activation(
            zf, bank[:, 0:512], AF.Identity, bias=W["bg2c"][:, oc:oc + 1]))
        P.wait("s", "lg2")
        P.wait("s", "l2g_e")
        P.op("s", "lg2", 1, lambda s, zf=zf: s.activation(
            env.tmpf[:, 0:512], zf, AF.Abs))
        P.wait("s", "lg2")
        P.op("s", "lg2", 1, lambda s: s.activation(
            env.tmpf[:, 0:512], env.tmpf[:, 0:512], AF.Exp, scale=-1.0))
        P.wait("s", "lg2")
        P.op("s", "lg2", 1, lambda s: s.activation(
            env.tmpf[:, 0:512], env.tmpf[:, 0:512], AF.Ln,
            bias=env.ones_col_f32[:, 0:1]))
        P.wait("v", "lg2")
        P.op("v", "l2g_e", 1, lambda v, zf=zf: v.tensor_scalar_min(zf, zf, 0.0))
        P.wait("v", "l2g_e")
        P.op("v", "l2g_e", 1, lambda v, zf=zf: v.tensor_sub(zf, zf, env.tmpf[:, 0:512]))
        P.wait("v", "l2g_e")
        P.op("v", "l2g_e", 1, lambda v, zf=zf: v.tensor_scalar_mul(zf, zf, 1.0 / 16.0))


def emit_l2_chunk(P, env, W, ch, is_b):
    tg = "l2B" if is_b else "l2A"
    o = (ch % 4) * 128
    Sf, Sb = W["Sf"], W["Sb"]
    P.wait("v", tg + "_fin")
    for dt_ in range(NC):
        P.op("v", tg + "_cs", 1, lambda v, dt_=dt_, o=o: v.tensor_tensor_scan(
            env.f1[:, dt_ * 512 + o:dt_ * 512 + o + 128],
            env.f1[:, dt_ * 512 + o:dt_ * 512 + o + 128],
            env.f1[:, dt_ * 512 + o:dt_ * 512 + o + 128],
            0.0, ALU.add, ALU.bypass))
    P.wait("s", tg + "_cs")
    for dt_ in range(NC):
        P.op("s", tg + "_ex", 1, lambda s, dt_=dt_, o=o: s.activation(
            env.eL[:, dt_ * 128:(dt_ + 1) * 128],
            env.f1[:, dt_ * 512 + o:dt_ * 512 + o + 128], AF.Exp))
        P.op("s", tg + "_ex", 1, lambda s, dt_=dt_, o=o: s.activation(
            env.eLn[:, dt_ * 128:(dt_ + 1) * 128],
            env.f1[:, dt_ * 512 + o:dt_ * 512 + o + 128], AF.Exp, scale=-1.0))
        P.op("s", tg + "_ex", 1, lambda s, dt_=dt_, o=o: s.activation(
            W["eLend"][:, dt_:dt_ + 1],
            env.f1[:, dt_ * 512 + o + 127:dt_ * 512 + o + 128], AF.Exp))
    P.wait("v", tg + "_ex")
    for dt_ in range(NC):
        kk = env.ar1[:, 4096 + dt_ * 512 + o:4096 + dt_ * 512 + o + 128]
        if is_b:
            qq = env.ar1[:, dt_ * 512 + o:dt_ * 512 + o + 128]
            P.op("v", tg + "_qs", 1, lambda v, dt_=dt_, qq=qq: v.tensor_mul(
                qq, qq, env.eL[:, dt_ * 128:(dt_ + 1) * 128]))
        P.op("v", tg + "_qs", 1, lambda v, dt_=dt_, kk=kk: v.tensor_mul(
            kk, kk, env.eLn[:, dt_ * 128:(dt_ + 1) * 128]))
    P.wait("v", tg + "_qs")
    P.wait("t", tg + "_qs")
    if is_b:
        P.wait("v", tg + "_og")
        for h in range(16):
            dt_, b_ = h // 2, (h % 2) * 64
            bank = env.pb[4 + h % 2]
            if P.planned(tg + "_ae") >= 2:
                P.wait("t", tg + "_ae", P.planned(tg + "_ae") - 1)
            P.op("t", tg + "_amm", 1, lambda t, dt_=dt_, b_=b_, o=o, bank=bank: t.matmul(
                bank[0:128, 0:128],
                env.ar1[b_:b_ + 64, 4096 + dt_ * 512 + o:4096 + dt_ * 512 + o + 128],
                env.ar1[b_:b_ + 64, dt_ * 512 + o:dt_ * 512 + o + 128],
                start=True, stop=True))
            P.wait("v", tg + "_amm")
            P.op("v", tg + "_ae", 1, lambda v, h=h, bank=bank: v.tensor_mul(
                env.abuf[:, h * 128:(h + 1) * 128], bank[0:128, 0:128], W["triu"]))
    for dt_ in range(NC):
        kk = env.ar1[:, 4096 + dt_ * 512 + o:4096 + dt_ * 512 + o + 128]
        P.op("v", tg + "_ke", 1, lambda v, dt_=dt_, kk=kk: v.tensor_scalar(
            kk, kk, W["eLend"][:, dt_:dt_ + 1], None, ALU.mult))
    P.wait("t", tg + "_ke")
    for dt_ in range(NC):
        bank = env.pb[6 + dt_ % 2]
        if P.planned(tg + "_kc") >= 2:
            P.wait("t", tg + "_kc", P.planned(tg + "_kc") - 1)
        P.op("t", tg + "_ktr", 1, lambda t, dt_=dt_, o=o, bank=bank: t.transpose(
            bank.bitcast(BF16)[:, 0:128],
            env.ar1[:, 4096 + dt_ * 512 + o:4096 + dt_ * 512 + o + 128],
            env.ident_bf))
        P.wait("v", tg + "_ktr")
        P.op("v", tg + "_kc", 1, lambda v, dt_=dt_, bank=bank: v.tensor_copy(
            env.ktm[:, dt_ * 128:(dt_ + 1) * 128], bank.bitcast(BF16)[:, 0:128]))
    P.wait("y", tg + "_smm")
    if is_b:
        P.wait("y", tg + "_og")
    P.wait_if("y", "lg4")
    P.op("y", tg + "_vl", 16, lambda y, ch=ch: y.dma_start(
        out=env.vch, in_=env.l2v[ch * 128:(ch + 1) * 128, :]))
    P.wait("t", tg + "_vl")
    P.wait("t", tg + "_kc")
    if is_b:
        P.wait("t", tg + "_sbc")
        base_oe = P.planned(tg + "_oe")
        for h in range(16):
            dt_, b_ = h // 2, (h % 2) * 64
            bank = env.pb[h % 2]
            if P.planned(tg + "_oe") >= 2:
                P.wait("t", tg + "_oe", P.planned(tg + "_oe") - 1)

            def mmo(t, h=h, dt_=dt_, b_=b_, o=o, bank=bank):
                t.matmul(bank[0:64, 0:128], env.vch[:, h * 64:(h + 1) * 64],
                         env.abuf[:, h * 128:(h + 1) * 128], start=True, stop=False)
                return t.matmul(bank[0:64, 0:128],
                                Sb[b_:b_ + 64, (h // 2) * 64:(h // 2 + 1) * 64],
                                env.ar1[b_:b_ + 64, dt_ * 512 + o:dt_ * 512 + o + 128],
                                start=False, stop=True)
            P.op("t", tg + "_og", 1, mmo)
            P.wait("v", tg + "_og")
            P.op("v", tg + "_oe", 1, lambda v, h=h, bank=bank: v.tensor_copy(
                env.osb[0:64, (h % 2) * 128:(h % 2) * 128 + 128], bank[0:64, 0:128]))
            P.wait("t", tg + "_oe")
            bank2 = env.pb[2 + h % 2]
            P.op("t", tg + "_otr", 1, lambda t, h=h, bank2=bank2: t.transpose(
                bank2.bitcast(BF16)[:, 0:64],
                env.osb[0:64, (h % 2) * 128:(h % 2) * 128 + 128],
                env.ident_bf[0:64, 0:64]))
            P.wait("v", tg + "_otr")
            P.op("v", tg + "_oc", 1, lambda v, h=h, ch=ch, bank2=bank2: v.tensor_copy(
                env.ar2[:, ch * 1024 + h * 64:ch * 1024 + (h + 1) * 64],
                bank2.bitcast(BF16)[:, 0:64]))
        P.wait("t", tg + "_oc")
    for h in range(16):
        dt_, b_ = h // 2, (h % 2) * 64
        bank = env.pb[4 + h % 2]
        if P.planned(tg + "_se") >= 2:
            P.wait("t", tg + "_se", P.planned(tg + "_se") - 1)
        P.op("t", tg + "_smm", 1, lambda t, h=h, bank=bank: t.matmul(
            bank[0:64, 0:64], env.ktm[:, h * 64:(h + 1) * 64],
            env.vch[:, h * 64:(h + 1) * 64], start=True, stop=True))
        P.wait("v", tg + "_smm")

        P.op("v", tg + "_s1", 1, lambda v, h=h, dt_=dt_, b_=b_: v.tensor_scalar(
            Sf[b_:b_ + 64, (h // 2) * 64:(h // 2 + 1) * 64],
            Sf[b_:b_ + 64, (h // 2) * 64:(h // 2 + 1) * 64],
            W["eLend"][b_:b_ + 64, dt_:dt_ + 1], None, ALU.mult))
        P.wait("v", tg + "_s1")
        P.op("v", tg + "_se", 1, lambda v, h=h, b_=b_, bank=bank: v.tensor_add(
            Sf[b_:b_ + 64, (h // 2) * 64:(h // 2 + 1) * 64],
            Sf[b_:b_ + 64, (h // 2) * 64:(h // 2 + 1) * 64],
            bank[0:64, 0:64]))
    P.wait("v", tg + "_se")
    if is_b:
        P.op("v", tg + "_sbc", 1, lambda v: v.tensor_copy(Sb, Sf))


def emit_l2_attn(P, env, W):
    nc = P.nc
    Sf, Sb = W["Sf"], W["Sb"]
    P.wait("v", "vload")
    P.op("v", "lg1", 1, lambda v: v.memset(Sf, 0.0))
    for its, t0 in enumerate(range(0, T, 512)):
        emit_l2a_qkg(P, env, W, t0, want_q=False)
        P.wait("v", "pj_ev")
        P.wait("v", "l2g_e")

        def evac_v(P, tt, oh, ps, gtag, gval):
            P.wait("v", gtag, gval)
            P.wait("v", "lg4")
            P.op("v", "pjtm_ev", 1, lambda v, tt=tt, oh=oh, ps=ps: v.tensor_copy(
                env.ar2[:, (tt % 4) * 1024 + oh * 512:(tt % 4) * 1024 + (oh + 1) * 512],
                ps))
            if tt % 4 == 3 and oh == 1:
                P.wait("y", "pjtm_ev")
                ttg = tt // 4
                P.op("y", "lg4", 16, lambda y, ttg=ttg: y.dma_start(
                    out=env.l2v[ttg * 512:(ttg + 1) * 512, :].rearrange(
                        "(a p) o -> p a o", p=128),
                    in_=env.ar2[:, 0:4096].rearrange("p (a o) -> p a o", a=4)))
        P.wait("t", "l2g_e")
        emit_proj_tm(P, env, W["wv_bf"], D,
                     [fm(env.xnorm, ic) for ic in range(NC)],
                     None, evac_v, "pjtm", ts_sub=(t0, t0 + 512))
        P.wait("v", "pjtm_ev")
        for cc in range(4):
            emit_l2_chunk(P, env, W, its * 4 + cc, is_b=False)
    P.wait("v", "l2A_se")
    P.op("v", "lg1", 1, lambda v: v.tensor_copy(W["Scc"], Sf))
    P.wait("y", "lg1")
    P.op("y", "lg4", 16, lambda y: y.dma_start(out=env.l2c_in, in_=W["Scc"]))
    P.wait("g", "lg4")
    P.op("g", "ccl2", 1, lambda g: g.collective_compute(
        "AllGather", ALU.bypass, replica_groups=[[0, 1], [2, 3], [4, 5], [6, 7]],
        ins=[env.l2c_in], outs=[env.l2c_out]))
    P.wait("y", "ccl2")
    P.op("y", "lg4", 16, lambda y: y.dma_start(
        out=W["Scc"], in_=env.l2c_out[0:128, :]))
    P.wait("v", "lg4")
    P.op("v", "lg1", 1, lambda v: v.tensor_scalar(
        Sf, W["Scc"], W["flag"], None, ALU.mult))
    P.wait("v", "lg1")
    P.op("v", "lg1", 1, lambda v: v.tensor_copy(Sb, Sf))
    for its, t0 in enumerate(range(0, T, 512)):
        emit_l2a_qkg(P, env, W, t0, want_q=True)
        P.wait("v", "pj_ev")
        P.wait("v", "l2g_e")
        for cc in range(4):
            emit_l2_chunk(P, env, W, its * 4 + cc, is_b=True)
        P.wait("s", "l2B_oc")
        for cc in range(4):
            ch = its * 4 + cc
            sqv = env.eL
            P.op("s", "lg2", 1, lambda s, ch=ch, sqv=sqv: s.activation(
                sqv[:, 0:1024], env.ar2[:, ch * 1024:(ch + 1) * 1024], AF.Square))
            P.wait("v", "lg2")
            P.op("v", "lg1", 1, lambda v, sqv=sqv: v.reduce_sum(
                W["rcol"][:, 0:16],
                sqv[:, 0:1024].rearrange("p (a b) -> p a b", a=16), axis=AX.X))
            P.wait("v", "lg1")
            P.op("v", "lg1", 1, lambda v: v.tensor_scalar(
                W["rcol"][:, 0:16], W["rcol"][:, 0:16], 1.0 / 64.0, EPS_LN,
                ALU.mult, ALU.add))
            P.wait("s", "lg1")
            P.op("s", "lg2", 1, lambda s: s.activation(
                W["rcol"][:, 0:16], W["rcol"][:, 0:16], AF.Ln))
            P.wait("s", "lg2")
            P.op("s", "lg2", 1, lambda s: s.activation(
                W["rcol"][:, 0:16], W["rcol"][:, 0:16], AF.Exp, scale=-0.5))
            P.wait("v", "lg2")
            for h in range(16):
                P.op("v", "lg1", 1, lambda v, ch=ch, h=h: v.tensor_scalar(
                    env.ar2[:, ch * 1024 + h * 64:ch * 1024 + (h + 1) * 64],
                    env.ar2[:, ch * 1024 + h * 64:ch * 1024 + (h + 1) * 64],
                    W["rcol"][:, h:h + 1], None, ALU.mult))
            P.wait("v", "lg1")
            P.op("v", "lg1", 1, lambda v, ch=ch: v.tensor_mul(
                env.ar2[:, ch * 1024:(ch + 1) * 1024],
                env.ar2[:, ch * 1024:(ch + 1) * 1024], W["gwb"]))
            P.wait("s", "lg1")

    def evac_g(P, tt, oh, ps, gtag, gval):
        P.wait("s", gtag, gval)
        P.wait_if("s", "lg1")
        P.op("s", "pjtm_ev", 1, lambda s, tt=tt, oh=oh, ps=ps: s.activation(
            env.kgt[:, oh * 512:(oh + 1) * 512], ps, AF.Silu))
        P.wait("v", "pjtm_ev")
        P.op("v", "lg1", 1, lambda v, tt=tt, oh=oh: v.tensor_mul(
            env.ar2[:, tt * 1024 + oh * 512:tt * 1024 + (oh + 1) * 512],
            env.ar2[:, tt * 1024 + oh * 512:tt * 1024 + (oh + 1) * 512],
            env.kgt[:, oh * 512:(oh + 1) * 512]))
    emit_proj_tm(P, env, W["wg_bf"], D,
                 [fm(env.xnorm, ic) for ic in range(NC)], None, evac_g, "pjtm")
    P.wait("t", "lg1")
    for tt in range(16):
        for dt_ in range(NC):
            bank = env.pb[dt_ % 2]
            if P.planned("l2o_c") >= 2:
                P.wait("t", "l2o_c", P.planned("l2o_c") - 1)
            P.op("t", "lg3", 1, lambda t, tt=tt, dt_=dt_, bank=bank: t.transpose(
                bank.bitcast(BF16)[:, 0:128],
                env.ar2[:, tt * 1024 + dt_ * 128:tt * 1024 + (dt_ + 1) * 128],
                env.ident_bf))
            P.wait("v", "lg3")
            P.op("v", "l2o_c", 1, lambda v, tt=tt, dt_=dt_, bank=bank: v.tensor_copy(
                env.ar1[:, dt_ * T + tt * 128:dt_ * T + (tt + 1) * 128],
                bank.bitcast(BF16)[:, 0:128]))
    P.wait("t", "l2o_c")
    def xin_o(ic, t0, t1):
        return env.ar1[:, ic * T + t0:ic * T + t1]
    for t0 in range(0, T, 512):
        def evac_o(P, oc, tt0, tt1, ps, gtag, gval):
            P.wait("s", gtag, gval)
            P.op("s", "pj_ev", 1, lambda s, oc=oc, ps=ps: s.activation(
                env.f1[:, oc * 512:(oc + 1) * 512], ps, AF.Identity))
        emit_proj(P, env, W["wo_bf"], D, D, xin_o, evac_o, "pj",
                  ts_range=[(t0, t0 + 512)])
        P.wait("v", "pj_ev")
        P.wait("s", "pj_ev")
        src_b = [env.f1[:, c * 512:(c + 1) * 512] for c in range(NC)]
        add = [fm(env.xres, c)[:, t0:t0 + 512] for c in range(NC)]
        emit_ln_span(P, env, src_b, None, add, "ln", width=512,
                     sqbuf=env.xnorm[:, 0:4096])


def setup_l2a(P, env, pfx=""):
    nc = P.nc
    env.l2v = nc.dram_tensor(pfx + "l2v", [T, D], BF16, kind="Internal").ap()
    env.l2c_in = nc.dram_tensor(pfx + "l2c_in", [128, 512], F32, kind="Internal").ap()
    env.l2c_out = nc.dram_tensor(pfx + "l2c_out", [256, 512], F32, kind="Internal").ap()
    W = {}
    for nm, key in (("wq", "L2q"), ("wk", "L2k"), ("wv", "L2v_"), ("wg", "L2g"),
                    ("wo", "L2o")):
        W[nm + "_bf"] = stage_weight(P, env, pfx + nm, (D, D))
    wg1d = nc.dram_tensor(pfx + "wg1", [D, 16], F32, kind="ExternalInput").ap()
    wg2d = nc.dram_tensor(pfx + "wg2", [16, D], F32, kind="ExternalInput").ap()
    sbt = lambda n, sh, dt: env.ctx.enter_context(nc.sbuf_tensor(n, sh, dt))[:, :]
    W["wg1c"] = sbt("wg1c", [128, 128], BF16)
    W["wg2c"] = sbt("wg2c", [16, 1024], BF16)
    P.op("g", "wstage", 16, lambda g: g.dma_start(
        out=W["wg1c"].rearrange("p (ic o) -> p ic o", o=16),
        in_=wg1d.rearrange("(ic p) o -> p ic o", p=128)))
    P.op("g", "wstage", 16, lambda g: g.dma_start(out=W["wg2c"], in_=wg2d))
    W["bg2c"] = load_vec_input(P, env, pfx + "bg2c", [128, 8])
    W["flag"] = load_vec_input(P, env, pfx + "l2flag", [128, 1])
    W["triu"] = load_vec_input(P, env, pfx + "triu", [128, 128], BF16)
    W["ident_bf"] = env.ident_bf
    W["ident"] = env.ident_f32
    growb = nc.dram_tensor(pfx + "growb", [1, D], BF16, kind="ExternalInput").ap()
    W["growb"] = env.stat_m.bitcast(BF16)[0:1, 0:1024]
    P.op("y", "vload", 16, lambda y: y.dma_start(out=W["growb"], in_=growb))
    if hasattr(env, "kbuf"):
        kb = env.kbuf
        W["Scc"] = kb.bitcast(F32)[:, 0:512]
        W["Sf"] = kb.bitcast(F32)[:, 512:1024]
        W["Sb"] = kb[:, 2048:2560]
        W["t1g"] = kb[0:16, 2560:3072]
        W["rcol"] = kb.bitcast(F32)[:, 1536:1552]
        W["eLend"] = kb.bitcast(F32)[:, 1560:1568]
    else:
        W["Sf"] = sbt("Sf", [128, 512], F32)
        W["Sb"] = sbt("Sb", [128, 512], BF16)
        W["Scc"] = sbt("Scc", [128, 512], F32)
        W["eLend"] = sbt("eLend", [128, 8], F32)
        W["t1g"] = sbt("t1g", [16, 512], BF16)
        W["rcol"] = sbt("rcol", [128, 16], F32)
    env.eL = env.ar1[:, 8192:9216]
    env.eLn = env.ar1[:, 9216:10240]
    env.vch = env.ar1[:, 10240:11264]
    env.ktm = env.ar1[:, 11264:12288]
    env.abuf = env.ar1[:, 12288:14336]
    env.kgt = env.ar1[:, 14336:15360]
    W["gwb"] = env.ar1[:, 15360:16384]
    env.osb = (env.kbuf[0:64, 3136:3392] if hasattr(env, "kbuf")
               else sbt("osb", [64, 256], BF16))
    for e in ("t", "y"):
        P.wait(e, "wstage")
    for e in ("t", "s", "v"):
        P.wait(e, "vload")
    for e in ("v", "s", "t"):
        P.wait(e, "const")
    P.op("t", "lg3", 1, lambda t: t.matmul(
        env.pb[0][:, 0:512], env.ones_row_bf, W["growb"][0:1, 0:512],
        start=True, stop=True))
    P.op("t", "lg3", 1, lambda t: t.matmul(
        env.pb[1][:, 0:512], env.ones_row_bf, W["growb"][0:1, 512:1024],
        start=True, stop=True))
    P.wait("v", "lg3")
    P.op("v", "lg1", 1, lambda v: v.tensor_copy(W["gwb"][:, 0:512], env.pb[0][:, 0:512]))
    P.op("v", "lg1", 1, lambda v: v.tensor_copy(W["gwb"][:, 512:1024], env.pb[1][:, 0:512]))
    P.wait("t", "lg1")
    emit_ln(P, env, [fm(env.xres, c) for c in range(NC)],
            [fm(env.xnorm, c) for c in range(NC)], tag="ln")
    emit_l2_attn(P, env, W)


def build_l2a_program():
    nc = bass.Bass("TRN2", target_bir_lowering=False)
    with ExitStack() as ctx:
        P = Plan(nc, ctx)
        rin = nc.dram_tensor("rin", [D, T], F32, kind="ExternalInput").ap()
        rout = nc.dram_tensor("rout", [D, T], F32, kind="ExternalOutput").ap()
        env = build_env(P, ctx, attn=False)
        emit_load_resid(P, env, rin)
        setup_l2a(P, env)
        emit_store_resid(P, env, rout)
        emit_block_sections(nc, P)
    return nc


def host_l2a_inputs(inp, core):
    b, h = core // 2, core % 2
    m = {}
    m["wq"] = np.ascontiguousarray(inp["L2_attn_Wq"])
    m["wk"] = np.ascontiguousarray(inp["L2_attn_Wk"])
    m["wv"] = np.ascontiguousarray(inp["L2_attn_Wv"])
    m["wg"] = np.ascontiguousarray(inp["L2_attn_Wg"])
    m["wo"] = np.ascontiguousarray(inp["L2_attn_Wo"])
    m["wg1"] = np.ascontiguousarray(inp["L2_attn_Wg1"])
    m["wg2"] = np.ascontiguousarray(inp["L2_attn_Wg2"])
    m["bg2c"] = np.ascontiguousarray(inp["L2_attn_bg2"].reshape(8, 128).T)
    m["l2flag"] = np.full((128, 1), float(h), np.float32)
    import ml_dtypes as md
    m["triu"] = np.triu(np.ones((128, 128), np.float32)).astype(md.bfloat16)
    m["ident_bf"] = np.eye(128, dtype=np.float32).astype(md.bfloat16)
    m["ident_f32"] = np.eye(128, dtype=np.float32)
    m["growb"] = np.tile(inp["L2_attn_g_norm_w"], 16).reshape(1, D).astype(md.bfloat16)
    return m


def run_l2a(x_resid, inp):
    nc = build_l2a_program()
    in_maps = []
    for c in range(8):
        b, h = c // 2, c % 2
        m = host_l2a_inputs(inp, c)
        add_global_inputs(m)
        m["rin"] = np.ascontiguousarray(x_resid[b, h * T:(h + 1) * T, :].T)
        in_maps.append(m)
    res = run_bass_kernel_spmd(nc, in_maps, core_ids=list(range(8)))
    out = np.empty((4, 4096, D), np.float32)
    for c in range(8):
        b, h = c // 2, c % 2
        out[b, h * T:(h + 1) * T, :] = res.results[c]["rout"].T
    return out


def setup_l2f(P, env, pfx=""):
    nc = P.nc
    W = {}
    W["w1T_bf"] = stage_weight(P, env, pfx + "w1T", (D, 4096))
    W["w2T_bf"] = stage_weight(P, env, pfx + "w2T", (D, 4096))
    W["h1T_bf"] = stage_weight(P, env, pfx + "h1T", (4096, 4096))
    W["h2T_bf"] = stage_weight(P, env, pfx + "h2T", (4096, 4096))
    W["w3T_bf"] = stage_weight(P, env, pfx + "w3T", (4096, D))
    W["s1"] = load_vec_input(P, env, pfx + "s1", [128, 32])
    W["s2"] = load_vec_input(P, env, pfx + "s2", [128, 32])
    W["sh1"] = load_vec_input(P, env, pfx + "sh1", [128, 32])
    W["sh2"] = load_vec_input(P, env, pfx + "sh2", [128, 32])
    W["s3"] = load_vec_input(P, env, pfx + "s3", [128, 8])

    P.wait("t", "wstage")
    P.wait("y", "wstage")
    P.wait("t", "vload")
    P.wait("s", "vload")
    P.wait("v", "const")
    P.wait("s", "const")
    P.wait("t", "const")
    emit_ln(P, env, [fm(env.xres, c) for c in range(NC)],
            [fm(env.xnorm, c) for c in range(NC)], tag="preln")
    P.wait("v", "preln_n")
    P.wait("s", "preln_n")
    emit_l2_ffn(P, env, W)


def build_l2f_program():
    nc = bass.Bass("TRN2", target_bir_lowering=False)
    with ExitStack() as ctx:
        P = Plan(nc, ctx)
        rin = nc.dram_tensor("rin", [D, T], F32, kind="ExternalInput").ap()
        rout = nc.dram_tensor("rout", [D, T], F32, kind="ExternalOutput").ap()
        env = build_env(P, ctx, attn=False)
        emit_load_resid(P, env, rin)
        setup_l2f(P, env)
        emit_store_resid(P, env, rout)
        emit_block_sections(nc, P)
    return nc


def host_l2f_inputs(inp, core):
    b, h = core // 2, core % 2
    m = {}
    m["w1T"] = np.ascontiguousarray(inp["L2_ffn_w1_w"].T)
    m["w2T"] = np.ascontiguousarray(inp["L2_ffn_w2_w"].T)
    m["h1T"] = np.ascontiguousarray(inp["L2_ffn_h1_w"].T)
    m["h2T"] = np.ascontiguousarray(inp["L2_ffn_h2_w"].T)
    m["w3T"] = np.ascontiguousarray(inp["L2_ffn_w3_w"].T)
    m["s1"] = np.ascontiguousarray(inp["L2_ffn_w1_s"].reshape(32, 128).T)
    m["s2"] = np.ascontiguousarray(inp["L2_ffn_w2_s"].reshape(32, 128).T)
    m["sh1"] = np.ascontiguousarray(inp["L2_ffn_h1_s"].reshape(32, 128).T)
    m["sh2"] = np.ascontiguousarray(inp["L2_ffn_h2_s"].reshape(32, 128).T)
    m["s3"] = np.ascontiguousarray(inp["L2_ffn_w3_s"].reshape(8, 128).T)
    return m


def run_l2f(x_resid, inp):
    nc = build_l2f_program()
    in_maps = []
    for c in range(8):
        b, h = c // 2, c % 2
        m = host_l2f_inputs(inp, c)
        add_global_inputs(m)
        m["rin"] = np.ascontiguousarray(x_resid[b, h * T:(h + 1) * T, :].T)
        in_maps.append(m)
    res = run_bass_kernel_spmd(nc, in_maps, core_ids=list(range(8)))
    out = np.empty((4, 4096, D), np.float32)
    for c in range(8):
        b, h = c // 2, c % 2
        out[b, h * T:(h + 1) * T, :] = res.results[c]["rout"].T
    return out



def emit_l0_attn(P, env, W):
    nc = P.nc
    NCH = T // 128

    P.op("s", "l0c", 1, lambda s: s.activation(W["ldrow"], W["tdrow"], AF.Exp))
    P.wait("v", "l0c")
    P.op("v", "l0c", 1, lambda v: v.tensor_scalar_mul(W["ldrow"], W["ldrow"], -1.0))
    P.wait("s", "l0c")
    P.op("s", "l0c", 1, lambda s: s.activation(W["wsrow"], W["ldrow"], AF.Exp,
                                               scale=128.0))
    P.wait("t", "l0c")
    P.op("t", "lg3", 1, lambda t: t.matmul(
        env.pb[6][:, 0:16], env.ones_row_f32[0:1, :], W["ldrow"],
        start=True, stop=True))
    P.op("t", "lg3", 1, lambda t: t.matmul(
        env.pb[7][:, 0:16], env.ones_row_f32[0:1, :], W["urow"],
        start=True, stop=True))
    P.wait("v", "lg3")
    P.op("v", "lg1", 1, lambda v: v.tensor_copy(W["ldb"], env.pb[6][:, 0:16]))
    P.op("v", "lg1", 1, lambda v: v.tensor_copy(W["ub"], env.pb[7][:, 0:16]))

    xp = W["xprev"]
    P.op("s", "lg2", 1, lambda s: s.activation(env.tmpf[:, 8:16], xp, AF.Square))
    P.wait("t", "lg2")
    def mmxp(t):
        t.matmul(env.pb[4][0:1, 0:8], env.ones_col_f32, xp, start=True, stop=True)
        return t.matmul(env.pb[5][0:1, 0:8], env.ones_col_f32, env.tmpf[:, 8:16],
                        start=True, stop=True)
    P.op("t", "lg3", 1, mmxp)
    P.wait("v", "lg3")
    P.op("v", "lg1", 1, lambda v: v.reduce_sum(
        env.tmpf[0:1, 0:1], env.pb[4][0:1, 0:8], axis=AX.X))
    P.op("v", "lg1", 1, lambda v: v.reduce_sum(
        env.tmpf[0:1, 1:2], env.pb[5][0:1, 0:8], axis=AX.X))
    P.wait("v", "lg1")
    P.op("v", "lg1", 1, lambda v: v.tensor_scalar_mul(
        env.tmpf[0:1, 0:2], env.tmpf[0:1, 0:2], 1.0 / 1024.0))
    P.wait("v", "lg1")
    P.op("v", "lg1", 1, lambda v: v.tensor_mul(
        env.tmpf[0:1, 2:3], env.tmpf[0:1, 0:1], env.tmpf[0:1, 0:1]))
    P.wait("v", "lg1")
    P.op("v", "lg1", 1, lambda v: v.tensor_sub(
        env.tmpf[0:1, 1:2], env.tmpf[0:1, 1:2], env.tmpf[0:1, 2:3]))
    P.wait("s", "lg1")
    P.op("s", "lg2", 1, lambda s: s.activation(
        env.tmpf[0:1, 1:2], env.tmpf[0:1, 1:2], AF.Ln,
        bias=env.eps_col[0:1, 0:1]))
    P.wait("s", "lg2")
    P.op("s", "lg2", 1, lambda s: s.activation(
        env.tmpf[0:1, 1:2], env.tmpf[0:1, 1:2], AF.Exp, scale=-0.5))
    P.wait("t", "lg2")
    def mmxb(t):
        return t.matmul(env.pb[4][:, 0:2], env.ones_row_f32[0:1, :],
                        env.tmpf[0:1, 0:2], start=True, stop=True)
    P.op("t", "lg3", 1, mmxb)
    P.wait("v", "lg3")
    P.op("v", "lg1", 1, lambda v: v.tensor_copy(
        env.tmpf[:, 4:6], env.pb[4][:, 0:2]))
    P.wait("v", "lg1")
    P.op("v", "lg1", 1, lambda v: v.tensor_scalar(
        env.tmpf[:, 8:16], xp, env.tmpf[:, 4:5], None, ALU.subtract))
    P.wait("v", "lg1")
    P.op("v", "lg1", 1, lambda v: v.tensor_scalar(
        W["xx0col"], env.tmpf[:, 8:16], env.tmpf[:, 5:6], None, ALU.mult))
    P.wait("v", "lg1")

    for c in range(NC):
        P.op("v", "lg1", 1, lambda v, c=c: v.tensor_copy(
            env.ar2[:, c * T + 1:c * T + T], fm(env.xnorm, c)[:, 0:T - 1]))
        P.op("v", "lg1", 1, lambda v, c=c: v.tensor_copy(
            env.ar2[:, c * T:c * T + 1], W["xx0col"][:, c:c + 1]))
    P.wait("v", "lg1")
    P.wait("t", "lg1")

    dbuf = env.ar1[:, 0:4096]
    mixbuf = env.ar1[:, 4096:8192]
    def premix(t0, mcols):
        for c in range(NC):
            P.op("v", "lg1", 1, lambda v, c=c, t0=t0: v.tensor_sub(
                dbuf[:, c * 512:(c + 1) * 512],
                fm(env.xnorm, c)[:, t0:t0 + 512],
                env.ar2[:, c * T + t0:c * T + t0 + 512]))
        P.wait("v", "lg1")
        for c in range(NC):
            P.op("v", "lg1", 1, lambda v, c=c, t0=t0, mcols=mcols: v.scalar_tensor_tensor(
                mixbuf[:, c * 512:(c + 1) * 512],
                dbuf[:, c * 512:(c + 1) * 512], mcols[:, c:c + 1],
                env.ar2[:, c * T + t0:c * T + t0 + 512],
                ALU.mult, ALU.add))
        P.wait("v", "lg1")
        P.wait("t", "lg1")

    def premix_only(t0, mcols):
        for c in range(NC):
            P.op("v", "lg1", 1, lambda v, c=c, t0=t0, mcols=mcols: v.scalar_tensor_tensor(
                mixbuf[:, c * 512:(c + 1) * 512],
                dbuf[:, c * 512:(c + 1) * 512], mcols[:, c:c + 1],
                env.ar2[:, c * T + t0:c * T + t0 + 512],
                ALU.mult, ALU.add))
        P.wait("v", "lg1")
        P.wait("t", "lg1")

    def xin_mix(ic, tt0, tt1):
        return mixbuf[:, ic * 512:(ic + 1) * 512]

    def mk_evac_dram(dram, t0, func=None, scale=1.0):
        def evac(P, oc, tt0, tt1, ps, gtag, gval):
            P.wait("s", gtag, gval)
            P.wait_if("s", "lg4")
            P.wait_if("s", "lg4")
            f = func if func is not None else AF.Identity
            P.op("s", "pj_ev", 1, lambda s, oc=oc, ps=ps, f=f: s.activation(
                env.kbuf[:, (oc % 8) * 512:(oc % 8) * 512 + 512], ps, f,
                scale=scale))
            P.wait("y", "pj_ev")
            P.op("y", "lg4", 16, lambda y, oc=oc, t0=t0: y.dma_start(
                out=dram[oc * 128:(oc + 1) * 128, t0:t0 + 512],
                in_=env.kbuf[:, (oc % 8) * 512:(oc % 8) * 512 + 512]))
        return evac

    for t0 in range(0, T, 512):
        premix(t0, W["tmr"])
        emit_proj(P, env, W["wrT_bf"], D, D, xin_mix,
                  mk_evac_dram(env.l0r, t0), "pj", ts_range=[(t0, t0 + 512)])
        P.wait("v", "pj_g")
        premix_only(t0, W["tmk"])
        emit_proj(P, env, W["wkT_bf"], D, D, xin_mix,
                  mk_evac_dram(env.l0k, t0), "pj", ts_range=[(t0, t0 + 512)])
        P.wait("v", "pj_g")
        premix_only(t0, W["tmg"])
        emit_proj(P, env, W["wgT_bf"], D, D, xin_mix,
                  mk_evac_dram(env.l0g, t0, func=AF.Silu), "pj",
                  ts_range=[(t0, t0 + 512)])
        P.wait("v", "pj_g")
        premix_only(t0, W["tmv"])

        def evac_v(P, tt, oh, ps, gtag, gval):
            P.wait("s", gtag, gval)
            P.wait_if("s", "lg4")
            P.wait_if("s", "lg4")
            P.op("s", "pjtm_ev", 1, lambda s, tt=tt, oh=oh, ps=ps: s.activation(
                env.kbuf[:, (tt % 2) * 1024 + oh * 512:(tt % 2) * 1024 + (oh + 1) * 512],
                ps, AF.Identity, scale=0.125))
            if oh == 1:
                P.wait("y", "pjtm_ev")
                P.op("y", "lg4", 16, lambda y, tt=tt: y.dma_start(
                    out=env.l0v[tt * 128:(tt + 1) * 128, :],
                    in_=env.kbuf[:, (tt % 2) * 1024:(tt % 2) * 1024 + 1024]))
        emit_proj_tm(P, env, W["wvT_bf"], D,
                     lambda ic, tt, t0=t0: mixbuf[:, ic * 512 + (tt * 128 - t0):
                                                  ic * 512 + (tt * 128 - t0) + 128],
                     None, evac_v, "pjtm", ts_sub=(t0, t0 + 512))
        P.wait("v", "pjtm_g")

    P.wait("y", "ln_n")
    P.op("y", "lg4", 16, lambda y: y.dma_start(out=W["irow"], in_=W["irow_dr"]))
    P.op("y", "lg4", 16, lambda y: y.dma_start(out=W["irevrow"], in_=W["irevrow_dr"]))
    P.op("y", "lg4", 16, lambda y: y.dma_start(out=W["lnwrow"], in_=W["lnwrow_dr"]))
    P.op("y", "lg4", 16, lambda y: y.dma_start(out=W["lnbrow"], in_=W["lnbrow_dr"]))
    P.wait("t", "lg4")
    P.wait("s", "lg4")
    P.wait("t", "l0c")
    P.wait("t", "lg1")
    tb16 = env.tmpf.bitcast(BF16)
    rows = [tb16[0:1, j * 128:(j + 1) * 128] for j in range(4)]
    for c in range(8):
        P.wait("s", "lg3")
        for j, (src_row, h) in enumerate(
                ((W["irow"], 2 * c), (W["irow"], 2 * c + 1),
                 (W["irevrow"], 2 * c), (W["irevrow"], 2 * c + 1))):
            P.op("s", "lg2", 1, lambda s, j=j, src_row=src_row, h=h: s.activation(
                rows[j], src_row, AF.Exp, scale=W["ldrow"][0:1, h:h + 1]))
        P.wait("t", "lg2")

        def bc2(t, c=c):
            t.matmul(env.pb[4][0:64, 0:128], env.ones_row_bf[0:1, 0:64],
                     rows[0], start=True, stop=True)
            t.matmul(env.pb[4][64:128, 0:128], env.ones_row_bf[0:1, 0:64],
                     rows[1], start=True, stop=True)
            t.matmul(env.pb[5][0:64, 0:128], env.ones_row_bf[0:1, 0:64],
                     rows[2], start=True, stop=True)
            return t.matmul(env.pb[5][64:128, 0:128], env.ones_row_bf[0:1, 0:64],
                            rows[3], start=True, stop=True)
        P.op("t", "lg3", 1, bc2)
        P.wait("v", "lg3")
        P.op("v", "lg1", 1, lambda v, c=c: v.tensor_copy(
            W["wbb"][:, c * 128:(c + 1) * 128], env.pb[4][:, 0:128]))
        P.op("v", "lg1", 1, lambda v, c=c: v.tensor_copy(
            W["wkb"][:, c * 128:(c + 1) * 128], env.pb[5][:, 0:128]))
        P.wait("t", "lg1")
    for c in range(8):
        def bc4(t, c=c):
            t.matmul(env.pb[6][0:64, c:c + 1], env.ones_row_f32[0:1, 0:64],
                     W["wsrow"][0:1, 2 * c:2 * c + 1], start=True, stop=True)
            return t.matmul(env.pb[6][64:128, c:c + 1], env.ones_row_f32[0:1, 0:64],
                            W["wsrow"][0:1, 2 * c + 1:2 * c + 2], start=True, stop=True)
        P.op("t", "lg3", 1, bc4)
    P.wait("v", "lg3")
    P.op("v", "lg1", 1, lambda v: v.tensor_copy(W["wscols"], env.pb[6][:, 0:8]))
    P.wait("t", "lg1")
    for half in range(2):
        def bc5(t, half=half):
            return t.matmul(env.pb[7][:, 0:512], env.ones_row_bf[0:1, :],
                            W["lnwrow"][0:1, half * 512:(half + 1) * 512],
                            start=True, stop=True)
        P.op("t", "lg3", 1, bc5)
        P.wait("v", "lg3")
        P.op("v", "lg1", 1, lambda v, half=half: v.tensor_copy(
            W["lnwb"][:, half * 512:(half + 1) * 512], env.pb[7][:, 0:512]))
        P.wait("t", "lg1")

        def bc6(t, half=half):
            return t.matmul(env.pb[7][:, 0:512], env.ones_row_bf[0:1, :],
                            W["lnbrow"][0:1, half * 512:(half + 1) * 512],
                            start=True, stop=True)
        P.op("t", "lg3", 1, bc6)
        P.wait("v", "lg3")
        P.op("v", "lg1", 1, lambda v, half=half: v.tensor_copy(
            W["lnbb"][:, half * 512:(half + 1) * 512], env.pb[7][:, 0:512]))
        P.wait("t", "lg1")
    for h in range(16):
        wm = W["wmT"][:, h * 128:(h + 1) * 128]
        P.op("s", "lg2", 1, lambda s, h=h, wm=wm: s.activation(
            wm, W["m1T"], AF.Exp, scale=W["ldb"][:, h:h + 1]))
        P.wait("v", "lg2")
        P.op("v", "lg1", 1, lambda v, wm=wm: v.tensor_mul(wm, wm, W["maskT"]))
        P.wait("v", "lg1")
        P.op("v", "lg1", 1, lambda v, h=h, wm=wm: v.scalar_tensor_tensor(
            wm, W["ident_bf"], W["ub"][:, h:h + 1], wm, ALU.mult, ALU.add))
    P.wait("t", "lg1")
    P.wait("v", "lg1")

    rc = env.ar2[:, 0:1024]
    kc = env.ar2[:, 1024:2048]
    vc = env.ar2[:, 2048:3072]
    rwb = env.ar2[:, 3072:4096]
    kwk = env.ar2[:, 4096:5120]
    kwktm = env.ar2[:, 5120:6144]
    attsb = env.ar2[:, 6144:6400]
    sbf = env.ar2[:, 6656:7168]
    sall = env.tmpf
    yraw = env.xnorm

    P.wait("y", "lg4")
    P.wait("y", "lg4")
    P.wait("v", "lg1")
    P.op("v", "lg1", 1, lambda v: v.memset(sall[:, :], 0.0))
    P.wait("v", "lg1")
    P.op("v", "lg1", 1, lambda v: v.memset(sbf, 0.0))
    P.wait("t", "lg1")
    P.wait("y", "lg1")

    for n in range(NCH):
        for tg in ("l0og", "l0sd", "lg1", "l0ktc"):
            P.wait_if("y", tg)
        P.op("y", "lg4", 16, lambda y, n=n: y.dma_start(
            out=rc.rearrange("p (c t) -> p c t", c=8),
            in_=env.l0r[:, n * 128:(n + 1) * 128].rearrange(
                "(c p) t -> p c t", p=128)))
        P.op("y", "lg4", 16, lambda y, n=n: y.dma_start(
            out=kc.rearrange("p (c t) -> p c t", c=8),
            in_=env.l0k[:, n * 128:(n + 1) * 128].rearrange(
                "(c p) t -> p c t", p=128)))
        P.op("y", "lg4", 16, lambda y, n=n: y.dma_start(
            out=vc, in_=env.l0v[n * 128:(n + 1) * 128, :]))
        P.wait("v", "lg4")
        P.op("v", "lg1", 1, lambda v: v.tensor_mul(rwb, rc, W["wbb"]))
        P.op("v", "lg1", 1, lambda v: v.tensor_mul(kwk, kc, W["wkb"]))
        P.wait("t", "lg4")
        P.wait("t", "lg1")
        base_og = P.planned("l0og")
        base_ae = P.planned("l0ae")
        for h in range(16):
            c, b_ = h // 2, 64 * (h % 2)
            abank = env.pb[4 + h % 2]
            obank = env.pb[h % 4]
            if base_ae + h - 1 >= 1:
                P.wait("t", "l0ae", base_ae + h - 1)
            P.op("t", "lg3", 1, lambda t, c=c, b_=b_, abank=abank: t.matmul(
                abank[:, 0:128], kc[b_:b_ + 64, c * 128:(c + 1) * 128],
                rc[b_:b_ + 64, c * 128:(c + 1) * 128], start=True, stop=True))
            P.wait("v", "lg3")
            if base_og + h - 1 >= 1:
                P.wait("v", "l0og", base_og + h - 1)
            P.op("v", "l0ae", 1, lambda v, h=h, abank=abank: v.tensor_mul(
                attsb[:, (h % 2) * 128:(h % 2) * 128 + 128],
                abank[:, 0:128], W["wmT"][:, h * 128:(h + 1) * 128]))
            P.wait("t", "l0ae")
            if P.planned("l0oe") - 3 >= 1:
                P.wait("t", "l0oe", P.planned("l0oe") - 3)

            def mmo(t, h=h, c=c, b_=b_, obank=obank):
                t.matmul(obank[:, 0:64], attsb[:, (h % 2) * 128:(h % 2) * 128 + 128],
                         vc[:, h * 64:(h + 1) * 64], start=True, stop=False)
                return t.matmul(obank[:, 0:64], rwb[b_:b_ + 64, c * 128:(c + 1) * 128],
                                sbf[b_:b_ + 64, c * 64:c * 64 + 64],
                                start=False, stop=True)
            P.op("t", "l0og", 1, mmo)
            P.wait("v", "l0og")
            P.op("v", "l0oe", 1, lambda v, n=n, h=h, obank=obank: v.tensor_copy(
                yraw[:, n * 1024 + h * 64:n * 1024 + (h + 1) * 64],
                obank[:, 0:64]))
        P.wait("t", "l0se")
        base_ktc = P.planned("l0ktc")
        for c in range(8):
            tb = env.pb[6 + c % 2]
            if c >= 2:
                P.wait("t", "l0ktc", base_ktc + c - 1)
            P.op("t", "lg3", 1, lambda t, c=c, tb=tb: t.transpose(
                tb.bitcast(BF16)[:, 0:128], kwk[:, c * 128:(c + 1) * 128],
                W["ident_bf"]))
            P.wait("v", "lg3")
            P.op("v", "l0ktc", 1, lambda v, c=c, tb=tb: v.tensor_copy(
                kwktm[:, c * 128:(c + 1) * 128], tb.bitcast(BF16)[:, 0:128]))
        P.wait("t", "l0ktc")
        P.wait("t", "l0oe")
        base_sd = P.planned("l0sd")
        for h in range(16):
            c, b_ = h // 2, 64 * (h % 2)
            dbank = env.pb[6 + h % 2]
            if P.planned("l0se") - 1 >= 1:
                P.wait("t", "l0se", P.planned("l0se") - 1)
            P.op("t", "l0sd", 1, lambda t, h=h, c=c, b_=b_, dbank=dbank: t.matmul(
                dbank[0:64, 0:64], kwktm[:, c * 128 + b_:c * 128 + b_ + 64],
                vc[:, h * 64:(h + 1) * 64], start=True, stop=True))
            P.wait("v", "l0sd")
            P.op("v", "lg1", 1, lambda v, h=h, c=c, b_=b_: v.tensor_scalar(
                sall[b_:b_ + 64, c * 64:c * 64 + 64],
                sall[b_:b_ + 64, c * 64:c * 64 + 64],
                W["wscols"][b_:b_ + 64, c:c + 1], None, ALU.mult))
            P.wait("v", "lg1")
            P.op("v", "l0se", 1, lambda v, h=h, c=c, b_=b_, dbank=dbank: v.tensor_add(
                sall[b_:b_ + 64, c * 64:c * 64 + 64],
                sall[b_:b_ + 64, c * 64:c * 64 + 64],
                dbank[0:64, 0:64]))
        P.wait("v", "l0se")
        P.op("v", "lg1", 1, lambda v: v.tensor_copy(sbf, sall[:, :]))
        P.wait("t", "lg1")

    P.wait("y", "lg1")
    P.op("y", "lg4", 16, lambda y: y.dma_start(out=env.l0c_in, in_=sall[:, :]))
    P.wait("g", "lg4")
    P.op("g", "ccl0", 1, lambda g: g.collective_compute(
        "AllGather", ALU.bypass, replica_groups=[[0, 1], [2, 3], [4, 5], [6, 7]],
        ins=[env.l0c_in], outs=[env.l0c_out]))
    P.wait("y", "ccl0")
    P.op("y", "lg4", 16, lambda y: y.dma_start(
        out=sall[:, :], in_=env.l0c_out[0:128, :]))
    P.wait("v", "lg4")
    P.op("v", "lg1", 1, lambda v: v.tensor_scalar(
        sall[:, :], sall[:, :], W["flag"], None, ALU.mult))
    P.wait("v", "lg1")
    P.op("v", "lg1", 1, lambda v: v.tensor_copy(sbf, sall[:, :]))
    P.wait("t", "lg1")
    P.wait("y", "lg1")
    P.op("v", "lg1", 1, lambda v: v.memset(W["wsn"], 1.0))
    P.wait("v", "lg1")

    lnwb = env.f1[:, 2048:3072]
    lnbb = env.f1[:, 3072:4096]
    ycf = env.f1[:, 0:1024]
    sqf = env.f1[:, 1024:2048]
    ygn = env.ar2[:, 4096:5120]
    for n in range(NCH):
        P.wait_if("y", "lg3")
        P.wait_if("y", "l0gtc")
        P.op("y", "lg4", 16, lambda y, n=n: y.dma_start(
            out=rc.rearrange("p (c t) -> p c t", c=8),
            in_=env.l0r[:, n * 128:(n + 1) * 128].rearrange(
                "(c p) t -> p c t", p=128)))
        P.wait("v", "lg4")
        P.op("v", "lg1", 1, lambda v: v.tensor_mul(rwb, rc, W["wbb"]))
        P.wait("v", "lg1")
        for h in range(16):
            c, b_ = h // 2, 64 * (h % 2)
            P.op("v", "lg1", 1, lambda v, c=c, b_=b_: v.tensor_scalar(
                rwb[b_:b_ + 64, c * 128:(c + 1) * 128],
                rwb[b_:b_ + 64, c * 128:(c + 1) * 128],
                W["wsn"][b_:b_ + 64, c:c + 1], None, ALU.mult))
        P.wait("t", "lg1")
        base_fe = P.planned("l0fe")
        for h in range(16):
            c, b_ = h // 2, 64 * (h % 2)
            fbank = env.pb[h % 4]
            if base_fe + h - 3 >= 1:
                P.wait("t", "l0fe", base_fe + h - 3)
            P.op("t", "lg3", 1, lambda t, h=h, c=c, b_=b_, fbank=fbank: t.matmul(
                fbank[:, 0:64], rwb[b_:b_ + 64, c * 128:(c + 1) * 128],
                sbf[b_:b_ + 64, c * 64:c * 64 + 64], start=True, stop=True))
            P.wait("v", "lg3")
            P.op("v", "l0fe", 1, lambda v, n=n, h=h, fbank=fbank: v.tensor_add(
                yraw[:, n * 1024 + h * 64:n * 1024 + (h + 1) * 64],
                yraw[:, n * 1024 + h * 64:n * 1024 + (h + 1) * 64],
                fbank[:, 0:64]))
        P.wait("v", "l0fe")
        P.op("v", "lg1", 1, lambda v: v.tensor_mul(
            W["wsn"], W["wsn"], W["wscols"]))
        yslab = yraw[:, n * 1024:(n + 1) * 1024]
        P.op("v", "lg1", 1, lambda v, yslab=yslab: v.reduce_sum(
            env.mcol[:, 0:16], yslab.rearrange("p (h k) -> p h k", h=16),
            axis=AX.X))
        P.wait("v", "lg1")
        P.op("v", "lg1", 1, lambda v: v.tensor_scalar_mul(
            env.mcol[:, 0:16], env.mcol[:, 0:16], 1.0 / 64.0))
        P.wait("v", "lg1")
        for h in range(16):
            P.op("v", "lg1", 1, lambda v, h=h, yslab=yslab: v.tensor_scalar(
                ycf[:, h * 64:(h + 1) * 64], yslab[:, h * 64:(h + 1) * 64],
                env.mcol[:, h:h + 1], None, ALU.subtract))
        P.wait("v", "lg1")
        P.op("v", "lg1", 1, lambda v: v.tensor_mul(sqf, ycf, ycf))
        P.wait("v", "lg1")
        P.op("v", "lg1", 1, lambda v: v.reduce_sum(
            env.lcol[:, 0:16], sqf.rearrange("p (h k) -> p h k", h=16),
            axis=AX.X))
        P.wait("s", "lg1")
        P.op("s", "lg2", 1, lambda s: s.activation(
            env.lcol[:, 0:16], env.lcol[:, 0:16], AF.Ln, scale=1.0 / 64.0,
            bias=env.eps_col[:, 0:1]))
        P.wait("s", "lg2")
        P.op("s", "lg2", 1, lambda s: s.activation(
            env.lcol[:, 0:16], env.lcol[:, 0:16], AF.Exp, scale=-0.5))
        P.wait("v", "lg2")
        for h in range(16):
            P.op("v", "lg1", 1, lambda v, h=h: v.tensor_scalar(
                ycf[:, h * 64:(h + 1) * 64], ycf[:, h * 64:(h + 1) * 64],
                env.lcol[:, h:h + 1], None, ALU.mult))
        P.wait("v", "lg1")
        P.op("v", "lg1", 1, lambda v: v.tensor_mul(ycf, ycf, lnwb))
        P.wait("v", "lg1")
        P.op("v", "lg1", 1, lambda v: v.tensor_add(ygn, ycf, lnbb))
        P.wait("t", "lg1")
        base_gt = P.planned("l0gtc")
        for c in range(8):
            tb = env.pb[4 + c % 2]
            if c >= 2:
                P.wait("t", "l0gtc", base_gt + c - 1)
            P.op("t", "lg3", 1, lambda t, c=c, tb=tb: t.transpose(
                tb.bitcast(BF16)[:, 0:128], ygn[:, c * 128:(c + 1) * 128],
                W["ident_bf"]))
            P.wait("v", "lg3")
            P.op("v", "l0gtc", 1, lambda v, n=n, c=c, tb=tb: v.tensor_copy(
                env.ar1[:, c * T + n * 128:c * T + (n + 1) * 128],
                tb.bitcast(BF16)[:, 0:128]))
    P.wait("y", "l0gtc")
    if getattr(env, "l0_dump", None):
        P.wait_all("y")
        P.op("y", "lg4", 16, lambda y: y.dma_start(
            out=env.l0_dump["yraw"], in_=yraw[:, :]))
        P.op("y", "lg4", 16, lambda y: y.dma_start(
            out=env.l0_dump["yfm"], in_=env.ar1[:, :]))
        if "ar2" in env.l0_dump:
            P.op("y", "lg4", 16, lambda y: y.dma_start(
                out=env.l0_dump["ar2"], in_=env.ar2[:, :]))
        P.wait("y", "lg4")

    P.op("y", "lg4", 16, lambda y: y.dma_start(
        out=env.ar2.rearrange("p (c t) -> p c t", c=8),
        in_=env.l0g.rearrange("(c p) t -> p c t", p=128)))
    P.wait("v", "lg4")
    P.wait("v", "l0gtc")
    for c in range(NC):
        P.op("v", "lg1", 1, lambda v, c=c: v.tensor_mul(
            fm(env.ar1, c), fm(env.ar1, c), fm(env.ar2, c)))
    P.wait("t", "lg1")

    def xin_y(ic, tt0, tt1):
        return fm(env.ar1, ic)[:, tt0:tt1]

    for t0 in range(0, T, 512):
        def evac_o(P, oc, tt0, tt1, ps, gtag, gval):
            P.wait("s", gtag, gval)
            P.op("s", "pj_ev", 1, lambda s, oc=oc, ps=ps: s.activation(
                env.bo[:, oc * 512:(oc + 1) * 512], ps, AF.Identity))
        emit_proj(P, env, W["woT_bf"], D, D, xin_y, evac_o, "pj",
                  ts_range=[(t0, t0 + 512)])
        P.wait("v", "pj_ev")
        P.wait("s", "pj_ev")
        src_b = [env.bo[:, c * 512:(c + 1) * 512] for c in range(NC)]
        add = [fm(env.xres, c)[:, t0:t0 + 512] for c in range(NC)]
        emit_ln_span(P, env, src_b, None, add, "ln", width=512,
                     sqbuf=env.xnorm[:, 0:4096])


def setup_l0a(P, env, pfx=""):
    nc = P.nc
    env.l0r = nc.dram_tensor(pfx + "l0r", [D, T], BF16, kind="Internal").ap()
    env.l0k = nc.dram_tensor(pfx + "l0k", [D, T], BF16, kind="Internal").ap()
    env.l0g = nc.dram_tensor(pfx + "l0g", [D, T], BF16, kind="Internal").ap()
    env.l0v = nc.dram_tensor(pfx + "l0v", [T, D], BF16, kind="Internal").ap()
    env.l0c_in = nc.dram_tensor(pfx + "l0c_in", [128, 512], F32, kind="Internal").ap()
    env.l0c_out = nc.dram_tensor(pfx + "l0c_out", [256, 512], F32, kind="Internal").ap()
    W = {}
    for nm in ("wr", "wk", "wv", "wg", "wo"):
        W[nm + "T_bf"] = stage_weight(P, env, pfx + nm + "T", (D, D))
    for nm in ("tmr", "tmk", "tmv", "tmg"):
        W[nm] = load_vec_input(P, env, pfx + nm, [128, 8])
    W["xprev"] = load_vec_input(P, env, pfx + "xprev", [128, 8])
    W["flag"] = load_vec_input(P, env, pfx + "flag", [128, 1])
    W["tdrow"] = load_vec_input(P, env, pfx + "tdrow", [1, 16])
    W["urow"] = load_vec_input(P, env, pfx + "urow", [1, 16])
    W["lnwrow"] = env.stat_t.bitcast(BF16)[0:1, 0:1024]
    W["lnbrow"] = env.stat_r.bitcast(BF16)[0:1, 0:1024]
    W["lnwrow_dr"] = nc.dram_tensor(
        pfx + "lnwrow", [1, 1024], BF16, kind="ExternalInput").ap()
    W["lnbrow_dr"] = nc.dram_tensor(
        pfx + "lnbrow", [1, 1024], BF16, kind="ExternalInput").ap()
    W["irow"] = env.tmpf[0:1, 256:384]
    W["irevrow"] = env.tmpf[0:1, 384:512]
    W["m1T"] = env.f1[:, 0:128]
    W["maskT"] = env.f1.bitcast(BF16)[:, 256:384]
    for nm, ap, dt in (("m1T", W["m1T"], F32), ("maskT", W["maskT"], BF16)):
        dr = nc.dram_tensor(pfx + nm, list(ap.shape), dt, kind="ExternalInput").ap()
        P.op("y", "vload", 16, lambda y, ap=ap, dr=dr: y.dma_start(
            out=ap, in_=dr))
    for nm, ap in (("irow", W["irow"]), ("irevrow", W["irevrow"])):
        W[nm + "_dr"] = nc.dram_tensor(
            pfx + nm, list(ap.shape), F32, kind="ExternalInput").ap()
    W["ident_bf"] = env.ident_bf
    sb = lambda n, sh, dt: env.ctx.enter_context(nc.sbuf_tensor(n, sh, dt))[:, :]
    W["ldrow"] = sb("ldrow", [1, 16], F32)
    W["wsrow"] = sb("wsrow", [1, 16], F32)
    W["ldb"] = sb("ldb", [128, 16], F32)
    W["ub"] = sb("ub", [128, 16], F32)
    W["wbb"] = env.ar2[:, 7168:8192]
    W["wkb"] = env.ar2[:, 8192:9216]
    W["wscols"] = sb("wscols", [128, 8], F32)
    W["wsn"] = sb("wsn", [128, 8], F32)
    W["wmT"] = env.ar2[:, 9216:11264]
    W["lnwb"] = None
    W["xx0col"] = env.mcol.bitcast(BF16)[:, 0:8]
    W["lnwb"] = env.f1[:, 2048:3072]
    W["lnbb"] = env.f1[:, 3072:4096]
    for e in ("t", "y"):
        P.wait(e, "wstage")
    for e in ("t", "s", "v"):
        P.wait(e, "vload")
    for e in ("v", "s", "t"):
        P.wait(e, "const")
    emit_ln(P, env, [fm(env.xres, c) for c in range(NC)],
            [fm(env.xnorm, c) for c in range(NC)], tag="ln")
    emit_l0_attn(P, env, W)


def build_l0a_program():
    nc = bass.Bass("TRN2", target_bir_lowering=False)
    with ExitStack() as ctx:
        P = Plan(nc, ctx)
        rin = nc.dram_tensor("rin", [D, T], F32, kind="ExternalInput").ap()
        rout = nc.dram_tensor("rout", [D, T], F32, kind="ExternalOutput").ap()
        env = build_env(P, ctx, attn=True)
        emit_load_resid(P, env, rin)
        setup_l0a(P, env)
        emit_store_resid(P, env, rout)
        emit_block_sections(nc, P)
    return nc


def host_l0a_inputs(inp, core):
    b, h = core // 2, core % 2
    m = {}
    for nm, key in (("wr", "L0_attn_W_r"), ("wk", "L0_attn_W_k"),
                    ("wv", "L0_attn_W_v"), ("wg", "L0_attn_W_g"),
                    ("wo", "L0_attn_W_o")):
        m[nm + "T"] = np.ascontiguousarray(inp[key].T)
    for nm, key in (("tmr", "L0_attn_time_mix_r"), ("tmk", "L0_attn_time_mix_k"),
                    ("tmv", "L0_attn_time_mix_v"), ("tmg", "L0_attn_time_mix_g")):
        m[nm] = np.ascontiguousarray(inp[key].reshape(8, 128).T)
    m["tdrow"] = inp["L0_attn_time_decay"].reshape(1, 16).astype(np.float32)
    m["urow"] = inp["L0_attn_time_faaaa"].reshape(1, 16).astype(np.float32)
    m["lnwrow"] = inp["L0_attn_ln_w"].reshape(1, 1024).astype(ml_dtypes.bfloat16)
    m["lnbrow"] = inp["L0_attn_ln_b"].reshape(1, 1024).astype(ml_dtypes.bfloat16)
    m["flag"] = np.full((128, 1), float(h), np.float32)
    m["irow"] = np.arange(128, dtype=np.float32).reshape(1, 128)
    m["irevrow"] = (127.0 - np.arange(128, dtype=np.float32)).reshape(1, 128)
    f = np.arange(128, dtype=np.float32)
    m["m1T"] = np.maximum(f[None, :] - f[:, None] - 1.0, 0.0)
    m["maskT"] = (f[None, :] > f[:, None]).astype(ml_dtypes.bfloat16)
    m["ident_bf"] = np.eye(128, dtype=ml_dtypes.bfloat16)
    return m


def run_l0a(x_resid, inp):
    nc = build_l0a_program()
    in_maps = []
    for c in range(8):
        b, h = c // 2, c % 2
        m = host_l0a_inputs(inp, c)
        add_global_inputs(m)
        m["rin"] = np.ascontiguousarray(x_resid[b, h * T:(h + 1) * T, :].T)
        if h == 1:
            m["xprev"] = np.ascontiguousarray(
                x_resid[b, T - 1, :].reshape(8, 128).T)
        else:
            m["xprev"] = np.zeros((128, 8), np.float32)
        in_maps.append(m)
    res = run_bass_kernel_spmd(nc, in_maps, core_ids=list(range(8)))
    out = np.empty((4, 4096, D), np.float32)
    for c in range(8):
        b, h = c // 2, c % 2
        out[b, h * T:(h + 1) * T, :] = res.results[c]["rout"].T
    return out



def l0f_score_hook(W):
    def hook(P, env, si, width):
        q0 = si * 512
        xhi = env.ar1[:, 4096:8192]
        xlo = env.ar1[:, 8192:12288]
        pz = env.pb[4]
        P.wait("t", "l3z_e")
        P.wait("s", "l3z_e")
        for c in range(NC):
            P.op("s", "lq2", 1, lambda s, c=c, q0=q0: s.activation(
                xhi[:, c * 512:(c + 1) * 512], fm(env.xres, c)[:, q0:q0 + 512],
                AF.Identity))
        P.wait("v", "lq2")
        for c in range(NC):
            P.op("v", "lq1", 1, lambda v, c=c, q0=q0: v.tensor_sub(
                xlo[:, c * 512:(c + 1) * 512], fm(env.xres, c)[:, q0:q0 + 512],
                xhi[:, c * 512:(c + 1) * 512]))
        P.wait("t", "lq1")
        nmm = 0
        for c in range(NC):
            for wa, xa in ((W["whi"], xhi), (W["whi"], xlo), (W["wlo"], xhi)):
                nmm += 1
                def mm(t, c=c, wa=wa, xa=xa, first=(nmm == 1), last=(nmm == 24)):
                    return t.matmul(pz[0:1, 0:512], wa[:, c:c + 1],
                                    xa[:, c * 512:(c + 1) * 512],
                                    start=first, stop=last)
                if nmm == 24:
                    P.op("t", "l3z_mm", 1, mm)
                else:
                    P.run("t", lambda t, f=mm: f(t))
        P.wait("v", "l3z_mm")
        P.op("v", "l3z_e", 1, lambda v: v.tensor_copy(
            env.tmpf[0:1, 0:512], pz[0:1, 0:512]))
        P.wait("v", "l3z_e")
        P.op("v", "l3z_e", 1, lambda v: v.tensor_mul(
            env.tmpf[0:1, 0:512], env.tmpf[0:1, 0:512], env.stat_r[0:1, :]))
        P.wait("v", "l3z_e")
        P.op("v", "l3z_e", 1, lambda v: v.scalar_tensor_tensor(
            env.tmpf[0:1, 0:512], env.stat_b[0:1, :], W["sumw"],
            env.tmpf[0:1, 0:512], ALU.mult, ALU.add))
        srhi = env.stat_t.bitcast(BF16)[0:1, 0:512]
        srlo = env.stat_t.bitcast(BF16)[0:1, 512:1024]
        P.wait("s", "l3z_e")
        P.op("s", "l3z_e", 1, lambda s, srhi=srhi: s.activation(
            srhi, env.tmpf[0:1, 0:512], AF.Identity))
        P.wait("v", "l3z_e")
        P.op("v", "l3z_e", 1, lambda v, srhi=srhi, srlo=srlo: v.tensor_sub(
            srlo, env.tmpf[0:1, 0:512], srhi))
        P.wait("y", "l3z_e")
        P.op("y", "lq4", 16, lambda y, si=si, srhi=srhi: y.dma_start(
            out=env.l0f_in[2048 + si // 2:2049 + si // 2,
                           (si % 2) * 512:(si % 2) * 512 + 512],
            in_=srhi))
        P.op("y", "lq4", 16, lambda y, si=si, srlo=srlo: y.dma_start(
            out=env.l0f_in[2050 + si // 2:2051 + si // 2,
                           (si % 2) * 512:(si % 2) * 512 + 512],
            in_=srlo))
        P.wait("y", "lq4")
        P.wait("v", "lq4")
        P.wait("s", "lq4")
    return hook


def emit_l0_ffn(P, env, W):
    nc = P.nc

    def vx(tag, fn, eng="v"):
        P.wait(eng, tag)
        P.op(eng, tag, 1, fn)

    P.op("s", "lq2", 1, lambda s: s.activation(W["whi"], W["wcol"], AF.Identity))
    P.wait("v", "lq2")
    P.op("v", "lq1", 1, lambda v: v.tensor_sub(W["wlo"], W["wcol"], W["whi"]))
    P.wait("t", "lq1")

    def mmsw(t):
        t.matmul(env.pb[5][0:1, 0:8], env.ones_col_bf, W["whi"],
                 start=True, stop=True)
        return t.matmul(env.pb[5][0:1, 8:16], env.ones_col_bf, W["wlo"],
                        start=True, stop=True)
    P.op("t", "lq3", 1, mmsw)
    P.wait("v", "lq3")
    P.op("v", "lq1", 1, lambda v: v.reduce_sum(
        W["sumw"], env.pb[5][0:1, 0:16], axis=AX.X))
    P.wait("v", "lq1")

    emit_ln(P, env, [fm(env.xres, c) for c in range(NC)],
            [fm(env.xnorm, c) for c in range(NC)], tag="ln",
            span_hook=l0f_score_hook(W))

    stg = env.ar2[:, 0:1024]
    for tt in range(16):
        for c in range(NC):
            tb = env.pb[c % 2]
            if P.planned("l0f_tc") >= 2:
                P.wait("t", "l0f_tc", P.planned("l0f_tc") - 1)
            P.op("t", "lq3", 1, lambda t, tt=tt, c=c, tb=tb: t.transpose(
                tb.bitcast(BF16)[:, 0:128],
                fm(env.xnorm, c)[:, tt * 128:(tt + 1) * 128], W["ident_bf"]))
            P.wait("v", "lq3")
            P.op("v", "l0f_tc", 1, lambda v, c=c, tb=tb: v.tensor_copy(
                stg[:, c * 128:(c + 1) * 128], tb.bitcast(BF16)[:, 0:128]))
        P.wait("y", "l0f_tc")
        P.op("y", "lq4", 16, lambda y, tt=tt: y.dma_start(
            out=env.l0f_in[tt * 128:(tt + 1) * 128, :], in_=stg))
        P.wait("y", "lq4")
        P.wait("t", "lq4")
    P.wait("g", "lq4")
    P.wait("g", "lq4")
    P.op("g", "ccl0f", 1, lambda g: g.collective_compute(
        "AllGather", ALU.bypass, replica_groups=[[0, 1], [2, 3], [4, 5], [6, 7]],
        ins=[env.l0f_in], outs=[env.l0f_out]))
    P.wait("y", "ccl0f")

    for r in range(2):
        for j, (nm, r0) in enumerate((("shi", 2048), ("slo", 2050))):
            P.op("y", "lq4", 16, lambda y, r=r, j=j, r0=r0: y.dma_start(
                out=env.kbuf[0:16, (r * 2 + j) * 128:(r * 2 + j + 1) * 128],
                in_=env.l0f_out[r * 2052 + r0:r * 2052 + r0 + 2, :]
                .rearrange("a b -> (a b)").rearrange("(f p) -> f p", p=128)))
    P.wait("y", "lq4")
    P.wait("t", "lq4")
    for r in range(2):
        for j, nm in enumerate(("shi", "slo")):
            P.op("t", "lq3", 1, lambda t, r=r, j=j: t.transpose(
                env.pb[6].bitcast(BF16)[0:128, 0:16],
                env.kbuf[0:16, (r * 2 + j) * 128:(r * 2 + j + 1) * 128],
                W["ident_bf"][0:16, 0:16]))
            P.wait("v", "lq3")
            P.op("v", "lq1", 1, lambda v, r=r, nm=nm: v.tensor_copy(
                W[nm][:, r * 16:(r + 1) * 16], env.pb[6].bitcast(BF16)[:, 0:16]))
            P.wait("t", "lq1")
    P.wait("v", "lq4")
    P.wait("v", "lq1")
    P.op("v", "lq1", 1, lambda v: v.tensor_add(W["sfull"], W["shi"], W["slo"]))
    P.wait("v", "lq1")
    P.wait("s", "lq1")

    mc = env.mcol
    vx("lq1", lambda v: v.reduce_max(mc[:, 0:1], W["sfull"], axis=AX.X))
    vx("lq1", lambda v: v.tensor_scalar_mul(W["m"], W["sfull"], -1.0))
    vx("lq1", lambda v: v.reduce_max(mc[:, 1:2], W["m"], axis=AX.X))
    P.wait("t", "lq1")
    def mmnx(t):
        t.transpose(env.pb[6][0:1, 0:128], mc[:, 0:1], W["ident_f32"])
        return t.transpose(env.pb[7][0:1, 0:128], mc[:, 1:2], W["ident_f32"])
    P.op("t", "lq3", 1, mmnx)
    P.wait("v", "lq3")
    vx("lq1", lambda v: v.reduce_max(mc[0:1, 2:3], env.pb[6][0:1, 0:128], axis=AX.X))
    vx("lq1", lambda v: v.reduce_max(mc[0:1, 3:4], env.pb[7][0:1, 0:128], axis=AX.X))
    vx("lq1", lambda v: v.tensor_scalar(mc[0:1, 8:9], mc[0:1, 3:4], -1.0, None,
                                       ALU.mult))
    vx("lq1", lambda v: v.tensor_scalar(mc[0:1, 9:10], mc[0:1, 2:3], 1.0, None,
                                       ALU.add))
    for it in range(48):
        vx("lq1", lambda v: v.tensor_add(mc[0:1, 10:11], mc[0:1, 8:9],
                                        mc[0:1, 9:10]))
        vx("lq1", lambda v: v.tensor_scalar_mul(mc[0:1, 10:11], mc[0:1, 10:11], 0.5))
        P.wait("t", "lq1")
        P.op("t", "lq3", 1, lambda t: t.matmul(
            env.pb[6][:, 0:1], env.ones_row_f32[0:1, :], mc[0:1, 10:11],
            start=True, stop=True))
        P.wait("v", "lq3")
        vx("lq1", lambda v: v.tensor_copy(mc[:, 11:12], env.pb[6][:, 0:1]))
        vx("lq1", lambda v: v.tensor_scalar(W["m"], W["sfull"], mc[:, 11:12], None,
                                           ALU.is_ge))
        P.wait("t", "lq1")
        P.op("t", "lq3", 1, lambda t: t.matmul(
            env.pb[7][0:1, 0:32], env.ones_col_f32, W["m"], start=True, stop=True))
        P.wait("v", "lq3")
        vx("lq1", lambda v: v.reduce_sum(mc[0:1, 12:13], env.pb[7][0:1, 0:32],
                                        axis=AX.X))
        vx("lq1", lambda v: v.tensor_scalar(mc[0:1, 13:14], mc[0:1, 12:13], 512.0,
                                           None, ALU.is_ge))
        vx("lq1", lambda v: v.tensor_sub(mc[0:1, 14:15], mc[0:1, 10:11],
                                        mc[0:1, 8:9]))
        vx("lq1", lambda v: v.tensor_mul(mc[0:1, 14:15], mc[0:1, 14:15],
                                        mc[0:1, 13:14]))
        vx("lq1", lambda v: v.tensor_add(mc[0:1, 8:9], mc[0:1, 8:9],
                                        mc[0:1, 14:15]))
        vx("lq1", lambda v: v.tensor_sub(mc[0:1, 14:15], mc[0:1, 9:10],
                                        mc[0:1, 10:11]))
        vx("lq1", lambda v: v.tensor_mul(mc[0:1, 14:15], mc[0:1, 14:15],
                                        mc[0:1, 13:14]))
        vx("lq1", lambda v: v.tensor_add(mc[0:1, 9:10], mc[0:1, 10:11],
                                        mc[0:1, 14:15]))
    P.wait("t", "lq1")
    P.op("t", "lq3", 1, lambda t: t.matmul(
        env.pb[6][:, 0:1], env.ones_row_f32[0:1, :], mc[0:1, 8:9],
        start=True, stop=True))
    P.wait("v", "lq3")
    vx("lq1", lambda v: v.tensor_copy(mc[:, 11:12], env.pb[6][:, 0:1]))
    vx("lq1", lambda v: v.tensor_scalar(W["m"], W["sfull"], mc[:, 11:12], None,
                                       ALU.is_ge))
    vx("lq1", lambda v: v.tensor_tensor_scan(
        W["pos"], W["m"], W["m"], 0.0, ALU.add, ALU.bypass))
    vx("lq1", lambda v: v.tensor_sub(W["pos"], W["pos"], W["m"]))
    vx("lq1", lambda v: v.tensor_add(mc[:, 12:13], W["pos"][:, 31:32],
                                    W["m"][:, 31:32]))
    P.wait("t", "lq1")
    P.op("t", "lq3", 1, lambda t: t.transpose(
        env.pb[6][0:1, 0:128], mc[:, 12:13], W["ident_f32"]))
    P.wait("v", "lq3")
    vx("lq1", lambda v: v.tensor_copy(env.tmpf[0:1, 0:128], env.pb[6][0:1, 0:128]))
    vx("lq1", lambda v: v.tensor_tensor_scan(
        env.tmpf[0:1, 128:256], env.tmpf[0:1, 0:128],
        env.tmpf[0:1, 0:128], 0.0, ALU.add, ALU.bypass))
    vx("lq1", lambda v: v.tensor_sub(env.tmpf[0:1, 128:256],
                                    env.tmpf[0:1, 128:256], env.tmpf[0:1, 0:128]))
    P.wait("t", "lq1")
    P.op("t", "lq3", 1, lambda t: t.transpose(
        env.pb[6][0:128, 0:1], env.tmpf[0:1, 128:256],
        W["ident_f32"][0:1, 0:1]))
    P.wait("v", "lq3")
    vx("lq1", lambda v: v.tensor_copy(mc[:, 13:14], env.pb[6][:, 0:1]))
    vx("lq1", lambda v: v.tensor_scalar(W["pos"], W["pos"], mc[:, 13:14], None,
                                       ALU.add))

    P.wait("v", "lq1")
    for f in range(32):
        vx("lq1", lambda v, f=f: v.tensor_scalar(
            env.tmpf[:, 256:257], W["pos"][:, f:f + 1], W["m"][:, f:f + 1],
            None, ALU.mult))
        for qb in range(4):
            vx("lq1", lambda v, f=f, qb=qb: v.tensor_scalar(
                env.tmpf[:, 257:385], W["io128f"], env.tmpf[:, 256:257],
                float(128 * qb), ALU.subtract, ALU.add))
            vx("lq1", lambda v, f=f, qb=qb: v.tensor_scalar(
                env.tmpf[:, 257:385], env.tmpf[:, 257:385], 0.0, None,
                ALU.is_equal))
            vx("lq1", lambda v, f=f, qb=qb: v.tensor_scalar(
                env.ar1[:, f * 512 + qb * 128:f * 512 + (qb + 1) * 128],
                env.tmpf[:, 257:385], W["m"][:, f:f + 1], None, ALU.mult))
    P.wait("t", "lq1")
    P.wait("y", "lq1")

    base_lk = [P.planned("lk0"), P.planned("lk1")]
    for F in range(32):
        r, fl = F // 16, F % 16
        P.wait_if("y", "lq3")
        P.op("y", "lk%d" % (F % 2), 16, lambda y, F=F, r=r, fl=fl: y.dma_start(
            out=env.kbuf[:, (F % 2) * 1024:(F % 2) * 1024 + 1024],
            in_=env.l0f_out[r * 2052 + fl * 128:r * 2052 + (fl + 1) * 128, :]))
        P.wait("t", "lk%d" % (F % 2), base_lk[F % 2] + 16 * (F // 2 + 1))
        for qb in range(4):
            for half in range(2):
                bank = env.pb[qb * 2 + half]

                def mm(t, F=F, qb=qb, half=half, bank=bank):
                    return t.matmul(
                        bank[:, 0:512],
                        env.ar1[:, F * 512 + qb * 128:F * 512 + (qb + 1) * 128],
                        env.kbuf[:, (F % 2) * 1024 + half * 512:
                                 (F % 2) * 1024 + (half + 1) * 512],
                        start=(F == 0), stop=(F == 31))
                if qb == 3 and half == 1:
                    P.op("t", "lq3", 1, mm)
                else:
                    P.run("t", lambda t, f2=mm: f2(t))
    P.wait("v", "lq3")
    for qb in range(4):
        for half in range(2):
            P.op("v", "lq1", 1, lambda v, qb=qb, half=half: v.tensor_copy(
                env.ar2[:, qb * 1024 + half * 512:qb * 1024 + (half + 1) * 512],
                env.pb[qb * 2 + half][:, 0:512]))
    P.wait("t", "lq1")
    for F in range(32):
        for qb in range(4):
            for part, sv in ((0, W["shi"]), (4, W["slo"])):
                def mm2(t, F=F, qb=qb, part=part, sv=sv):
                    return t.matmul(
                        env.pb[qb + part][:, 0:1],
                        env.ar1[:, F * 512 + qb * 128:F * 512 + (qb + 1) * 128],
                        sv[:, F:F + 1], start=(F == 0), stop=(F == 31))
                if qb == 3 and part == 4:
                    P.op("t", "lq3", 1, mm2)
                else:
                    P.run("t", lambda t, f2=mm2: f2(t))
    P.wait("v", "lq3")
    for qb in range(4):
        P.op("v", "lq1", 1, lambda v, qb=qb: v.tensor_copy(
            mc[:, qb:qb + 1], env.pb[qb][:, 0:1]))
        P.wait("v", "lq1")
        P.op("v", "lq1", 1, lambda v, qb=qb: v.tensor_add(
            mc[:, qb:qb + 1], mc[:, qb:qb + 1], env.pb[qb + 4][:, 0:1]))
    P.wait("t", "lq1")
    for qb in range(4):
        P.op("t", "lq3", 1, lambda t, qb=qb: t.transpose(
            env.pb[4][0:1, 0:128], mc[:, qb:qb + 1], W["ident_f32"]))
        P.wait("v", "lq3")
        P.op("v", "lq1", 1, lambda v, qb=qb: v.tensor_copy(
            env.stat_m[0:1, qb * 128:(qb + 1) * 128], env.pb[4][0:1, 0:128]))
        P.wait("t", "lq1")
    P.op("t", "lq3", 1, lambda t: t.matmul(
        env.pb[5][:, 0:512], env.ones_row_f32[0:1, :], env.stat_m[0:1, :],
        start=True, stop=True))
    P.wait("v", "lq3")
    P.op("v", "lq1", 1, lambda v: v.tensor_copy(env.tmpf[:, :], env.pb[5][:, 0:512]))
    P.wait("v", "lq1")
    P.op("y", "lq4", 16, lambda y: y.dma_start(
        out=W["iofull"], in_=W["iofull_dr"]))
    P.wait("y", "lq4")
    P.wait("v", "lq4")
    G = env.f1[:, 0:512]
    E = env.f1[:, 512:1024]
    base_rm = P.planned("l0f_rm")
    for qb in range(4):
        if qb:
            P.wait("v", "l0f_rm", base_rm + qb)
        vx("lq1", lambda v, qb=qb: v.tensor_scalar(
            G, env.tmpf[:, :], mc[:, qb:qb + 1], None, ALU.is_lt))
        vx("lq1", lambda v, qb=qb: v.tensor_scalar(
            E, env.tmpf[:, :], mc[:, qb:qb + 1], None, ALU.is_equal))
        vx("lq1", lambda v, qb=qb: v.tensor_scalar(
            env.f1[:, 1024:1536], W["iofull"], W["pcol"], float(-128 * qb),
            ALU.subtract, ALU.add))
        vx("lq1", lambda v, qb=qb: v.tensor_scalar(
            env.f1[:, 1024:1536], env.f1[:, 1024:1536], 0.0, None, ALU.is_gt))
        vx("lq1", lambda v: v.tensor_mul(E, E, env.f1[:, 1024:1536]))
        vx("lq1", lambda v: v.tensor_add(G, G, E))
        P.wait("t", "lq1")
        P.op("t", "l0f_rm", 1, lambda t, qb=qb: t.matmul(
            env.pb[6][0:1, 0:512], env.ones_col_f32, G,
            start=(qb == 0), stop=(qb == 3)))
    P.wait("v", "l0f_rm")
    P.op("v", "lq1", 1, lambda v: v.tensor_copy(
        env.stat_t[0:1, :], env.pb[6][0:1, 0:512]))
    P.wait("t", "lq1")
    for qb in range(4):
        P.op("t", "lq3", 1, lambda t, qb=qb: t.transpose(
            env.pb[4][0:128, 0:1], env.stat_t[0:1, qb * 128:(qb + 1) * 128],
            W["ident_f32"][0:1, 0:1]))
        P.wait("v", "lq3")
        P.op("v", "lq1", 1, lambda v, qb=qb: v.tensor_copy(
            mc[:, 4 + qb:5 + qb], env.pb[4][:, 0:1]))
        P.wait("t", "lq1")
    P.wait("v", "lq1")
    for qb in range(4):
        for rb in range(4):
            vx("lq1", lambda v, qb=qb, rb=rb: v.tensor_scalar(
                env.tmpf[:, 0:128], W["io128f"], mc[:, 4 + qb:5 + qb],
                float(128 * rb), ALU.subtract, ALU.add))
            vx("lq1", lambda v, qb=qb, rb=rb: v.tensor_scalar(
                env.ar2[:, 4096 + (qb * 4 + rb) * 128:4096 + (qb * 4 + rb + 1) * 128],
                env.tmpf[:, 0:128], 0.0, None, ALU.is_equal))
    P.wait("t", "lq1")

    if getattr(env, "l0f_dump", None):
        P.wait_all("y")
        P.op("y", "lqd", 16, lambda y: y.dma_start(
            out=env.l0f_dump["qt"], in_=env.ar2[:, 4096:6144]))
        P.op("y", "lqd", 16, lambda y: y.dma_start(
            out=env.l0f_dump["mc"], in_=mc[:, 4:8]))
        P.op("y", "lqd", 16, lambda y: y.dma_start(
            out=env.l0f_dump["rrow"], in_=env.stat_t[0:1, :]))
        P.op("y", "lqd", 16, lambda y: y.dma_start(
            out=env.l0f_dump["sel"], in_=env.ar1[:, :]))
        P.op("y", "lqd", 16, lambda y: y.dma_start(
            out=env.l0f_dump["pos"], in_=W["pos"]))
        P.wait("y", "lqd")
        P.wait("t", "lqd")

    qt = lambda qb, rb: env.ar2[:, 4096 + (qb * 4 + rb) * 128:
                                4096 + (qb * 4 + rb + 1) * 128]
    for qb in range(4):
        for rb in range(4):
            for half in range(2):
                bank = env.pb[rb * 2 + half]

                def mm(t, qb=qb, rb=rb, half=half, bank=bank):
                    return t.matmul(
                        bank[:, 0:512], qt(qb, rb),
                        env.ar2[:, qb * 1024 + half * 512:qb * 1024 + (half + 1) * 512],
                        start=(qb == 0), stop=(qb == 3))
                if qb == 3 and rb == 3 and half == 1:
                    P.op("t", "lq3", 1, mm)
                else:
                    P.run("t", lambda t, f2=mm: f2(t))
    P.wait("v", "lq3")
    for rb in range(4):
        for half in range(2):
            P.op("v", "lq1", 1, lambda v, rb=rb, half=half: v.tensor_copy(
                env.ar2[:, 6144 + rb * 1024 + half * 512:
                        6144 + rb * 1024 + (half + 1) * 512],
                env.pb[rb * 2 + half][:, 0:512]))
    P.wait("t", "lq1")

    selt = env.f1.bitcast(BF16)[:, 4096:6144]
    for j in range(32):
        P.wait("t", "l0f_cm")
        for qb in range(4):
            tb = env.pb[4 + qb % 2]
            if P.planned("l0f_se2") >= 2:
                P.wait("t", "l0f_se2", P.planned("l0f_se2") - 1)

            def tr(t, j=j, qb=qb, tb=tb):
                return t.transpose(
                    tb.bitcast(BF16)[:, 0:128],
                    env.ar1[:, j * 512 + qb * 128:j * 512 + (qb + 1) * 128],
                    W["ident_bf"])
            P.op("t", "lq3", 1, tr)
            P.wait("v", "lq3")
            P.op("v", "l0f_se2", 1, lambda v, qb=qb, tb=tb: v.tensor_copy(
                selt[:, qb * 128:(qb + 1) * 128], tb.bitcast(BF16)[:, 0:128]))
        P.wait("t", "l0f_se2")
        for rb in range(4):
            for qb in range(4):
                bank = env.pb[rb]

                def mm(t, rb=rb, qb=qb, bank=bank):
                    return t.matmul(
                        bank[:, 0:128], qt(qb, rb), selt[:, qb * 128:(qb + 1) * 128],
                        start=(qb == 0), stop=(qb == 3))
                if qb == 3:
                    P.op("t", "l0f_cm", 1, mm)
                else:
                    P.run("t", lambda t, f2=mm: f2(t))
        P.wait("v", "l0f_cm")
        P.wait_if("v", "lq4")
        for rb in range(4):
            P.op("v", "lq1", 1, lambda v, rb=rb: v.tensor_copy(
                env.ar2[:, 14336 + rb * 128:14336 + (rb + 1) * 128],
                env.pb[rb][:, 0:128]))
        P.wait("y", "lq1")
        P.op("y", "lq4", 16, lambda y, j=j: y.dma_start(
            out=env.l0f_C[:, j * 128:(j + 1) * 128].rearrange(
                "(rb q) t -> q rb t", q=128),
            in_=env.ar2[:, 14336:14848].rearrange("q (rb t) -> q rb t", rb=4)))
        P.wait("y", "lq4")
    P.wait("t", "lq4")
    P.wait("v", "lq4")

    for half in range(2):
        P.op("t", "lq3", 1, lambda t, half=half: t.matmul(
            env.pb[6][:, 0:512], env.ones_row_bf[0:1, :],
            W["g1row"][0:1, half * 512:(half + 1) * 512], start=True, stop=True))
        P.wait("v", "lq3")
        P.op("v", "lq1", 1, lambda v, half=half: v.tensor_copy(
            W["g1b"][:, half * 512:(half + 1) * 512], env.pb[6][:, 0:512]))
        P.wait("t", "lq1")

    P.op("v", "lq1", 1, lambda v: v.memset(mc[:, 15:16], 1e-6))
    P.wait("s", "lq1")
    for rb in range(4):
        xs = env.ar2[:, 6144 + rb * 1024:6144 + (rb + 1) * 1024]
        h1 = env.ar2[:, 10240 + rb * 1024:10240 + (rb + 1) * 1024]
        P.op("s", "lq2", 1, lambda s, xs=xs: s.activation(
            env.f1[:, 0:1024], xs, AF.Square, accum_out=env.lcol[:, 0:1]))
        P.wait("s", "lq2")
        P.wait("s", "lq1")
        P.op("s", "lq2", 1, lambda s: s.activation(
            env.lcol[:, 1:2], env.lcol[:, 0:1], AF.Ln, scale=1.0 / 1024.0,
            bias=mc[:, 15:16]))
        P.wait("s", "lq2")
        P.op("s", "lq2", 1, lambda s: s.activation(
            env.lcol[:, 1:2], env.lcol[:, 1:2], AF.Exp, scale=-0.5))
        P.wait("v", "lq2")
        P.op("v", "lq1", 1, lambda v, xs=xs, h1=h1: v.tensor_scalar(
            h1, xs, env.lcol[:, 1:2], None, ALU.mult))
        P.wait("v", "lq1")
        P.op("v", "lq1", 1, lambda v, h1=h1: v.tensor_mul(h1, h1, W["g1b"]))
    P.wait("t", "lq1")
    for rb in range(4):
        for c in range(8):
            tb = env.pb[c % 2]
            if P.planned("l0f_hc2") >= 2:
                P.wait("t", "l0f_hc2", P.planned("l0f_hc2") - 1)
            P.op("t", "lq3", 1, lambda t, rb=rb, c=c, tb=tb: t.transpose(
                tb.bitcast(BF16)[:, 0:128],
                env.ar2[:, 10240 + rb * 1024 + c * 128:10240 + rb * 1024 + (c + 1) * 128],
                W["ident_bf"]))
            P.wait("v", "lq3")
            P.op("v", "l0f_hc2", 1, lambda v, rb=rb, c=c, tb=tb: v.tensor_copy(
                env.ar2[:, c * 512 + rb * 128:c * 512 + (rb + 1) * 128],
                tb.bitcast(BF16)[:, 0:128]))
    P.wait("t", "l0f_hc2")
    P.op("y", "lq4", 16, lambda y: y.dma_start(
        out=W["masksb"], in_=W["masksb_dr"]))
    P.wait("y", "lq4")
    P.wait("v", "lq4")

    def xin_h1(ic, tt0, tt1):
        return env.ar2[:, ic * 512 + tt0:ic * 512 + tt1]

    def evac_q(P, oc, t0, t1, ps, gtag, gval):
        P.wait("s", gtag, gval)
        P.op("s", "pj_ev", 1, lambda s, oc=oc, ps=ps: s.activation(
            env.kbuf[:, oc * 512:(oc + 1) * 512], ps, AF.Identity))
    emit_proj(P, env, W["wqkvT_bf"][:, 0:1024], D, D, xin_h1, evac_q, "pj",
              ts_range=[(0, 512)])
    kslot = [4096 + oc * 512 if oc < 4 else 14336 + (oc - 4) * 512
             for oc in range(8)]

    def evac_k(P, oc, t0, t1, ps, gtag, gval):
        P.wait("s", gtag, gval)
        P.op("s", "pj_ev", 1, lambda s, oc=oc, ps=ps: s.activation(
            env.ar2[:, kslot[oc]:kslot[oc] + 512], ps, AF.Identity))
    emit_proj(P, env, W["wqkvT_bf"][:, 1024:2048], D, D, xin_h1, evac_k, "pj",
              ts_range=[(0, 512)])

    def xin_h1c(ic, tt):
        return env.ar2[:, ic * 512 + tt * 128:ic * 512 + (tt + 1) * 128]

    def evac_v(P, tt, oh, ps, gtag, gval):
        P.wait("v", gtag, gval)
        P.op("v", "pjtm_ev", 1, lambda v, tt=tt, oh=oh, ps=ps: v.tensor_copy(
            env.ar2[:, 10240 + tt * 1024 + oh * 512:10240 + tt * 1024 + (oh + 1) * 512],
            ps))
    emit_proj_tm(P, env, W["wqkvT_bf"][:, 2048:3072], D, xin_h1c, None,
                 evac_v, "pjtm", ts_sub=(0, 512))
    P.wait("t", "pjtm_ev")
    P.wait("t", "pj_ev")

    sheet = env.ar1.bitcast(F32)
    pts = [env.f1.bitcast(BF16)[:, 6144 + i * 512:6144 + (i + 1) * 512]
           for i in range(4)]
    oT = env.ar2[:, 0:4096]
    for qb in range(4):
        base_sm = P.planned("lq3")
        for h in range(16):
            c, b_ = h // 2, 64 * (h % 2)
            bank = env.pb[4 + h % 2]
            if P.planned("l0f_ae") >= 2:
                P.wait("t", "l0f_ae", P.planned("l0f_ae") - 1)

            def mm(t, qb=qb, h=h, c=c, b_=b_, bank=bank):
                return t.matmul(
                    bank[:, 0:512],
                    env.kbuf[b_:b_ + 64, c * 512 + qb * 128:c * 512 + (qb + 1) * 128],
                    env.ar2[b_:b_ + 64, kslot[c]:kslot[c] + 512],
                    start=True, stop=True)
            P.op("t", "lq3", 1, mm)
            P.wait("s", "lq3")
            P.op("s", "l0f_ae", 1, lambda s, h=h, bank=bank: s.activation(
                sheet[:, h * 512:(h + 1) * 512], bank[:, 0:512], AF.Identity,
                scale=0.125))
        P.wait("v", "l0f_ae")
        for h in range(16):
            P.op("v", "lq1", 1, lambda v, h=h, qb=qb: v.tensor_add(
                sheet[:, h * 512:(h + 1) * 512], sheet[:, h * 512:(h + 1) * 512],
                W["masksb"][:, qb * 512:(qb + 1) * 512]))
        P.wait("v", "lq1")
        P.op("v", "lq1", 1, lambda v: v.reduce_max(
            env.lcol[:, 0:16], sheet[:, 0:8192].rearrange("p (h k) -> p h k", h=16),
            axis=AX.X))
        P.wait("v", "lq1")
        P.op("v", "lq1", 1, lambda v: v.tensor_scalar_mul(
            env.lcol[:, 0:16], env.lcol[:, 0:16], -1.0))
        P.wait("s", "lq1")
        for h in range(16):
            P.op("s", "lq2", 1, lambda s, h=h: s.activation(
                sheet[:, h * 512:(h + 1) * 512], sheet[:, h * 512:(h + 1) * 512],
                AF.Exp, bias=env.lcol[:, h:h + 1],
                accum_out=env.mcol[:, h:h + 1]))
        P.wait("v", "lq2")
        P.op("v", "lq1", 1, lambda v: v.reciprocal(
            env.lcol[:, 0:16], env.mcol[:, 0:16]))
        P.wait("v", "lq1")
        for h in range(16):
            P.op("v", "lq1", 1, lambda v, h=h: v.tensor_scalar(
                sheet[:, h * 512:(h + 1) * 512],
                sheet[:, h * 512:(h + 1) * 512], env.lcol[:, h:h + 1], None,
                ALU.mult))
        P.wait("t", "lq1")
        for h in range(16):
            c, b_ = h // 2, 64 * (h % 2)
            P.wait_if("v", "lq3")
            for kb in range(4):
                tb = env.pb[6 + kb % 2]
                if P.planned("l0f_ptc") >= 2:
                    P.wait("t", "l0f_ptc", P.planned("l0f_ptc") - 1)
                P.op("t", "lq3", 1, lambda t, h=h, kb=kb, tb=tb: t.transpose(
                    tb[:, 0:128], sheet[:, h * 512 + kb * 128:h * 512 + (kb + 1) * 128],
                    W["ident_f32"]))
                P.wait("v", "lq3")
                P.op("v", "l0f_ptc", 1, lambda v, h=h, kb=kb, tb=tb: v.tensor_copy(
                    pts[kb][:, 0:128], tb[:, 0:128]))
            P.wait("t", "l0f_ptc")
            bank = env.pb[h % 2]
            if P.planned("l0f_oe") >= 2:
                P.wait("t", "l0f_oe", P.planned("l0f_oe") - 1)
            for kb in range(4):
                def mmo(t, h=h, kb=kb, qb=qb, bank=bank):
                    return t.matmul(
                        bank[0:64, 0:128],
                        env.ar2[:, 10240 + kb * 1024 + h * 64:10240 + kb * 1024 + (h + 1) * 64],
                        pts[kb][:, 0:128],
                        start=(kb == 0), stop=(kb == 3))
                if kb == 3:
                    P.op("t", "lq3", 1, mmo)
                else:
                    P.run("t", lambda t, f2=mmo: f2(t))
            P.wait("v", "lq3")
            P.op("v", "l0f_oe", 1, lambda v, h=h, qb=qb, bank=bank: v.tensor_copy(
                oT[64 * (h % 2):64 * (h % 2) + 64,
                   (h // 2) * 512 + qb * 128:(h // 2) * 512 + (qb + 1) * 128],
                bank[0:64, 0:128]))
    P.wait("t", "l0f_oe")

    for rb in range(4):
        for c in range(8):
            tb = env.pb[6 + c % 2]
            if P.planned("l0f_xc") >= 2:
                P.wait("t", "l0f_xc", P.planned("l0f_xc") - 1)
            P.op("t", "lq3", 1, lambda t, rb=rb, c=c, tb=tb: t.transpose(
                tb.bitcast(BF16)[:, 0:128],
                env.ar2[:, 6144 + rb * 1024 + c * 128:6144 + rb * 1024 + (c + 1) * 128],
                W["ident_bf"]))
            P.wait("v", "lq3")
            P.op("v", "l0f_xc", 1, lambda v, rb=rb, c=c, tb=tb: v.tensor_copy(
                env.kbuf[:, c * 512 + rb * 128:c * 512 + (rb + 1) * 128],
                tb.bitcast(BF16)[:, 0:128]))
    P.wait("t", "l0f_xc")
    P.wait("v", "l0f_xc")

    def xin_o(ic, tt0, tt1):
        return oT[:, ic * 512:(ic + 1) * 512]

    def evac_wo(P, oc, t0, t1, ps, gtag, gval):
        P.wait("v", gtag, gval)
        P.op("v", "pj_ev", 1, lambda v, oc=oc, ps=ps: v.tensor_add(
            env.f1[:, oc * 512:(oc + 1) * 512], ps,
            env.kbuf[:, oc * 512:(oc + 1) * 512]))
    emit_proj(P, env, W["woutT_bf"], D, D, xin_o, evac_wo, "pj",
              ts_range=[(0, 512)])
    P.wait("s", "pj_ev")
    P.wait("v", "pj_ev")

    for c in range(NC):
        P.op("s", "lq2", 1, lambda s, c=c: s.activation(
            env.tmpf[:, 0:512], env.f1[:, c * 512:(c + 1) * 512], AF.Square))
        P.wait("t", "lq2")
        P.op("t", "l0f_qm2", 1, lambda t, c=c: t.matmul(
            env.pb[6][0:1, 0:512], env.ones_col_f32, env.tmpf[:, 0:512],
            start=(c == 0), stop=(c == NC - 1)))
        if c < NC - 1:
            P.wait("s", "l0f_qm2", P.planned("l0f_qm2"))
    P.wait("s", "l0f_qm2")
    P.wait("v", "l0f_qm2")
    P.op("v", "lq1", 1, lambda v: v.memset(mc[:, 15:16], 1e-6))
    P.wait("s", "lq1")
    P.op("s", "lq2", 1, lambda s: s.activation(
        env.stat_m[0:1, :], env.pb[6][0:1, 0:512], AF.Ln, scale=1.0 / 1024.0,
        bias=mc[0:1, 15:16]))
    P.wait("s", "lq2")
    P.op("s", "lq2", 1, lambda s: s.activation(
        env.stat_m[0:1, :], env.stat_m[0:1, :], AF.Exp, scale=-0.5))
    P.wait("t", "lq2")
    P.op("t", "lq3", 1, lambda t: t.matmul(
        env.pb[7][:, 0:512], env.ones_row_f32[0:1, :], env.stat_m[0:1, :],
        start=True, stop=True))
    P.wait("v", "lq3")
    for c in range(NC):
        P.op("v", "lq1", 1, lambda v, c=c: v.tensor_mul(
            env.ar2[:, 4096 + c * 512:4096 + (c + 1) * 512],
            env.f1[:, c * 512:(c + 1) * 512], env.pb[7][:, 0:512]))
        P.wait("v", "lq1")
        P.op("v", "lq1", 1, lambda v, c=c: v.tensor_scalar(
            env.ar2[:, 4096 + c * 512:4096 + (c + 1) * 512],
            env.ar2[:, 4096 + c * 512:4096 + (c + 1) * 512],
            W["g2cols"][:, c:c + 1], None, ALU.mult))
    P.wait("t", "lq1")

    def xin_h2(ic, tt0, tt1):
        return env.ar2[:, 4096 + ic * 512:4096 + (ic + 1) * 512]

    def evac_w1(P, oc, t0, t1, ps, gtag, gval):
        P.wait("s", gtag, gval)
        P.op("s", "pj_ev", 1, lambda s, oc=oc, ps=ps: s.activation(
            env.ar1[:, oc * 512:(oc + 1) * 512], ps, AF.Silu))
    emit_proj(P, env, W["w1T_bf"], D, 2816, xin_h2, evac_w1, "pj",
              ts_range=[(0, 512)])

    def evac_w2(P, oc, t0, t1, ps, gtag, gval):
        P.wait("v", gtag, gval)
        P.op("v", "pj_ev", 1, lambda v, oc=oc, ps=ps: v.tensor_mul(
            env.ar1[:, oc * 512:(oc + 1) * 512],
            env.ar1[:, oc * 512:(oc + 1) * 512], ps))
    emit_proj(P, env, W["w2T_bf"], D, 2816, xin_h2, evac_w2, "pj",
              ts_range=[(0, 512)])

    def xin_a(ic, tt0, tt1):
        return env.ar1[:, ic * 512:(ic + 1) * 512]

    def evac_w3(P, oc, t0, t1, ps, gtag, gval):
        P.wait("v", gtag, gval)
        P.op("v", "pj_ev", 1, lambda v, oc=oc, ps=ps: v.tensor_add(
            env.f1[:, oc * 512:(oc + 1) * 512],
            env.f1[:, oc * 512:(oc + 1) * 512], ps))
    emit_proj(P, env, W["w3T_bf"], 2816, D, xin_a, evac_w3, "pj",
              ts_range=[(0, 512)])
    P.wait("t", "pj_ev")
    P.wait("v", "pj_ev")

    if getattr(env, "l0f_dump", None) and "xp" in env.l0f_dump:
        P.wait_all("y")
        P.op("y", "lqd", 16, lambda y: y.dma_start(
            out=env.l0f_dump["xp"], in_=env.f1[:, :]))
        P.wait("y", "lqd")
        P.wait("t", "lqd")

    for c in range(NC):
        for rb in range(4):
            tb = env.pb[0 + rb % 2]
            if P.planned("l0f_pc2") >= 2:
                P.wait("t", "l0f_pc2", P.planned("l0f_pc2") - 1)
            P.op("t", "lq3", 1, lambda t, c=c, rb=rb, tb=tb: t.transpose(
                tb[:, 0:128], env.f1[:, c * 512 + rb * 128:c * 512 + (rb + 1) * 128],
                W["ident_f32"]))
            P.wait("v", "lq3")
            P.op("v", "l0f_pc2", 1, lambda v, c=c, rb=rb, tb=tb: v.tensor_copy(
                env.ar2[:, 10240 + rb * 1024 + c * 128:10240 + rb * 1024 + (c + 1) * 128],
                tb[:, 0:128]))
    P.wait("t", "l0f_pc2")
    P.wait("y", "l0f_pc2")

    for si in range(4):
        t0 = si * 512
        P.wait_if("y", "lq1")
        P.wait_if("y", "lq3")
        for rb in range(4):
            P.op("y", "lq4", 16, lambda y, t0=t0, rb=rb: y.dma_start(
                out=env.ar2[:, rb * 1024:rb * 1024 + 512],
                in_=env.l0f_C[rb * 128:(rb + 1) * 128, t0:t0 + 512]))
            P.op("y", "lq4", 16, lambda y, t0=t0, rb=rb: y.dma_start(
                out=env.ar2[:, rb * 1024 + 512:rb * 1024 + 1024],
                in_=env.l0f_C[rb * 128:(rb + 1) * 128, 2048 + t0:2048 + t0 + 512]))
        P.wait("y", "lq4")
        P.wait("v", "lq4")
        for rb in range(4):
            P.op("v", "lq1", 1, lambda v, rb=rb: v.tensor_scalar(
                env.ar2[:, rb * 1024:rb * 1024 + 512],
                env.ar2[:, rb * 1024:rb * 1024 + 512], W["flag0"], None, ALU.mult))
            P.op("v", "lq1", 1, lambda v, rb=rb: v.tensor_scalar(
                env.ar2[:, rb * 1024 + 512:rb * 1024 + 1024],
                env.ar2[:, rb * 1024 + 512:rb * 1024 + 1024], W["flag1"], None,
                ALU.mult))
        P.wait("v", "lq1")
        for rb in range(4):
            P.op("v", "lq1", 1, lambda v, rb=rb: v.tensor_add(
                env.ar2[:, rb * 1024:rb * 1024 + 512],
                env.ar2[:, rb * 1024:rb * 1024 + 512],
                env.ar2[:, rb * 1024 + 512:rb * 1024 + 1024]))
        for j in range(4):
            vx("lq1", lambda v, si=si, j=j: v.tensor_scalar(
                mc[:, j:j + 1], W["m"][:, si * 4 + j:si * 4 + j + 1],
                W["flag0"], None, ALU.mult))
            vx("lq1", lambda v, si=si, j=j: v.scalar_tensor_tensor(
                mc[:, j:j + 1], W["m"][:, 16 + si * 4 + j:16 + si * 4 + j + 1],
                W["flag1"], mc[:, j:j + 1], ALU.mult, ALU.add))
        P.wait("t", "lq1")
        for j in range(4):
            P.op("t", "lq3", 1, lambda t, j=j: t.transpose(
                env.pb[6 + j % 2][0:1, 0:128], mc[:, j:j + 1], W["ident_f32"]))
            P.wait("v", "lq3")
            vx("lq1", lambda v, j=j: v.tensor_copy(
                env.stat_m[0:1, j * 128:(j + 1) * 128],
                env.pb[6 + j % 2][0:1, 0:128]))
            P.wait("t", "lq1")
        P.wait("t", "lq1")
        P.op("t", "lq3", 1, lambda t: t.matmul(
            env.pb[7][:, 0:512], env.ones_row_f32[0:1, :], env.stat_m[0:1, :],
            start=True, stop=True))
        P.wait("s", "lq3")
        P.op("s", "lq2", 1, lambda s: s.activation(
            env.tmpf[:, 0:512], env.pb[7][:, 0:512], AF.Identity, scale=-1.0,
            bias=env.ones_col_f32))
        P.wait("t", "lq2")
        P.wait("v", "lq2")
        base_sc = P.planned("lq1")
        for mch in range(NC):
            bank = env.pb[mch]
            for rb in range(4):
                def mm(t, mch=mch, rb=rb, bank=bank):
                    return t.matmul(
                        bank[:, 0:512],
                        env.ar2[:, 10240 + rb * 1024 + mch * 128:
                                10240 + rb * 1024 + (mch + 1) * 128],
                        env.ar2[:, rb * 1024:rb * 1024 + 512],
                        start=(rb == 0), stop=(rb == 3))
                if rb == 3:
                    P.op("t", "lq3", 1, mm)
                else:
                    P.run("t", lambda t, f2=mm: f2(t))
            P.wait("v", "lq3")
            P.wait_if("v", "ln_sq")
            P.op("v", "lq1", 1, lambda v, mch=mch, t0=t0: v.tensor_mul(
                env.bo[:, mch * 512:(mch + 1) * 512],
                fm(env.xnorm, mch)[:, t0:t0 + 512], env.tmpf[:, 0:512]))
            P.wait("v", "lq1")
            P.op("v", "lq1", 1, lambda v, mch=mch, bank=bank: v.tensor_add(
                env.bo[:, mch * 512:(mch + 1) * 512],
                env.bo[:, mch * 512:(mch + 1) * 512], bank[:, 0:512]))
        P.wait("v", "lq1")
        P.wait("s", "lq1")
        if getattr(env, "l0f_dump", None) and si == 0 and "bo0" in env.l0f_dump:
            P.wait_all("y")
            P.op("y", "lqd", 16, lambda y: y.dma_start(
                out=env.l0f_dump["bo0"], in_=env.bo[:, :]))
            P.wait("y", "lqd")
            P.wait("t", "lqd")
        src_b = [env.bo[:, c * 512:(c + 1) * 512] for c in range(NC)]
        add = [fm(env.xres, c)[:, t0:t0 + 512] for c in range(NC)]
        emit_ln_span(P, env, src_b, None, add, "ln", width=512)


def setup_l0f(P, env, pfx=""):
    nc = P.nc
    env.l0f_in = nc.dram_tensor(pfx + "l0f_in", [2052, 1024], BF16, kind="Internal").ap()
    env.l0f_out = nc.dram_tensor(pfx + "l0f_out", [4104, 1024], BF16, kind="Internal").ap()
    env.l0f_C = nc.dram_tensor(pfx + "l0f_C", [512, 4096], BF16, kind="Internal").ap()
    W = {}
    W["wqkvT_bf"] = stage_weight(P, env, pfx + "wqkvT", (D, 3072))
    W["woutT_bf"] = stage_weight(P, env, pfx + "woutT", (D, D))
    W["w1T_bf"] = stage_weight(P, env, pfx + "w1T", (D, 2816))
    W["w2T_bf"] = stage_weight(P, env, pfx + "w2T", (D, 2816))
    W["w3T_bf"] = stage_weight(P, env, pfx + "w3T", (2816, D))
    sb = lambda n, sh, dt: env.ctx.enter_context(nc.sbuf_tensor(n, sh, dt))[:, :]
    W["wcol"] = load_vec_input(P, env, pfx + "wcol", [128, 8])
    W["g2cols"] = load_vec_input(P, env, pfx + "g2cols", [128, 8])
    W["g1row"] = load_vec_input(P, env, pfx + "g1row", [1, 1024], BF16)
    W["masksb"] = env.f1.bitcast(BF16)[:, 4096:6144]
    W["masksb_dr"] = nc.dram_tensor(
        pfx + "masksb", [128, 2048], BF16, kind="ExternalInput").ap()
    W["iofull"] = env.f1[:, 1536:2048]
    W["iofull_dr"] = nc.dram_tensor(
        pfx + "iofull", [128, 512], F32, kind="ExternalInput").ap()
    W["io128f"] = load_vec_input(P, env, pfx + "io128f", [128, 128])
    W["pcol"] = load_vec_input(P, env, pfx + "pcol", [128, 1])
    W["flag0"] = load_vec_input(P, env, pfx + "flag0", [128, 1])
    W["flag1"] = load_vec_input(P, env, pfx + "flag1", [128, 1])
    W["ident_f32"] = env.ident_f32
    W["ident_bf"] = env.ident_bf
    W["sfull"] = sb("sfull", [128, 32], F32)
    W["m"] = sb("msel", [128, 32], F32)
    W["pos"] = sb("possel", [128, 32], F32)
    W["shi"] = sb("shi", [128, 32], BF16)
    W["slo"] = sb("slo", [128, 32], BF16)
    W["whi"] = sb("whi", [128, 8], BF16)
    W["wlo"] = sb("wlo", [128, 8], BF16)
    W["sumw"] = sb("sumw", [1, 1], F32)
    W["g1b"] = env.f1.bitcast(BF16)[:, 2048:3072]
    for e in ("t", "y"):
        P.wait(e, "wstage")
    for e in ("t", "s", "v"):
        P.wait(e, "vload")
    for e in ("v", "s", "t"):
        P.wait(e, "const")
    emit_l0_ffn(P, env, W)


def build_l0f_program():
    nc = bass.Bass("TRN2", target_bir_lowering=False)
    with ExitStack() as ctx:
        P = Plan(nc, ctx)
        rin = nc.dram_tensor("rin", [D, T], F32, kind="ExternalInput").ap()
        rout = nc.dram_tensor("rout", [D, T], F32, kind="ExternalOutput").ap()
        env = build_env(P, ctx, attn=True)
        emit_load_resid(P, env, rin)
        setup_l0f(P, env)
        emit_store_resid(P, env, rout)
        emit_block_sections(nc, P)
    return nc


def host_l0f_inputs(inp, core):
    b, h = core // 2, core % 2
    m = {}
    m["wqkvT"] = np.ascontiguousarray(inp["L0_ffn_W_qkv"].T)
    m["woutT"] = np.ascontiguousarray(inp["L0_ffn_W_out"].T)
    w1t = np.zeros((1024, 2816), np.float32)
    w1t[:, 0:2730] = inp["L0_ffn_W1"].T
    m["w1T"] = w1t
    w2t = np.zeros((1024, 2816), np.float32)
    w2t[:, 0:2730] = inp["L0_ffn_W2"].T
    m["w2T"] = w2t
    w3t = np.zeros((2816, 1024), np.float32)
    w3t[0:2730, :] = inp["L0_ffn_W3"].T
    m["w3T"] = w3t
    m["wcol"] = np.ascontiguousarray(inp["L0_ffn_w_router"].reshape(8, 128).T)
    m["g2cols"] = np.ascontiguousarray(inp["L0_ffn_g2"].reshape(8, 128).T)
    m["g1row"] = inp["L0_ffn_g1"].reshape(1, 1024).astype(ml_dtypes.bfloat16)
    p = np.arange(128, dtype=np.float32)
    k512 = np.arange(512, dtype=np.float32)
    msk = np.zeros((128, 2048), np.float32)
    for qb in range(4):
        qg = qb * 128 + p
        msk[:, qb * 512:(qb + 1) * 512] = np.where(
            k512[None, :] > qg[:, None], -1e9, 0.0)
    m["masksb"] = msk.astype(ml_dtypes.bfloat16)
    m["iofull"] = np.broadcast_to(k512[None, :], (128, 512)).copy()
    m["io128f"] = np.broadcast_to(np.arange(128, dtype=np.float32)[None, :],
                                  (128, 128)).copy()
    m["pcol"] = p.reshape(128, 1).copy()
    m["flag0"] = np.full((128, 1), 1.0 - h, np.float32)
    m["flag1"] = np.full((128, 1), float(h), np.float32)
    m["ident_f32"] = np.eye(128, dtype=np.float32)
    m["ident_bf"] = np.eye(128, dtype=ml_dtypes.bfloat16)
    return m


def run_l0f(x_resid, inp):
    nc = build_l0f_program()
    in_maps = []
    for c in range(8):
        b, h = c // 2, c % 2
        m = host_l0f_inputs(inp, c)
        add_global_inputs(m)
        m["rin"] = np.ascontiguousarray(x_resid[b, h * T:(h + 1) * T, :].T)
        in_maps.append(m)
    res = run_bass_kernel_spmd(nc, in_maps, core_ids=list(range(8)))
    out = np.empty((4, 4096, D), np.float32)
    for c in range(8):
        b, h = c // 2, c % 2
        out[b, h * T:(h + 1) * T, :] = res.results[c]["rout"].T
    return out



MEGA_DEBUG = False
MEGA_KEEP_SBUF = False

MEGA_BLOCKS = [("b0", "setup_l0a"), ("b1", "setup_l0f"), ("b2", "setup_l1a"),
               ("b3", "setup_l1f"), ("b4", "setup_l2a"), ("b5", "setup_l2f"),
               ("b6", "setup_l3a"), ("b7", "setup_l3f")]


def build_mega_program(blocks=None):
    nc = bass.Bass("TRN2", target_bir_lowering=False)
    with ExitStack() as octx:
        P = Plan(nc, octx)
        rin = nc.dram_tensor("rin", [D, T], F32, kind="ExternalInput").ap()
        rout = nc.dram_tensor("rout", [D, T], F32, kind="ExternalOutput").ap()
        env = build_env(P, octx, attn=True)
        emit_load_resid(P, env, rin)
        for e in EN:
            P.wait(e, "const")
            P.wait(e, "rload")
        rdumps = []
        if MEGA_DEBUG:
            for bi in range(8):
                rdumps.append(nc.dram_tensor(
                    "rdump%d" % bi, [D, T], F32, kind="ExternalOutput").ap())
        keep = []
        for bi, (pfx, fn) in enumerate(blocks or MEGA_BLOCKS):
            P.begin_block()
            bctx = ExitStack()
            try:
                env.ctx = bctx
                globals()[fn](P, env, pfx + "_")
                for e in EN:
                    P.wait_all(e)
                if MEGA_DEBUG:
                    for c in range(NC):
                        P.op("y", "rdmp", 16, lambda y, bi=bi, c=c: y.dma_start(
                            out=rdumps[bi][c * 128:(c + 1) * 128, :],
                            in_=fm(env.xres, c)))
                    P.wait("y", "rdmp")
            finally:
                if MEGA_KEEP_SBUF:
                    keep.append(bctx)
                else:
                    bctx.close()
        for bc in keep:
            bc.close()
        env.ctx = octx
        emit_store_resid(P, env, rout)
        stg = [f for f in P.q["g"] if getattr(f, "mark", None) == "stage"]
        rest = [f for f in P.q["g"] if getattr(f, "mark", None) != "stage"]
        P.q["g"] = stg + rest
        emit_block_sections(nc, P)
    return nc


def add_global_inputs(m):
    m["g_identf32"] = np.eye(128, dtype=np.float32)
    m["g_identbf"] = np.eye(128, dtype=ml_dtypes.bfloat16)
    return m


def host_mega_inputs(inp, core):
    m = {}
    hosts = [("b0", host_l0a_inputs), ("b1", host_l0f_inputs),
             ("b2", host_l1a_inputs), ("b3", host_l1f_inputs),
             ("b4", host_l2a_inputs), ("b5", host_l2f_inputs),
             ("b6", host_l3a_inputs), ("b7", host_l3f_inputs)]
    for pfx, hf in hosts:
        for k, v in hf(inp, core).items():
            m[pfx + "_" + k] = v
    return m


def run_mega(inp):
    x = np.ascontiguousarray(np.asarray(inp["L0_attn_xq"], np.float32))
    nc = build_mega_program()
    in_maps = []
    for c in range(8):
        b, h = c // 2, c % 2
        m = host_mega_inputs(inp, c)
        add_global_inputs(m)
        m["rin"] = np.ascontiguousarray(x[b, h * T:(h + 1) * T, :].T)
        if h == 1:
            m["b0_xprev"] = np.ascontiguousarray(
                x[b, T - 1, :].reshape(8, 128).T)
        else:
            m["b0_xprev"] = np.zeros((128, 8), np.float32)
        in_maps.append(m)
    res = run_bass_kernel_spmd(nc, in_maps, core_ids=list(range(8)))
    out = np.empty((4, 4096, D), np.float32)
    for c in range(8):
        b, h = c // 2, c % 2
        out[b, h * T:(h + 1) * T, :] = res.results[c]["rout"].T
    return out



def kernel(**inputs):
    return run_mega(inputs)
